# version 08
# baseline (speedup 1.0000x reference)
import numpy as np
from contextlib import ExitStack

import concourse.bass as bass
import concourse.mybir as mybir
from concourse.bass_utils import run_bass_kernel_spmd

F32 = mybir.dt.float32
BF16 = mybir.dt.bfloat16
AF = mybir.ActivationFunctionType
ALU = mybir.AluOpType
AX = mybir.AxisListType

MAGIC = float(np.float32(1.5 * 2 ** 23))

NCORES = 8
NSEQ = 2
S = 4096
D = 1024
DC = D // 128
T = 512
NT_SEQ = S // T
NTILES = NSEQ * NT_SEQ
EPS = 1e-5

HID = 4096
HC = HID // 128
NEXP = 8
TB = 1024
NT_B = TB // T
NBLK = NSEQ * S // TB
BLK_PER_SEQ = S // TB

MEM = 40

FUSED = True



def _cm(w):
    w = np.ascontiguousarray(np.asarray(w), dtype=np.float32)
    return np.ascontiguousarray(w.reshape(w.shape[0] // 128, 128, w.shape[1]))


def _cv(v):
    v = np.ascontiguousarray(np.asarray(v), dtype=np.float32)
    return np.ascontiguousarray(v.reshape(-1, 128).T)


def prep_weights(i):
    d = {}
    d["p0_wd"] = _cm(np.asarray(i["L0_attn_W_delta"]).T)
    d["p0_bd"] = _cv(i["L0_attn_b_delta"])
    d["p0_wb"] = _cm(np.asarray(i["L0_attn_W_B"]).T)
    d["p0_bB"] = np.ascontiguousarray(np.asarray(i["L0_attn_b_B"], dtype=np.float32).reshape(16, 1))
    d["p0_alog"] = np.ascontiguousarray(np.asarray(i["L0_attn_A_log"], dtype=np.float32).reshape(16, 1))
    d["p0_wo"] = np.ascontiguousarray(np.asarray(i["L0_attn_W_out"], dtype=np.float32).T)
    d["p0_bo"] = _cv(i["L0_attn_b_out"])
    d["p0_dp"] = _cv(i["L0_attn_D_param"])
    d["ident"] = np.eye(128, dtype=np.float32)
    d["p1_w"] = _cm(np.asarray(i["L0_ffn_weight"]).T)
    d["p1_b"] = _cv(i["L0_ffn_bias"])
    d["p2_w1"] = _cm(i["L1_attn_W1"])
    d["p2_b1"] = _cv(i["L1_attn_bias1"])
    d["p2_w2"] = _cm(i["L1_attn_W2"])
    d["p2_b2"] = _cv(i["L1_attn_bias2"])
    d["p2_mw"] = np.ascontiguousarray(
        np.asarray(i["L1_attn_memory_weights"], dtype=np.float32).T.reshape(DC, 128, MEM + 1))
    d["p3_wr"] = _cm(i["L1_ffn_Wr"])
    wfc = np.concatenate([np.asarray(i["L1_ffn_Wfc_s"], dtype=np.float32)[None],
                          np.asarray(i["L1_ffn_Wfc"], dtype=np.float32)], axis=0)
    d["p3_wfc"] = np.ascontiguousarray(wfc.reshape(NEXP + 1, DC, 128, HID))
    wpj = np.concatenate([np.asarray(i["L1_ffn_Wproj_s"], dtype=np.float32)[None],
                          np.asarray(i["L1_ffn_Wproj"], dtype=np.float32)], axis=0)
    d["p3_wpj"] = np.ascontiguousarray(wpj.reshape(NEXP + 1, HC, 128, D))
    bfc = np.concatenate([np.asarray(i["L1_ffn_bfc_s"], dtype=np.float32)[None],
                          np.asarray(i["L1_ffn_bfc"], dtype=np.float32)], axis=0)
    d["p3_bfc"] = np.ascontiguousarray(
        bfc.reshape(NEXP + 1, HC, 128).transpose(0, 2, 1))
    bpj = np.concatenate([np.asarray(i["L1_ffn_bproj_s"], dtype=np.float32)[None],
                          np.asarray(i["L1_ffn_bproj"], dtype=np.float32)], axis=0)
    d["p3_bpj"] = np.ascontiguousarray(
        bpj.reshape(NEXP + 1, DC, 128).transpose(0, 2, 1))
    return d


def shard_x(x):
    x = np.asarray(x, dtype=np.float32)
    out = []
    for c in range(NCORES):
        xc = x[NSEQ * c:NSEQ * (c + 1)]
        out.append(np.ascontiguousarray(
            xc.reshape(NSEQ, S, DC, 128).transpose(0, 2, 3, 1)))
    return out


def unshard_out(shards):
    out = np.empty((NCORES * NSEQ, S, D), dtype=np.float32)
    for c, o in enumerate(shards):
        out[NSEQ * c:NSEQ * (c + 1)] = np.asarray(o).transpose(0, 3, 1, 2).reshape(NSEQ, S, D)
    return out


def tile_bt(i):
    return i // NT_SEQ, (i % NT_SEQ) * T



ENGINES = ("sync", "gpsimd", "vector", "scalar", "tensor")
QUEUES = ("sync", "gpsimd")


class _Op:
    __slots__ = ("eng", "fn", "inc_val", "deps", "needs_inc", "inc_target",
                 "batch_end", "self_wait", "dma")

    def __init__(self, eng, fn, inc_val, dma):
        self.eng = eng
        self.fn = fn
        self.inc_val = inc_val
        self.dma = dma
        self.deps = []
        self.needs_inc = False
        self.inc_target = None
        self.batch_end = None
        self.self_wait = None


class FW:
    def __init__(self, nc, es):
        self.nc = nc
        self.sems = {e: es.enter_context(nc.semaphore(f"S_{e}")) for e in ENGINES}
        self.bsems = {q: es.enter_context(nc.semaphore(f"B_{q}")) for q in QUEUES}
        self.csems = {q: es.enter_context(nc.semaphore(f"C_{q}")) for q in QUEUES}
        self.icount = {e: 0 for e in ENGINES}
        self.bcount = {q: 0 for q in QUEUES}
        self.ccount = {q: 0 for q in QUEUES}
        self.prev_totals = None
        self.ops = []
        self.last_w = {}
        self.readers = {}

    def op(self, eng, fn, reads=(), writes=(), inc_val=1, dma=False):
        o = _Op(eng, fn, inc_val, dma)
        deps = {}
        for t in reads:
            w = self.last_w.get(t)
            if w is not None:
                deps[id(w)] = w
        for t in writes:
            w = self.last_w.get(t)
            if w is not None:
                deps[id(w)] = w
            for r in self.readers.get(t, ()):
                deps[id(r)] = r
        for t in reads:
            self.readers.setdefault(t, []).append(o)
        for t in writes:
            self.last_w[t] = o
            self.readers[t] = []
        for d in deps.values():
            if d is o:
                continue
            if d.eng == "tensor" and eng == "tensor":
                continue
            o.deps.append(d)
            d.needs_inc = True
        self.ops.append(o)
        return o

    def dma(self, queue, fn, reads=(), writes=()):
        return self.op(queue, fn, reads, writes, inc_val=16, dma=True)

    def flush(self, block):
        for o in self.ops:
            if o.eng in QUEUES and o.dma:
                self.icount[o.eng] += o.inc_val
                o.inc_target = self.icount[o.eng]
            elif o.eng in QUEUES and o.needs_inc:
                self.ccount[o.eng] += o.inc_val
                o.inc_target = self.ccount[o.eng]
            elif o.needs_inc:
                self.icount[o.eng] += o.inc_val
                o.inc_target = self.icount[o.eng]
        open_batch = {q: [] for q in QUEUES}
        for o in self.ops:
            for d in o.deps:
                if d.eng in QUEUES and d.dma and d.batch_end is None:
                    ob = open_batch[d.eng]
                    if ob:
                        last = ob[-1]
                        val = last.inc_target
                        for m in ob:
                            m.batch_end = val
                        last.self_wait = val
                        open_batch[d.eng] = []
            if o.eng in QUEUES and o.dma:
                open_batch[o.eng].append(o)
        for q in QUEUES:
            for m in open_batch[q]:
                m.batch_end = self.icount[q]

        queue_totals = {e: self.icount[e] for e in ENGINES}
        for e in ENGINES:
            if e not in QUEUES:
                self.icount[e] += 1
        for q in QUEUES:
            self.bcount[q] += 1
        new_totals = {e: (self.bcount[e] if e in QUEUES else self.icount[e])
                      for e in ENGINES}
        prev = self.prev_totals
        ops = self.ops
        sems = self.sems
        bsems = self.bsems
        csems = self.csems

        for ename in ENGINES:
            eng_ops = [o for o in ops if o.eng == ename]
            sec = getattr(block, ename)

            def _body(eng, ename=ename, eng_ops=eng_ops, prev=prev,
                      queue_totals=queue_totals):
                hwm = {}
                if prev is not None:
                    for oe in ENGINES:
                        if oe != ename:
                            eng.wait_ge(bsems[oe] if oe in QUEUES else sems[oe], prev[oe])
                            if oe not in QUEUES:
                                hwm[oe] = prev[oe]
                for o in eng_ops:
                    for d in o.deps:
                        if d.eng in QUEUES and d.dma:
                            key, sem, v = d.eng, sems[d.eng], d.batch_end
                        elif d.eng in QUEUES:
                            key, sem, v = "c" + d.eng, csems[d.eng], d.inc_target
                        else:
                            key, sem, v = d.eng, sems[d.eng], d.inc_target
                        if hwm.get(key, -1) < v:
                            eng.wait_ge(sem, v)
                            hwm[key] = v
                    inst = o.fn(eng)
                    if o.inc_target is not None:
                        if o.eng in QUEUES and not o.dma:
                            inst.then_inc(csems[o.eng], o.inc_val)
                        else:
                            inst.then_inc(sems[o.eng], o.inc_val)
                    if o.self_wait is not None:
                        eng.wait_ge(sems[ename], o.self_wait)
                        hwm[ename] = o.self_wait
                if ename in QUEUES:
                    eng.drain()
                    eng.wait_ge(sems[ename], queue_totals[ename])
                    eng.sem_inc(bsems[ename], 1)
                else:
                    eng.drain()
                    eng.sem_inc(sems[ename], 1)

            sec(_body)

        self.ops = []
        self.last_w = {}
        self.readers = {}
        self.prev_totals = new_totals




def emit_p0(fw, block, DR, es):
    nc = fw.nc
    sb = lambda n, s, dt=F32: es.enter_context(nc.sbuf_tensor("a0" + n, s, dt))
    psm = lambda n, s: es.enter_context(nc.psum_tensor("a0" + n, s, F32))
    v, sc, te, sy, gp = "vector", "scalar", "tensor", "sync", "gpsimd"

    wd = [sb(f"wd{k}", [128, D], BF16) for k in range(DC)]
    wb = [sb(f"wb{k}", [128, 16], BF16) for k in range(DC)]
    wo = sb("wo", [16, D], BF16)
    bd2 = sb("bd2", [128, DC])
    bB = sb("bB", [16, 1])
    alog = sb("alog", [16, 1])
    acol = sb("acol", [16, 1])
    bo = sb("bo", [128, DC])
    ecol = sb("ecol", [128, DC])
    ident = sb("ident", [128, 128])
    ones_bf = sb("onesbf", [128, 1], BF16)
    ones16 = sb("ones16", [1, 16])
    c16 = sb("c16", [1, 16])
    ones128r = sb("o128r", [1, 128])
    xf = [[sb(f"xf{u}_{k}", [128, T]) for k in range(DC)] for u in range(2)]
    xbt = [[sb(f"xb{u}_{k}", [128, T], BF16) for k in range(DC)] for u in range(2)]
    tnh = [sb(f"tnh{k}", [128, T], BF16) for k in range(DC)]
    sqb = [sb(f"sqb{k}", [128, T], BF16) for k in range(2)]
    bm = sb("bm", [16, T])
    abar = sb("abar", [16, T])
    bu = sb("bu", [16, T])
    states = sb("states", [16, T])
    st_bf = sb("stbf", [16, T], BF16)
    carry = sb("carry", [16, 1])
    delta = sb("delta", [1, T])
    grow = sb("grow", [1, T])
    y = [sb(f"y{k}", [128, T]) for k in range(DC)]
    ybf = [sb(f"ybf{k}", [128, T], BF16) for k in range(DC)]
    rows = [sb("mrow", [1, T]), sb("vrow", [1, T]), sb("srow", [1, T]), sb("rstd", [1, T])]
    mx = sb("mx", [128, DC])
    mxt = sb("mxt", [128, 1])
    lrow = sb("lrow", [1, 128])
    lmax = sb("lmax", [1, 1])

    pU = [psm("pU0", [128, T]), psm("pU1", [128, T])]
    pO = [psm("pO0", [128, T]), psm("pO1", [128, T])]
    pMB = psm("pMB", [128, T])
    pRB = psm("pRB", [128, T])
    bankA = psm("bankA", [128, T])
    bankB = psm("bankB", [128, T])

    fw.dma(sy, lambda e: e.dma_start(out=bd2[:], in_=DR["p0_bd"]), writes=("bd2",))
    fw.dma(sy, lambda e: e.dma_start(out=bo[:], in_=DR["p0_bo"]), writes=("bo",))
    fw.dma(sy, lambda e: e.dma_start(out=ecol[:], in_=DR["p0_dp"]), writes=("ecol",))
    fw.dma(sy, lambda e: e.dma_start(out=bB[:], in_=DR["p0_bB"]), writes=("bB",))
    fw.dma(sy, lambda e: e.dma_start(out=alog[:], in_=DR["p0_alog"]), writes=("alog",))
    fw.dma(sy, lambda e: e.dma_start(out=ident[:], in_=DR["ident"]), writes=("ident",))
    for k in range(DC):
        fw.dma(gp, lambda e, k=k: e.dma_start(out=wd[k][:], in_=DR["p0_wd"][k]), writes=(f"wd{k}",))
        fw.dma(gp, lambda e, k=k: e.dma_start(out=wb[k][:], in_=DR["p0_wb"][k]), writes=(f"wb{k}",))
    fw.dma(gp, lambda e: e.dma_start(out=wo[:], in_=DR["p0_wo"]), writes=("wo",))
    fw.op(v, lambda e: e.memset(ones_bf[:], 1.0), writes=("ones",))
    fw.op(v, lambda e: e.memset(ones16[:], 1.0), writes=("ones",))
    fw.op(v, lambda e: e.memset(c16[:], 1.0 / D), writes=("ones",))
    fw.op(v, lambda e: e.memset(ones128r[:], 1.0), writes=("ones",))
    fw.op(v, lambda e: e.memset(mx[:], 0.0), writes=("mx",))
    fw.op(v, lambda e: e.tensor_scalar(out=bd2[:], in0=bd2[:], scalar1=0.5, scalar2=None,
                                       op0=ALU.mult), reads=("bd2",), writes=("bd2",))
    fw.op(v, lambda e: e.tensor_scalar(out=ecol[:], in0=ecol[:], scalar1=1.0, scalar2=None,
                                       op0=ALU.add), reads=("ecol",), writes=("ecol",))
    fw.op(sc, lambda e: e.activation(out=acol[:], in_=alog[:], func=AF.Exp),
          reads=("alog",), writes=("acol",))
    fw.op(v, lambda e: e.tensor_scalar(out=acol[:], in0=acol[:], scalar1=-1.0, scalar2=None,
                                       op0=ALU.mult), reads=("acol",), writes=("acol",))

    ytags = [f"y{c}" for c in range(DC)]
    ybftags = [f"ybf{c}" for c in range(DC)]
    sqbtags = ["sqb0", "sqb1"]
    rowtags = ["mrow", "vrow", "srow", "rstd"]

    for i in range(NTILES):
        b, t0 = tile_bt(i)
        u = i % 2
        for c in range(DC):
            fw.dma(sy, lambda e, b=b, c=c, t0=t0, u=u:
                   e.dma_start(out=xf[u][c][:], in_=DR["x_in"][b, c, :, t0:t0 + T]),
                   writes=(f"xf{u}_{c}",))
            fw.dma(gp, lambda e, b=b, c=c, t0=t0, u=u:
                   e.dma_start(out=xbt[u][c][:], in_=DR["x_in"][b, c, :, t0:t0 + T]),
                   writes=(f"xb{u}_{c}",))
        if t0 == 0:
            fw.op(v, lambda e: e.memset(carry[:], 0.0), writes=("carry",))
        for m in range(DC):
            for k in range(DC):
                fw.op(te, lambda e, m=m, k=k, u=u:
                      e.matmul(out=pU[m % 2][:], lhsT=wd[k][:, m * 128:(m + 1) * 128],
                               rhs=xbt[u][k][:], start=(k == 0), stop=(k == DC - 1)),
                      reads=(f"wd{k}", f"xb{u}_{k}"), writes=(f"pU{m % 2}",))
            fw.op(sc, lambda e, m=m: e.activation(out=tnh[m][:], in_=pU[m % 2][:], func=AF.Tanh,
                                                  scale=0.5, bias=bd2[:, m:m + 1]),
                  reads=(f"pU{m % 2}", "bd2"), writes=(f"tnh{m}",))
        for k in range(DC):
            fw.op(te, lambda e, k=k, u=u:
                  e.matmul(out=bankB[32:48, :], lhsT=wb[k][:], rhs=xbt[u][k][:],
                           start=(k == 0), stop=(k == DC - 1)),
                  reads=(f"wb{k}", f"xb{u}_{k}"), writes=("bankB",))
        fw.op(sc, lambda e: e.activation(out=bm[:], in_=bankB[32:48, :], func=AF.Identity,
                                         bias=bB[:, 0:1]),
              reads=("bankB", "bB"), writes=("bm",))
        for k in range(DC):
            fw.op(te, lambda e, k=k, u=u:
                  e.matmul(out=bankA[32:33, :], lhsT=ones_bf[:], rhs=xbt[u][k][:],
                           start=(k == 0), stop=(k == DC - 1)),
                  reads=(f"xb{u}_{k}", "ones"), writes=("bankA",))
        for k in range(DC):
            fw.op(te, lambda e, k=k:
                  e.matmul(out=bankA[0:1, :], lhsT=ones_bf[:], rhs=tnh[k][:],
                           start=(k == 0), stop=(k == DC - 1)),
                  reads=(f"tnh{k}", "ones"), writes=("bankA",))
        fw.op(v, lambda e: e.tensor_scalar(out=delta[:], in0=bankA[0:1, :],
                                           scalar1=0.0495 / D, scalar2=0.0505,
                                           op0=ALU.mult, op1=ALU.add),
              reads=("bankA",), writes=("delta",))
        fw.op(v, lambda e: e.tensor_tensor(out=grow[:], in0=delta[:], in1=bankA[32:33, :],
                                           op=ALU.mult),
              reads=("delta", "bankA"), writes=("grow",))
        fw.op(te, lambda e: e.matmul(out=bankB[64:80, :], lhsT=ones16[:], rhs=delta[:],
                                     start=True, stop=True),
              reads=("delta", "ones"), writes=("bankB",))
        fw.op(sc, lambda e: e.activation(out=abar[:], in_=bankB[64:80, :], func=AF.Exp,
                                         scale=acol[:, 0:1]),
              reads=("bankB", "acol"), writes=("abar",))
        fw.op(te, lambda e: e.matmul(out=bankB[64:80, :], lhsT=c16[:], rhs=grow[:],
                                     start=True, stop=True),
              reads=("grow", "ones"), writes=("bankB",))
        fw.op(v, lambda e: e.tensor_tensor(out=bu[:], in0=bm[:], in1=bankB[64:80, :], op=ALU.mult),
              reads=("bm", "bankB"), writes=("bu",))
        fw.op(v, lambda e: e.tensor_tensor_scan(out=states[:], data0=abar[:], data1=bu[:],
                                                initial=carry[:, 0:1], op0=ALU.mult, op1=ALU.add),
              reads=("abar", "bu", "carry"), writes=("states",))
        fw.op(v, lambda e: e.tensor_copy(out=carry[:], in_=states[:, T - 1:T]),
              reads=("states",), writes=("carry",))
        fw.op(v, lambda e: e.tensor_copy(out=st_bf[:], in_=states[:]),
              reads=("states",), writes=("stbf",))
        for c in range(DC):
            fw.op(te, lambda e, c=c: e.matmul(out=pO[c % 2][:], lhsT=wo[:, c * 128:(c + 1) * 128],
                                              rhs=st_bf[:], start=True, stop=True),
                  reads=("wo", "stbf"), writes=(f"pO{c % 2}",))
            fw.op(sc, lambda e, c=c: e.activation(out=y[c][:], in_=pO[c % 2][:], func=AF.Identity,
                                                  bias=bo[:, c:c + 1]),
                  reads=(f"pO{c % 2}", "bo"), writes=(f"y{c}",))
            fw.op(v, lambda e, c=c, u=u: e.scalar_tensor_tensor(
                out=y[c][:], in0=xf[u][c][:], scalar=ecol[:, c:c + 1], in1=y[c][:],
                op0=ALU.mult, op1=ALU.add),
                reads=(f"xf{u}_{c}", "ecol", f"y{c}"), writes=(f"y{c}",))
        emit_postln(fw, y, ytags, ybf, ybftags, sqb, sqbtags, rows, rowtags,
                    bankA, "bankA", pMB, pRB, ones_bf, ones128r)
        for c in range(DC):
            fw.dma(sy, lambda e, b=b, c=c, t0=t0: e.dma_start(
                out=DR["x1_s"][b, c, :, t0:t0 + T], in_=y[c][:]),
                reads=(f"y{c}",))
            fw.op(v, lambda e, c=c: e.tensor_reduce(out=mxt[:], in_=y[c][:], axis=AX.X,
                                                    op=ALU.max, apply_absolute_value=True),
                  reads=(f"y{c}",), writes=("mxt",))
            fw.op(v, lambda e, c=c: e.tensor_tensor(out=mx[:, c:c + 1], in0=mx[:, c:c + 1],
                                                    in1=mxt[:], op=ALU.max),
                  reads=("mxt", "mx"), writes=("mx",))
    fw.op(v, lambda e: e.tensor_reduce(out=mxt[:], in_=mx[:], axis=AX.X, op=ALU.max),
          reads=("mx",), writes=("mxt",))
    fw.op(te, lambda e: e.transpose(out=bankA[0:1, 0:128], in_=mxt[:, 0:1], identity=ident[:]),
          reads=("mxt", "ident"), writes=("bankA",))
    fw.op(v, lambda e: e.tensor_copy(out=lrow[:], in_=bankA[0:1, 0:128]),
          reads=("bankA",), writes=("lrow",))
    fw.op(v, lambda e: e.tensor_reduce(out=lmax[:], in_=lrow[:], axis=AX.X, op=ALU.max),
          reads=("lrow",), writes=("lmax",))
    fw.dma(sy, lambda e: e.dma_start(out=DR["lmax_out"], in_=lmax[:]), reads=("lmax",))
    fw.flush(block)


def emit_cc(fw, block, DR):
    fw.op("gpsimd", lambda e: e.collective_compute(
        "AllReduce", ALU.max, replica_groups=[list(range(NCORES))],
        ins=[DR["lmax_out"]], outs=[DR["maxes"]]), inc_val=1, dma=True)
    fw.flush(block)



def emit_p1(fw, block, DR, es, n_max):
    nc = fw.nc
    sb = lambda n, s, dt=F32: es.enter_context(nc.sbuf_tensor("a1" + n, s, dt))
    psm = lambda n, s: es.enter_context(nc.psum_tensor("a1" + n, s, F32))
    v, sc, te, sy, gp = "vector", "scalar", "tensor", "sync", "gpsimd"

    w1f = [sb(f"w1f{k}", [128, D]) for k in range(DC)]
    qw = [sb(f"qw{k}", [128, D], BF16) for k in range(DC)]
    absw = sb("absw", [128, D])
    sgn = sb("sgn", [128, D])
    b1 = sb("b1", [128, DC])
    ident = sb("ident", [128, 128])
    ones_bf = sb("onesbf", [128, 1], BF16)
    ones128r = sb("o128r", [1, 128])
    mxs = sb("mxs", [1, n_max])
    gmax = sb("gmax", [1, 1])
    isc = sb("isc", [1, 1])
    wsis = sb("wsis", [1, 1])
    hws = sb("hws", [1, 1])
    wcol = sb("wcol", [128, DC])
    wcol1 = sb("wcol1", [128, 1])
    wrow = sb("wrow", [1, 128])
    ws = sb("ws", [1, 1])
    inv_col = sb("invcol", [128, 1])
    wsis_col = sb("wsiscol", [128, 1])
    hws_col = sb("hwscol", [128, 1])
    xf = [[sb(f"xf{u}_{k}", [128, T]) for k in range(DC)] for u in range(2)]
    qx = [sb(f"qx{k}", [128, T], BF16) for k in range(DC)]
    qt = sb("qt", [128, T])
    y = [sb(f"y{k}", [128, T]) for k in range(DC)]
    ybf = [sb(f"ybf{k}", [128, T], BF16) for k in range(DC)]
    sqb = [sb(f"sqb{k}", [128, T], BF16) for k in range(2)]
    rows = [sb("mrow", [1, T]), sb("vrow", [1, T]), sb("srow", [1, T]), sb("rstd", [1, T])]

    pQ = [psm("pQ0", [128, T]), psm("pQ1", [128, T])]
    pMB = psm("pMB", [128, T])
    pRB = psm("pRB", [128, T])
    bankA = psm("bankA", [128, T])
    pCol = psm("pCol", [128, 512])

    for k in range(DC):
        fw.dma(sy, lambda e, k=k: e.dma_start(out=w1f[k][:], in_=DR["p1_w"][k]), writes=(f"w1f{k}",))
    fw.dma(sy, lambda e: e.dma_start(out=b1[:], in_=DR["p1_b"]), writes=("b1",))
    fw.dma(sy, lambda e: e.dma_start(out=ident[:], in_=DR["ident"]), writes=("ident",))
    fw.dma(sy, lambda e: e.dma_start(out=mxs[:], in_=DR["maxes"]), writes=("mxs",))
    fw.op(v, lambda e: e.memset(ones_bf[:], 1.0), writes=("ones",))
    fw.op(v, lambda e: e.memset(ones128r[:], 1.0), writes=("ones",))

    for k in range(DC):
        fw.op(v, lambda e, k=k: e.tensor_reduce(out=wcol[:, k:k + 1], in_=w1f[k][:], axis=AX.X,
                                                op=ALU.add, apply_absolute_value=True),
              reads=(f"w1f{k}",), writes=("wcol",))
    fw.op(v, lambda e: e.tensor_reduce(out=wcol1[:], in_=wcol[:], axis=AX.X, op=ALU.add),
          reads=("wcol",), writes=("wcol1",))
    fw.op(te, lambda e: e.transpose(out=bankA[0:1, 0:128], in_=wcol1[:, 0:1], identity=ident[:]),
          reads=("wcol1", "ident"), writes=("bankA",))
    fw.op(v, lambda e: e.tensor_copy(out=wrow[:], in_=bankA[0:1, 0:128]),
          reads=("bankA",), writes=("wrow",))
    fw.op(v, lambda e: e.tensor_reduce(out=ws[:], in_=wrow[:], axis=AX.X, op=ALU.add),
          reads=("wrow",), writes=("ws",))
    fw.op(v, lambda e: e.tensor_scalar(out=ws[:], in0=ws[:], scalar1=1.0 / (D * D), scalar2=None,
                                       op0=ALU.mult), reads=("ws",), writes=("ws",))
    fw.op(v, lambda e: e.tensor_scalar(out=hws[:], in0=ws[:], scalar1=0.5, scalar2=None,
                                       op0=ALU.mult), reads=("ws",), writes=("hws",))
    fw.op(v, lambda e: e.tensor_reduce(out=gmax[:], in_=mxs[:], axis=AX.X, op=ALU.max),
          reads=("mxs",), writes=("gmax",))
    fw.op(v, lambda e: e.tensor_scalar(out=isc[:], in0=gmax[:], scalar1=1.0 / 127, scalar2=None,
                                       op0=ALU.mult), reads=("gmax",), writes=("isc",))
    fw.op(v, lambda e: e.tensor_tensor(out=wsis[:], in0=ws[:], in1=isc[:], op=ALU.mult),
          reads=("ws", "isc"), writes=("wsis",))
    fw.op(v, lambda e: e.reciprocal(out=gmax[:], in_=isc[:]), reads=("isc",), writes=("ginv", "gmax"))
    fw.op(te, lambda e: e.matmul(out=pCol[:, 0:1], lhsT=ones128r[:], rhs=gmax[:],
                                 start=True, stop=True), reads=("ginv", "ones"), writes=("pCol",))
    fw.op(v, lambda e: e.tensor_copy(out=inv_col[:], in_=pCol[:, 0:1]),
          reads=("pCol",), writes=("invcol",))
    fw.op(te, lambda e: e.matmul(out=pCol[:, 1:2], lhsT=ones128r[:], rhs=wsis[:],
                                 start=True, stop=True), reads=("wsis", "ones"), writes=("pCol",))
    fw.op(v, lambda e: e.tensor_copy(out=wsis_col[:], in_=pCol[:, 1:2]),
          reads=("pCol",), writes=("wsiscol",))
    fw.op(te, lambda e: e.matmul(out=pCol[:, 2:3], lhsT=ones128r[:], rhs=hws[:],
                                 start=True, stop=True), reads=("hws", "ones"), writes=("pCol",))
    fw.op(v, lambda e: e.tensor_copy(out=hws_col[:], in_=pCol[:, 2:3]),
          reads=("pCol",), writes=("hwscol",))
    for k in range(DC):
        fw.op(sc, lambda e, k=k: e.activation(out=absw[:], in_=w1f[k][:], func=AF.Abs),
              reads=(f"w1f{k}",), writes=("absw",))
        fw.op(sc, lambda e, k=k: e.activation(out=sgn[:], in_=w1f[k][:], func=AF.Sign),
              reads=(f"w1f{k}",), writes=("sgn",))
        fw.op(v, lambda e: e.tensor_scalar(out=absw[:], in0=absw[:], scalar1=hws_col[:, 0:1],
                                           scalar2=None, op0=ALU.is_gt),
              reads=("absw", "hwscol"), writes=("absw",))
        fw.op(v, lambda e, k=k: e.tensor_tensor(out=qw[k][:], in0=sgn[:], in1=absw[:], op=ALU.mult),
              reads=("sgn", "absw"), writes=(f"qw{k}",))

    ytags = [f"y{c}" for c in range(DC)]
    ybftags = [f"ybf{c}" for c in range(DC)]
    sqbtags = ["sqb0", "sqb1"]
    rowtags = ["mrow", "vrow", "srow", "rstd"]

    for i in range(NTILES):
        b, t0 = tile_bt(i)
        u = i % 2
        for c in range(DC):
            fw.dma(sy, lambda e, b=b, c=c, t0=t0, u=u:
                   e.dma_start(out=xf[u][c][:], in_=DR["x1_s"][b, c, :, t0:t0 + T]),
                   writes=(f"xf{u}_{c}",))
        for c in range(DC):
            fw.op(v, lambda e, c=c, u=u: e.tensor_scalar(
                out=qt[:], in0=xf[u][c][:], scalar1=inv_col[:, 0:1], scalar2=MAGIC,
                op0=ALU.mult, op1=ALU.add),
                reads=(f"xf{u}_{c}", "invcol"), writes=("qt",))
            fw.op(v, lambda e: e.tensor_scalar(
                out=qt[:], in0=qt[:], scalar1=MAGIC, scalar2=127.0,
                op0=ALU.subtract, op1=ALU.min),
                reads=("qt",), writes=("qt",))
            fw.op(v, lambda e, c=c: e.tensor_scalar(
                out=qx[c][:], in0=qt[:], scalar1=-128.0, scalar2=None, op0=ALU.max),
                reads=("qt",), writes=(f"qx{c}",))
        for m in range(DC):
            for k in range(DC):
                fw.op(te, lambda e, m=m, k=k: e.matmul(
                    out=pQ[m % 2][:], lhsT=qw[k][:, m * 128:(m + 1) * 128], rhs=qx[k][:],
                    start=(k == 0), stop=(k == DC - 1)),
                    reads=(f"qw{k}", f"qx{k}"), writes=(f"pQ{m % 2}",))
            fw.op(sc, lambda e, m=m: e.activation(out=y[m][:], in_=pQ[m % 2][:], func=AF.Identity,
                                                  scale=wsis_col[:, 0:1], bias=b1[:, m:m + 1]),
                  reads=(f"pQ{m % 2}", "wsiscol", "b1"), writes=(f"y{m}",))
            fw.op(v, lambda e, m=m, u=u: e.tensor_tensor(out=y[m][:], in0=y[m][:],
                                                         in1=xf[u][m][:], op=ALU.add),
                  reads=(f"y{m}", f"xf{u}_{m}"), writes=(f"y{m}",))
        emit_postln(fw, y, ytags, ybf, ybftags, sqb, sqbtags, rows, rowtags,
                    bankA, "bankA", pMB, pRB, ones_bf, ones128r)
        for c in range(DC):
            fw.dma(sy, lambda e, b=b, c=c, t0=t0: e.dma_start(
                out=DR["x2_s"][b, c, :, t0:t0 + T], in_=y[c][:]),
                reads=(f"y{c}",))
    fw.flush(block)



def emit_p2a(fw, block, DR, es, flush=True):
    nc = fw.nc
    sb = lambda n, s, dt=F32: es.enter_context(nc.sbuf_tensor("a2" + n, s, dt))
    psm = lambda n, s: es.enter_context(nc.psum_tensor("a2" + n, s, F32))
    v, sc, te, sy, gp = "vector", "scalar", "tensor", "sync", "gpsimd"

    w1 = [sb(f"w1{k}", [128, D], BF16) for k in range(DC)]
    b1 = sb("b1", [128, DC])
    xb2 = [[sb(f"xb{u}_{k}", [128, T], BF16) for k in range(DC)] for u in range(2)]
    pb = [sb(f"pb{k}", [128, T], BF16) for k in range(DC)]
    tcol = sb("tcol", [128, 1])
    tot = sb("tot", [128, 2 * DC])
    pP = [psm("pP0", [128, T])]

    for k in range(DC):
        fw.dma(gp, lambda e, k=k: e.dma_start(out=w1[k][:], in_=DR["p2_w1"][k]), writes=(f"w1{k}",))
    fw.dma(sy, lambda e: e.dma_start(out=b1[:], in_=DR["p2_b1"]), writes=("b1",))
    fw.op(v, lambda e: e.memset(tot[:], 0.0), writes=("tot",))

    for i in range(NTILES):
        b, t0 = tile_bt(i)
        u = i % 2
        for c in range(DC):
            fw.dma(gp, lambda e, b=b, c=c, t0=t0, u=u:
                   e.dma_start(out=xb2[u][c][:], in_=DR["x2_s"][b, c, :, t0:t0 + T]),
                   writes=(f"xb{u}_{c}",))
        for m in range(DC):
            for k in range(DC):
                fw.op(te, lambda e, m=m, k=k, u=u: e.matmul(
                    out=pP[0][:], lhsT=w1[k][:, m * 128:(m + 1) * 128], rhs=xb2[u][k][:],
                    start=(k == 0), stop=(k == DC - 1)),
                    reads=(f"w1{k}", f"xb{u}_{k}"), writes=("pP0",))
            fw.op(sc, lambda e, m=m: e.activation(out=pb[m][:], in_=pP[0][:], func=AF.Identity,
                                                  bias=b1[:, m:m + 1], accum_out=tcol[:]),
                  reads=("pP0", "b1"), writes=(f"pb{m}", "tcol"))
            j = b * DC + m
            fw.op(v, lambda e, j=j: e.tensor_tensor(out=tot[:, j:j + 1], in0=tot[:, j:j + 1],
                                                    in1=tcol[:], op=ALU.add),
                  reads=("tcol", "tot"), writes=("tot",))
            fw.dma(sy, lambda e, b=b, m=m, t0=t0: e.dma_start(
                out=DR["p_s"][b, m, :, t0:t0 + T], in_=pb[m][:]),
                reads=(f"pb{m}",), writes=(f"psd{b}_{m}",))
    fw.dma(sy, lambda e: e.dma_start(out=DR["tot_s"], in_=tot[:]), reads=("tot",),
           writes=("totd",))
    if flush:
        fw.flush(block)



def emit_p2b(fw, block, DR, es):
    nc = fw.nc
    sb = lambda n, s, dt=F32: es.enter_context(nc.sbuf_tensor("a3" + n, s, dt))
    psm = lambda n, s: es.enter_context(nc.psum_tensor("a3" + n, s, F32))
    v, sc, te, sy, gp = "vector", "scalar", "tensor", "sync", "gpsimd"

    w2 = [sb(f"w2{k}", [128, D], BF16) for k in range(DC)]
    b2 = sb("b2", [128, DC])
    mw = [sb(f"mw{k}", [128, MEM + 1]) for k in range(DC)]
    taps = [sb(f"tp{k}", [128, MEM]) for k in range(DC)]
    tot = sb("tot", [128, 2 * DC])
    ct = sb("ct", [128, 2 * DC])
    ident = sb("ident", [128, 128])
    diag = sb("diag", [128, 128 * MEM], BF16)
    pfull = [sb(f"pfl{u}", [128, MEM - 1 + S], BF16) for u in range(2)]
    bstage = [sb(f"bst{u}", [128, T], BF16) for u in range(2)]
    xf = [[sb(f"xf{u}_{k}", [128, T]) for k in range(DC)] for u in range(2)]
    bnd = [[sb(f"bnd{u}_{k}", [128, T], BF16) for k in range(DC)] for u in range(2)]
    y = [sb(f"y{k}", [128, T]) for k in range(DC)]
    ybf = [sb(f"ybf{k}", [128, T], BF16) for k in range(DC)]
    sqb = [sb(f"sqb{k}", [128, T], BF16) for k in range(2)]
    ones_bf = sb("onesbf", [128, 1], BF16)
    ones128r = sb("o128r", [1, 128])
    rows = [sb("mrow", [1, T]), sb("vrow", [1, T]), sb("srow", [1, T]), sb("rstd", [1, T])]
    pB = [psm("pB0", [128, T]), psm("pB1", [128, T])]
    pW = [psm("pW0", [128, T]), psm("pW1", [128, T])]
    pMB = psm("pMB", [128, T])
    pRB = psm("pRB", [128, T])
    bankA = psm("bankA", [128, T])

    for k in range(DC):
        fw.dma(gp, lambda e, k=k: e.dma_start(out=w2[k][:], in_=DR["p2_w2"][k]), writes=(f"w2{k}",))
        fw.dma(sy, lambda e, k=k: e.dma_start(out=mw[k][:], in_=DR["p2_mw"][k]), writes=(f"mw{k}",))
    fw.dma(sy, lambda e: e.dma_start(out=b2[:], in_=DR["p2_b2"]), writes=("b2",))
    fw.dma(sy, lambda e: e.dma_start(out=tot[:], in_=DR["tot_s"]), reads=("totd",), writes=("tot",))
    fw.dma(sy, lambda e: e.dma_start(out=ident[:], in_=DR["ident"]), writes=("ident",))
    fw.op(v, lambda e: e.memset(ones_bf[:], 1.0), writes=("ones",))
    fw.op(v, lambda e: e.memset(ones128r[:], 1.0), writes=("ones",))
    for k in range(DC):
        fw.op(v, lambda e, k=k: e.tensor_scalar(out=taps[k][:], in0=mw[k][:, 1:MEM + 1],
                                                scalar1=mw[k][:, 0:1], scalar2=None,
                                                op0=ALU.subtract),
              reads=(f"mw{k}",), writes=(f"tp{k}",))
    for b in range(NSEQ):
        for k in range(DC):
            j = b * DC + k
            fw.op(v, lambda e, j=j, k=k: e.tensor_tensor(
                out=ct[:, j:j + 1], in0=mw[k][:, 0:1], in1=tot[:, j:j + 1], op=ALU.mult),
                reads=(f"mw{k}", "tot"), writes=("ct",))

    for u in range(2):
        fw.op(v, lambda e, u=u: e.memset(pfull[u][:, 0:MEM - 1], 0.0), writes=(f"pfl{u}",))
    for b in range(NSEQ):
        for c in range(DC):
            for j in range(MEM):
                fw.op(v, lambda e, c=c, j=j: e.tensor_scalar(
                    out=diag[:, j * 128:(j + 1) * 128], in0=ident[:],
                    scalar1=taps[c][:, j:j + 1], scalar2=None, op0=ALU.mult),
                    reads=("ident", f"tp{c}"), writes=("diag",))
            u = (b * DC + c) % 2
            fw.dma(sy, lambda e, b=b, c=c, u=u: e.dma_start(
                out=pfull[u][:, MEM - 1:], in_=DR["p_s"][b, c, :, :]),
                reads=(f"psd{b}_{c}",), writes=(f"pfl{u}",))
            j0 = b * DC + c
            for tp in range(NT_SEQ // 2):
                for j in range(MEM):
                    for ti in range(2):
                        t8 = tp * 2 + ti
                        fw.op(te, lambda e, u=u, t8=t8, j=j, ti=ti: e.matmul(
                            out=pB[ti][:], lhsT=diag[:, j * 128:(j + 1) * 128],
                            rhs=pfull[u][:, MEM - 1 + t8 * T - j:MEM - 1 + t8 * T - j + T],
                            start=(j == 0), stop=(j == MEM - 1)),
                            reads=("diag", f"pfl{u}"), writes=(f"pB{ti}",))
                for ti in range(2):
                    t8 = tp * 2 + ti
                    fw.op(v, lambda e, ti=ti, j0=j0: e.tensor_scalar(
                        out=bstage[ti][:], in0=pB[ti][:],
                        scalar1=ct[:, j0:j0 + 1], scalar2=None, op0=ALU.add),
                        reads=(f"pB{ti}", "ct"), writes=(f"bst{ti}",))
                    fw.dma(sy, lambda e, b=b, c=c, t8=t8, ti=ti: e.dma_start(
                        out=DR["band_s"][b, c, :, t8 * T:(t8 + 1) * T], in_=bstage[ti][:]),
                        reads=(f"bst{ti}",), writes=(f"bdd{b}_{c}",))

    ytags = [f"y{c}" for c in range(DC)]
    ybftags = [f"ybf{c}" for c in range(DC)]
    sqbtags = ["sqb0", "sqb1"]
    rowtags = ["mrow", "vrow", "srow", "rstd"]

    for i in range(NTILES):
        b, t0 = tile_bt(i)
        u = i % 2
        for c in range(DC):
            fw.dma(sy, lambda e, b=b, c=c, t0=t0, u=u: e.dma_start(
                out=bnd[u][c][:], in_=DR["band_s"][b, c, :, t0:t0 + T]),
                reads=(f"bdd{b}_{c}",), writes=(f"bnd{u}_{c}",))
            fw.dma(sy, lambda e, b=b, c=c, t0=t0, u=u: e.dma_start(
                out=xf[u][c][:], in_=DR["x2_s"][b, c, :, t0:t0 + T]),
                writes=(f"xf{u}_{c}",))
        for m in range(DC):
            for k in range(DC):
                fw.op(te, lambda e, m=m, k=k, u=u: e.matmul(
                    out=pW[m % 2][:], lhsT=w2[k][:, m * 128:(m + 1) * 128], rhs=bnd[u][k][:],
                    start=(k == 0), stop=(k == DC - 1)),
                    reads=(f"w2{k}", f"bnd{u}_{k}"), writes=(f"pW{m % 2}",))
            fw.op(sc, lambda e, m=m: e.activation(out=y[m][:], in_=pW[m % 2][:], func=AF.Identity,
                                                  bias=b2[:, m:m + 1]),
                  reads=(f"pW{m % 2}", "b2"), writes=(f"y{m}",))
            fw.op(v, lambda e, m=m, u=u: e.tensor_tensor(out=y[m][:], in0=y[m][:],
                                                         in1=xf[u][m][:], op=ALU.add),
                  reads=(f"y{m}", f"xf{u}_{m}"), writes=(f"y{m}",))
        emit_postln(fw, y, ytags, ybf, ybftags, sqb, sqbtags, rows, rowtags,
                    bankA, "bankA", pMB, pRB, ones_bf, ones128r)
        for c in range(DC):
            fw.dma(sy, lambda e, b=b, c=c, t0=t0: e.dma_start(
                out=DR["x3_s"][b, c, :, t0:t0 + T], in_=y[c][:]),
                reads=(f"y{c}",))
    fw.flush(block)



def emit_p3(fw, block, DR, es):
    nc = fw.nc
    v, sc, te, sy, gp = "vector", "scalar", "tensor", "sync", "gpsimd"

    with ExitStack() as ges:
        gsb = lambda n, s, dt=F32: ges.enter_context(nc.sbuf_tensor("g_" + n, s, dt))
        gps = lambda n, s: ges.enter_context(nc.psum_tensor("g_" + n, s, F32))
        wr = [gsb(f"wr{k}", [128, NEXP]) for k in range(DC)]
        xb3 = [[gsb(f"xb{u}_{k}", [128, T]) for k in range(DC)] for u in range(2)]
        et = gsb("et", [128, NEXP])
        msk = gsb("msk", [128, NEXP])
        em1 = gsb("em1", [128, 1])
        nem1 = gsb("nem1", [128, 1])
        em2 = gsb("em2", [128, 1])
        den = gsb("den", [128, 1])
        gt = gsb("gt", [128, NEXP])
        gtb = gsb("gtb", [8, 128], BF16)
        ident = gsb("ident", [128, 128])
        pLf = gps("pLf", [128, 512])
        pTf = gps("pTf", [128, 512])
        pL = pLf[:, 0:NEXP]
        pT = pTf[0:8, 0:128]

        fw.dma(sy, lambda e: e.dma_start(out=ident[:], in_=DR["ident"]), writes=("ident",))
        for k in range(DC):
            fw.dma(sy, lambda e, k=k: e.dma_start(out=wr[k][:], in_=DR["p3_wr"][k]),
                   writes=(f"wr{k}",))
        for i in range(NTILES):
            b, t0 = tile_bt(i)
            u = i % 2
            for c in range(DC):
                fw.dma(sy, lambda e, b=b, c=c, t0=t0, u=u: e.dma_start(
                    out=xb3[u][c][:], in_=DR["x3_s"][b, c, :, t0:t0 + T]),
                    writes=(f"xb{u}_{c}",))
            for g in range(T // 128):
                for k in range(DC):
                    fw.op(te, lambda e, g=g, k=k, u=u: e.matmul(
                        out=pL, lhsT=xb3[u][k][:, g * 128:(g + 1) * 128], rhs=wr[k][:],
                        start=(k == 0), stop=(k == DC - 1)),
                        reads=(f"xb{u}_{k}", f"wr{k}"), writes=("pL",))
                fw.op(v, lambda e: e.tensor_reduce(out=em1[:], in_=pL, axis=AX.X, op=ALU.max),
                      reads=("pL",), writes=("em1",))
                fw.op(v, lambda e: e.tensor_scalar(out=nem1[:], in0=em1[:], scalar1=-1.0,
                                                   scalar2=None, op0=ALU.mult),
                      reads=("em1",), writes=("nem1",))
                fw.op(sc, lambda e: e.activation(out=et[:], in_=pL, func=AF.Exp,
                                                 bias=nem1[:, 0:1]),
                      reads=("pL", "nem1"), writes=("et",))
                fw.op(v, lambda e: e.tensor_reduce(out=em1[:], in_=et[:], axis=AX.X, op=ALU.max),
                      reads=("et",), writes=("em1",))
                fw.op(v, lambda e: e.tensor_scalar(out=msk[:], in0=et[:], scalar1=em1[:, 0:1],
                                                   scalar2=None, op0=ALU.is_lt),
                      reads=("et", "em1"), writes=("msk",))
                fw.op(v, lambda e: e.tensor_tensor(out=msk[:], in0=msk[:], in1=et[:], op=ALU.mult),
                      reads=("msk", "et"), writes=("msk",))
                fw.op(v, lambda e: e.tensor_reduce(out=em2[:], in_=msk[:], axis=AX.X, op=ALU.max),
                      reads=("msk",), writes=("em2",))
                fw.op(v, lambda e: e.tensor_scalar(out=gt[:], in0=et[:], scalar1=em2[:, 0:1],
                                                   scalar2=None, op0=ALU.is_ge),
                      reads=("et", "em2"), writes=("gt",))
                fw.op(v, lambda e: e.tensor_tensor(out=gt[:], in0=gt[:], in1=et[:], op=ALU.mult),
                      reads=("gt", "et"), writes=("gt",))
                fw.op(v, lambda e: e.tensor_tensor(out=den[:], in0=em1[:], in1=em2[:], op=ALU.add),
                      reads=("em1", "em2"), writes=("den",))
                fw.op(v, lambda e: e.reciprocal(out=den[:], in_=den[:]),
                      reads=("den",), writes=("den",))
                fw.op(v, lambda e: e.tensor_scalar(out=gt[:], in0=gt[:], scalar1=den[:, 0:1],
                                                   scalar2=None, op0=ALU.mult),
                      reads=("gt", "den"), writes=("gt",))
                fw.op(te, lambda e: e.transpose(out=pT, in_=gt[:], identity=ident[:]),
                      reads=("gt", "ident"), writes=("pT",))
                fw.op(v, lambda e: e.tensor_copy(out=gtb[:], in_=pT),
                      reads=("pT",), writes=("gtb",))
                fw.dma(sy, lambda e, i=i, g=g: e.dma_start(
                    out=DR["gates_s"][:, i * T + g * 128:i * T + (g + 1) * 128], in_=gtb[:]),
                    reads=("gtb",))
        fw.flush(block)

    with ExitStack() as bes:
        bsb = lambda n, s, dt=F32: bes.enter_context(nc.sbuf_tensor("e_" + n, s, dt))
        bps = lambda n, s: bes.enter_context(nc.psum_tensor("e_" + n, s, F32))
        xb3 = [bsb(f"xb{k}", [128, TB], BF16) for k in range(DC)]
        acc = [bsb(f"acc{k}", [128, TB]) for k in range(DC)]
        hid = [bsb(f"hid{k}", [128, TB], BF16) for k in range(HC)]
        wfcT = [bsb(f"wfcT{u}", [128, DC * 1024], BF16) for u in range(2)]
        wpjT = [bsb(f"wpjT{u}", [128, HC * 256], BF16) for u in range(2)]
        bfc = [bsb(f"bfc{u}", [128, HC]) for u in range(2)]
        bpj = [bsb(f"bpj{u}", [128, DC]) for u in range(2)]
        grow = [bsb(f"grow{u}", [1, TB], BF16) for u in range(2)]
        gbc = bsb("gbc", [128, TB], BF16)
        tmp = bsb("tmp", [128, T])
        ones_bf = bsb("onesbf", [128, 1], BF16)
        ones128r = bsb("o128r", [1, 128])
        ones128r_bf = bsb("o128rb", [1, 128], BF16)
        pFC = [bps(f"pFC{q}", [128, T]) for q in range(4)]
        pPJ = [bps(f"pPJ{q}", [128, T]) for q in range(4)]
        xf = [hid[c][:].bitcast(F32) for c in range(DC)]
        ybf = [hid[DC + c] for c in range(DC)]
        sqb = [hid[2 * DC], hid[2 * DC + 1]]
        rows = [hid[2 * DC + 2 + j][:].bitcast(F32)[0:1, 0:T] for j in range(4)]
        pMB = pFC[0]
        pRB = pFC[1]
        bankA = pFC[2]

        fw.op(v, lambda e: e.memset(ones_bf[:], 1.0), writes=("ones",))
        fw.op(v, lambda e: e.memset(ones128r[:], 1.0), writes=("ones",))
        fw.op(v, lambda e: e.memset(ones128r_bf[:], 1.0), writes=("ones",))

        wfc_tags = tuple(f"wfcT{u}" for u in range(2))
        wpj_tags = tuple(f"wpjT{u}" for u in range(2))

        ytags = [f"acc{c}" for c in range(DC)]
        ybftags = [f"hid{DC + c}" for c in range(DC)]
        sqbtags = [f"hid{2 * DC}", f"hid{2 * DC + 1}"]
        rowtags = [f"hid{2 * DC + 2 + j}" for j in range(4)]

        mg_ctr = [0]
        pj_ctr = [0]

        for blk in range(NBLK):
            b = blk // BLK_PER_SEQ
            tb0 = (blk % BLK_PER_SEQ) * TB
            for k in range(DC):
                fw.dma(gp, lambda e, b=b, k=k, tb0=tb0: e.dma_start(
                    out=xb3[k][:], in_=DR["x3_s"][b, k, :, tb0:tb0 + TB]),
                    writes=(f"xb3{k}",))
            for eidx in range(NEXP + 1):
                gu = eidx % 2
                if eidx > 0:
                    fw.dma(sy, lambda e, eidx=eidx, blk=blk, gu=gu: e.dma_start(
                        out=grow[gu][:], in_=DR["gates_s"][eidx - 1:eidx, blk * TB:(blk + 1) * TB]),
                        writes=(f"grow{gu}",))
                for mg in range(4):
                    u = mg_ctr[0] % 2
                    mg_ctr[0] += 1
                    fw.dma(gp, lambda e, eidx=eidx, mg=mg, u=u: e.dma_start(
                        out=wfcT[u][:].rearrange("p (k m) -> p k m", k=DC),
                        in_=DR["p3_wfc"][eidx, :, :, mg * 1024:(mg + 1) * 1024].rearrange(
                            "k p m -> p k m")),
                        writes=(wfc_tags[u],))
                    if mg == 0:
                        fw.dma(gp, lambda e, eidx=eidx, gu=gu: e.dma_start(
                            out=bfc[gu][:], in_=DR["p3_bfc"][eidx]),
                            writes=(f"bfc{gu}",))
                        fw.dma(gp, lambda e, eidx=eidx, gu=gu: e.dma_start(
                            out=bpj[gu][:], in_=DR["p3_bpj"][eidx]),
                            writes=(f"bpj{gu}",))
                    for mi in range(DC):
                        m = mg * DC + mi
                        q = (mi % 2) * 2
                        for k in range(DC):
                            for t in range(NT_B):
                                fw.op(te, lambda e, u=u, k=k, mi=mi, t=t, q=q: e.matmul(
                                    out=pFC[q + t][:],
                                    lhsT=wfcT[u][:, k * 1024 + mi * 128:k * 1024 + (mi + 1) * 128],
                                    rhs=xb3[k][:, t * T:(t + 1) * T],
                                    start=(k == 0), stop=(k == DC - 1)),
                                    reads=(wfc_tags[u], f"xb3{k}"), writes=(f"pFC{q + t}",))
                        for t in range(NT_B):
                            fw.op(sc, lambda e, gu=gu, m=m, t=t, q=q: e.activation(
                                out=hid[m][:, t * T:(t + 1) * T], in_=pFC[q + t][:],
                                func=AF.Gelu_apprx_tanh, bias=bfc[gu][:, m:m + 1]),
                                reads=(f"pFC{q + t}", f"bfc{gu}"), writes=(f"hid{m}",))
                if eidx > 0:
                    for t in range(NT_B):
                        fw.op(te, lambda e, gu=gu, t=t: e.matmul(
                            out=pFC[t][:], lhsT=ones128r_bf[:], rhs=grow[gu][0:1, t * T:(t + 1) * T],
                            start=True, stop=True),
                            reads=(f"grow{gu}", "ones"), writes=(f"pFC{t}",))
                        fw.op(v, lambda e, t=t: e.tensor_copy(out=gbc[:, t * T:(t + 1) * T],
                                                              in_=pFC[t][:]),
                              reads=(f"pFC{t}",), writes=("gbc",))
                for mg in range(4):
                    u = pj_ctr[0] % 2
                    pj_ctr[0] += 1
                    fw.dma(gp, lambda e, eidx=eidx, mg=mg, u=u: e.dma_start(
                        out=wpjT[u][:].rearrange("p (k m) -> p k m", k=HC),
                        in_=DR["p3_wpj"][eidx, :, :, mg * 256:(mg + 1) * 256].rearrange(
                            "k p m -> p k m")),
                        writes=(wpj_tags[u],))
                    for mi in range(2):
                        m2 = mg * 2 + mi
                        q = (m2 % 2) * 2
                        for k in range(HC):
                            for t in range(NT_B):
                                fw.op(te, lambda e, u=u, k=k, mi=mi, t=t, q=q: e.matmul(
                                    out=pPJ[q + t][:],
                                    lhsT=wpjT[u][:, k * 256 + mi * 128:k * 256 + (mi + 1) * 128],
                                    rhs=hid[k][:, t * T:(t + 1) * T],
                                    start=(k == 0), stop=(k == HC - 1)),
                                    reads=(wpj_tags[u], f"hid{k}"), writes=(f"pPJ{q + t}",))
                        for t in range(NT_B):
                            if eidx == 0:
                                fw.op(sc, lambda e, gu=gu, m2=m2, t=t, q=q: e.activation(
                                    out=acc[m2][:, t * T:(t + 1) * T], in_=pPJ[q + t][:],
                                    func=AF.Identity, bias=bpj[gu][:, m2:m2 + 1]),
                                    reads=(f"pPJ{q + t}", f"bpj{gu}"), writes=(f"acc{m2}",))
                            else:
                                fw.op(sc, lambda e, gu=gu, m2=m2, t=t, q=q: e.activation(
                                    out=tmp[:], in_=pPJ[q + t][:], func=AF.Identity,
                                    bias=bpj[gu][:, m2:m2 + 1]),
                                    reads=(f"pPJ{q + t}", f"bpj{gu}"), writes=("tmp",))
                                fw.op(v, lambda e, t=t: e.tensor_tensor(
                                    out=tmp[:], in0=tmp[:], in1=gbc[:, t * T:(t + 1) * T],
                                    op=ALU.mult),
                                    reads=("tmp", "gbc"), writes=("tmp",))
                                fw.op(v, lambda e, m2=m2, t=t: e.tensor_tensor(
                                    out=acc[m2][:, t * T:(t + 1) * T],
                                    in0=acc[m2][:, t * T:(t + 1) * T], in1=tmp[:], op=ALU.add),
                                    reads=("tmp", f"acc{m2}"), writes=(f"acc{m2}",))
            for t in range(NT_B):
                t0 = tb0 + t * T
                yv = [acc[c][:, t * T:(t + 1) * T] for c in range(DC)]
                xfv = [xf[c][:, 0:T] for c in range(DC)]
                ybfv = [ybf[c][:, 0:T] for c in range(DC)]
                sqbv = [sqb[0][:, 0:T], sqb[1][:, 0:T]]
                for c in range(DC):
                    fw.dma(sy, lambda e, b=b, c=c, t0=t0, xfv=xfv: e.dma_start(
                        out=xfv[c], in_=DR["x3_s"][b, c, :, t0:t0 + T]),
                        writes=(f"hid{c}",))
                    fw.op(v, lambda e, c=c, yv=yv, xfv=xfv: e.tensor_tensor(
                        out=yv[c], in0=yv[c], in1=xfv[c], op=ALU.add),
                        reads=(f"acc{c}", f"hid{c}"), writes=(f"acc{c}",))
                emit_postln(fw, yv, ytags,
                            ybfv, ybftags, sqbv, sqbtags, rows, rowtags,
                            bankA, "pFC2", pMB, pRB, ones_bf, ones128r,
                            psMBtag="pFC0", psRBtag="pFC1")
                for c in range(DC):
                    fw.dma(sy, lambda e, b=b, c=c, t0=t0, yv=yv: e.dma_start(
                        out=DR["out"][b, c, :, t0:t0 + T], in_=yv[c]),
                        reads=(f"acc{c}",))
        fw.flush(block)



def _ap(x):
    try:
        return x[:]
    except Exception:
        return x


def emit_postln(fw, y, ytags, ybf, ybftags, sqb, sqbtags, rows, rowtags,
                psA, psAtag, psMB, psRB, ones_bf, ones128r,
                psMBtag="psMB", psRBtag="psRB"):
    v, sc, te = "vector", "scalar", "tensor"
    ya = [_ap(t) for t in y]
    ybfa = [_ap(t) for t in ybf]
    sqba = [_ap(t) for t in sqb]
    rowsa = [_ap(t) for t in rows]
    mrow, vrow, srow, rstd = rowsa
    tm, tv, ts_, tr = rowtags
    for c in range(DC):
        fw.op(v, lambda e, c=c: e.tensor_copy(out=ybfa[c], in_=ya[c]),
              reads=(ytags[c],), writes=(ybftags[c],))
        fw.op(sc, lambda e, c=c: e.activation(out=sqba[c % 2], in_=ya[c], func=AF.Square),
              reads=(ytags[c],), writes=(sqbtags[c % 2],))
        fw.op(te, lambda e, c=c: e.matmul(out=psA[32:33, :], lhsT=ones_bf[:], rhs=sqba[c % 2],
                                          start=(c == 0), stop=(c == DC - 1)),
              reads=(sqbtags[c % 2], "ones"), writes=(psAtag,))
    for c in range(DC):
        fw.op(te, lambda e, c=c: e.matmul(out=psA[0:1, :], lhsT=ones_bf[:], rhs=ybfa[c],
                                          start=(c == 0), stop=(c == DC - 1)),
              reads=(ybftags[c], "ones"), writes=(psAtag,))
    fw.op(v, lambda e: e.tensor_scalar(out=mrow, in0=psA[0:1, :], scalar1=1.0 / D,
                                       scalar2=None, op0=ALU.mult),
          reads=(psAtag,), writes=(tm,))
    fw.op(v, lambda e: e.tensor_scalar(out=vrow, in0=psA[32:33, :], scalar1=1.0 / D,
                                       scalar2=EPS, op0=ALU.mult, op1=ALU.add),
          reads=(psAtag,), writes=(tv,))
    fw.op(v, lambda e: e.tensor_tensor(out=srow, in0=mrow, in1=mrow, op=ALU.mult),
          reads=(tm,), writes=(ts_,))
    fw.op(v, lambda e: e.tensor_tensor(out=vrow, in0=vrow, in1=srow, op=ALU.subtract),
          reads=(tv, ts_), writes=(tv,))
    fw.op(sc, lambda e: e.activation(out=srow, in_=vrow, func=AF.Sqrt),
          reads=(tv,), writes=(ts_,))
    fw.op(v, lambda e: e.reciprocal(out=rstd, in_=srow),
          reads=(ts_,), writes=(tr,))
    fw.op(te, lambda e: e.matmul(out=psMB[:], lhsT=ones128r[:], rhs=mrow, start=True, stop=True),
          reads=(tm, "ones"), writes=(psMBtag,))
    fw.op(te, lambda e: e.matmul(out=psRB[:], lhsT=ones128r[:], rhs=rstd, start=True, stop=True),
          reads=(tr, "ones"), writes=(psRBtag,))
    for c in range(DC):
        fw.op(v, lambda e, c=c: e.tensor_tensor(out=ya[c], in0=ya[c], in1=psMB[:],
                                                op=ALU.subtract),
              reads=(ytags[c], psMBtag), writes=(ytags[c],))
        fw.op(v, lambda e, c=c: e.tensor_tensor(out=ya[c], in0=ya[c], in1=psRB[:],
                                                op=ALU.mult),
              reads=(ytags[c], psRBtag), writes=(ytags[c],))



WEIGHT_SPECS = {
    "p0_wd": (DC, 128, D), "p0_bd": (128, DC), "p0_wb": (DC, 128, 16),
    "p0_bB": (16, 1), "p0_alog": (16, 1), "p0_wo": (16, D), "p0_bo": (128, DC),
    "p0_dp": (128, DC), "ident": (128, 128),
    "p1_w": (DC, 128, D), "p1_b": (128, DC),
    "p2_w1": (DC, 128, D), "p2_b1": (128, DC), "p2_w2": (DC, 128, D),
    "p2_b2": (128, DC), "p2_mw": (DC, 128, MEM + 1),
    "p3_wr": (DC, 128, NEXP), "p3_wfc": (NEXP + 1, DC, 128, HID),
    "p3_wpj": (NEXP + 1, HC, 128, D), "p3_bfc": (NEXP + 1, 128, HC),
    "p3_bpj": (NEXP + 1, 128, DC),
}

SCRATCH_SPECS = {
    "x1_s": ((NSEQ, DC, 128, S), F32),
    "x2_s": ((NSEQ, DC, 128, S), F32),
    "p_s": ((NSEQ, DC, 128, S), BF16),
    "tot_s": ((128, 2 * DC), F32),
    "band_s": ((NSEQ, DC, 128, S), BF16),
    "x3_s": ((NSEQ, DC, 128, S), F32),
    "gates_s": ((NEXP, NSEQ * S), BF16),
    "lmax_out": ((1, 1), F32),
}

PHASE_DEFS = {
    "p0": dict(weights=["p0_wd", "p0_bd", "p0_wb", "p0_bB", "p0_alog", "p0_wo",
                        "p0_bo", "p0_dp", "ident"],
               sin=["x_in"], sout=["x1_s", "lmax_out"]),
    "p1": dict(weights=["p1_w", "p1_b", "ident"],
               sin=["x1_s", "maxes"], sout=["x2_s"]),
    "p2a": dict(weights=["p2_w1", "p2_b1"], sin=["x2_s"], sout=["p_s", "tot_s"]),
    "p2b": dict(weights=["p2_w2", "p2_b2", "p2_mw", "ident"],
                sin=["p_s", "tot_s", "x2_s"], sout=["x3_s"], scratch=["band_s"]),
    "p3": dict(weights=["p3_wr", "p3_wfc", "p3_wpj", "p3_bfc", "p3_bpj", "ident"],
               sin=["x3_s"], sout=["out"], scratch=["gates_s"]),
}


def build_unfused(phase):
    nc = bass.Bass("TRN2", target_bir_lowering=False, debug=False)
    pd = PHASE_DEFS[phase]
    DR = {}
    for w in pd["weights"]:
        DR[w] = nc.dram_tensor(w, list(WEIGHT_SPECS[w]), F32, kind="ExternalInput").ap()
    for s in pd["sin"]:
        if s == "x_in":
            DR[s] = nc.dram_tensor("x_in", [NSEQ, DC, 128, S], F32, kind="ExternalInput").ap()
        elif s == "maxes":
            DR[s] = nc.dram_tensor("maxes", [1, NCORES], F32, kind="ExternalInput").ap()
        else:
            shp, dt = SCRATCH_SPECS[s]
            DR[s] = nc.dram_tensor(s, list(shp), dt, kind="ExternalInput").ap()
    for s in pd["sout"]:
        if s == "out":
            DR[s] = nc.dram_tensor("out", [NSEQ, DC, 128, S], F32, kind="ExternalOutput").ap()
        else:
            shp, dt = SCRATCH_SPECS[s]
            DR[s] = nc.dram_tensor(s, list(shp), dt, kind="ExternalOutput").ap()
    for s in pd.get("scratch", []):
        shp, dt = SCRATCH_SPECS[s]
        DR[s] = nc.dram_tensor(s, list(shp), dt).ap()

    with ExitStack() as es:
        fw = FW(nc, es)
        with nc.Block() as block, ExitStack() as pes:
            if phase == "p0":
                emit_p0(fw, block, DR, pes)
            elif phase == "p1":
                emit_p1(fw, block, DR, pes, n_max=NCORES)
            elif phase == "p2a":
                emit_p2a(fw, block, DR, pes)
            elif phase == "p2b":
                emit_p2b(fw, block, DR, pes)
            elif phase == "p3":
                emit_p3(fw, block, DR, pes)
    return nc


def build_fused():
    from concourse.replica_groups import maybe_share_collective_output_space
    nc = bass.Bass("TRN2", target_bir_lowering=False, debug=False)
    DR = {}
    for w, shp in WEIGHT_SPECS.items():
        DR[w] = nc.dram_tensor(w, list(shp), F32, kind="ExternalInput").ap()
    DR["x_in"] = nc.dram_tensor("x_in", [NSEQ, DC, 128, S], F32, kind="ExternalInput").ap()
    DR["out"] = nc.dram_tensor("out", [NSEQ, DC, 128, S], F32, kind="ExternalOutput").ap()
    for s, (shp, dt) in SCRATCH_SPECS.items():
        DR[s] = nc.dram_tensor(s, list(shp), dt).ap()
    aspace = maybe_share_collective_output_space("AllReduce", [list(range(NCORES))], prefer_local=True)
    DR["maxes"] = nc.dram_tensor("maxes", [1, 1], F32, addr_space=aspace).ap()

    with ExitStack() as es:
        fw = FW(nc, es)
        with nc.Block() as block:
            with ExitStack() as pes:
                emit_p0(fw, block, DR, pes)
            emit_cc(fw, block, DR)
            with ExitStack() as pes:
                emit_p1(fw, block, DR, pes, n_max=1)
            with ExitStack() as pes:
                emit_p2a(fw, block, DR, pes, flush=False)
                emit_p2b(fw, block, DR, pes)
            with ExitStack() as pes:
                emit_p3(fw, block, DR, pes)
    return nc



def kernel(**inputs):
    W = prep_weights(inputs)
    xs = shard_x(inputs["L0_attn_x"])

    if FUSED:
        nc = build_fused()
        in_maps = [dict(W, x_in=xs[c]) for c in range(NCORES)]
        res = run_bass_kernel_spmd(nc, in_maps, core_ids=list(range(NCORES)))
        return unshard_out([r["out"] for r in res.results])

    state = [dict() for _ in range(NCORES)]

    def run(phase, feed_keys, extra=None):
        nc = build_unfused(phase)
        pd = PHASE_DEFS[phase]
        in_maps = []
        for c in range(NCORES):
            m = {w: W[w] for w in pd["weights"]}
            for s in pd["sin"]:
                if s == "x_in":
                    m["x_in"] = xs[c]
                elif s == "maxes":
                    m["maxes"] = extra[c]
                else:
                    m[s] = state[c][s]
            in_maps.append(m)
        res = run_bass_kernel_spmd(nc, in_maps, core_ids=list(range(NCORES)))
        for c in range(NCORES):
            for s in pd["sout"]:
                state[c][s] = res.results[c][s]
        return res

    run("p0", None)
    maxes = np.concatenate([state[c]["lmax_out"].reshape(1, 1) for c in range(NCORES)],
                           axis=1).astype(np.float32)
    run("p1", None, extra=[maxes for _ in range(NCORES)])
    run("p2a", None)
    run("p2b", None)
    run("p3", None)
    return unshard_out([state[c]["out"] for c in range(NCORES)])


# version 09
# speedup vs baseline: 1.0137x; 1.0137x over previous
import numpy as np
from contextlib import ExitStack

import concourse.bass as bass
import concourse.mybir as mybir
from concourse.bass_utils import run_bass_kernel_spmd

F32 = mybir.dt.float32
BF16 = mybir.dt.bfloat16
AF = mybir.ActivationFunctionType
ALU = mybir.AluOpType
AX = mybir.AxisListType

MAGIC = float(np.float32(1.5 * 2 ** 23))

NCORES = 8
NSEQ = 2
S = 4096
D = 1024
DC = D // 128
T = 512
NT_SEQ = S // T
NTILES = NSEQ * NT_SEQ
EPS = 1e-5

HID = 4096
HC = HID // 128
NEXP = 8
TB = 1024
NT_B = TB // T
NBLK = NSEQ * S // TB
BLK_PER_SEQ = S // TB

MEM = 40

FUSED = True



def _cm(w):
    w = np.ascontiguousarray(np.asarray(w), dtype=np.float32)
    return np.ascontiguousarray(w.reshape(w.shape[0] // 128, 128, w.shape[1]))


def _cv(v):
    v = np.ascontiguousarray(np.asarray(v), dtype=np.float32)
    return np.ascontiguousarray(v.reshape(-1, 128).T)


def prep_weights(i):
    d = {}
    d["p0_wd"] = _cm(np.asarray(i["L0_attn_W_delta"]).T)
    d["p0_bd"] = _cv(i["L0_attn_b_delta"])
    d["p0_wb"] = _cm(np.asarray(i["L0_attn_W_B"]).T)
    d["p0_bB"] = np.ascontiguousarray(np.asarray(i["L0_attn_b_B"], dtype=np.float32).reshape(16, 1))
    d["p0_alog"] = np.ascontiguousarray(np.asarray(i["L0_attn_A_log"], dtype=np.float32).reshape(16, 1))
    d["p0_wo"] = np.ascontiguousarray(np.asarray(i["L0_attn_W_out"], dtype=np.float32).T)
    d["p0_bo"] = _cv(i["L0_attn_b_out"])
    d["p0_dp"] = _cv(i["L0_attn_D_param"])
    d["ident"] = np.eye(128, dtype=np.float32)
    d["p1_w"] = _cm(np.asarray(i["L0_ffn_weight"]).T)
    d["p1_b"] = _cv(i["L0_ffn_bias"])
    d["p2_w1"] = _cm(i["L1_attn_W1"])
    d["p2_b1"] = _cv(i["L1_attn_bias1"])
    d["p2_w2"] = _cm(i["L1_attn_W2"])
    d["p2_b2"] = _cv(i["L1_attn_bias2"])
    d["p2_mw"] = np.ascontiguousarray(
        np.asarray(i["L1_attn_memory_weights"], dtype=np.float32).T.reshape(DC, 128, MEM + 1))
    d["p3_wr"] = _cm(i["L1_ffn_Wr"])
    wfc = np.concatenate([np.asarray(i["L1_ffn_Wfc_s"], dtype=np.float32)[None],
                          np.asarray(i["L1_ffn_Wfc"], dtype=np.float32)], axis=0)
    d["p3_wfc"] = np.ascontiguousarray(wfc.reshape(NEXP + 1, DC, 128, HID))
    wpj = np.concatenate([np.asarray(i["L1_ffn_Wproj_s"], dtype=np.float32)[None],
                          np.asarray(i["L1_ffn_Wproj"], dtype=np.float32)], axis=0)
    d["p3_wpj"] = np.ascontiguousarray(wpj.reshape(NEXP + 1, HC, 128, D))
    bfc = np.concatenate([np.asarray(i["L1_ffn_bfc_s"], dtype=np.float32)[None],
                          np.asarray(i["L1_ffn_bfc"], dtype=np.float32)], axis=0)
    d["p3_bfc"] = np.ascontiguousarray(
        bfc.reshape(NEXP + 1, HC, 128).transpose(0, 2, 1))
    bpj = np.concatenate([np.asarray(i["L1_ffn_bproj_s"], dtype=np.float32)[None],
                          np.asarray(i["L1_ffn_bproj"], dtype=np.float32)], axis=0)
    d["p3_bpj"] = np.ascontiguousarray(
        bpj.reshape(NEXP + 1, DC, 128).transpose(0, 2, 1))
    return d


def shard_x(x):
    x = np.asarray(x, dtype=np.float32)
    out = []
    for c in range(NCORES):
        xc = x[NSEQ * c:NSEQ * (c + 1)]
        out.append(np.ascontiguousarray(
            xc.reshape(NSEQ, S, DC, 128).transpose(0, 2, 3, 1)))
    return out


def unshard_out(shards):
    out = np.empty((NCORES * NSEQ, S, D), dtype=np.float32)
    for c, o in enumerate(shards):
        out[NSEQ * c:NSEQ * (c + 1)] = np.asarray(o).transpose(0, 3, 1, 2).reshape(NSEQ, S, D)
    return out


def tile_bt(i):
    return i // NT_SEQ, (i % NT_SEQ) * T



ENGINES = ("sync", "gpsimd", "vector", "scalar", "tensor")
QUEUES = ("sync", "gpsimd")


class _Op:
    __slots__ = ("eng", "fn", "inc_val", "deps", "needs_inc", "inc_target",
                 "batch_end", "self_wait", "dma")

    def __init__(self, eng, fn, inc_val, dma):
        self.eng = eng
        self.fn = fn
        self.inc_val = inc_val
        self.dma = dma
        self.deps = []
        self.needs_inc = False
        self.inc_target = None
        self.batch_end = None
        self.self_wait = None


class FW:
    def __init__(self, nc, es):
        self.nc = nc
        self.sems = {e: es.enter_context(nc.semaphore(f"S_{e}")) for e in ENGINES}
        self.bsems = {q: es.enter_context(nc.semaphore(f"B_{q}")) for q in QUEUES}
        self.csems = {q: es.enter_context(nc.semaphore(f"C_{q}")) for q in QUEUES}
        self.icount = {e: 0 for e in ENGINES}
        self.bcount = {q: 0 for q in QUEUES}
        self.ccount = {q: 0 for q in QUEUES}
        self.prev_totals = None
        self.ops = []
        self.last_w = {}
        self.readers = {}

    def op(self, eng, fn, reads=(), writes=(), inc_val=1, dma=False):
        o = _Op(eng, fn, inc_val, dma)
        deps = {}
        for t in reads:
            w = self.last_w.get(t)
            if w is not None:
                deps[id(w)] = w
        for t in writes:
            w = self.last_w.get(t)
            if w is not None:
                deps[id(w)] = w
            for r in self.readers.get(t, ()):
                deps[id(r)] = r
        for t in reads:
            self.readers.setdefault(t, []).append(o)
        for t in writes:
            self.last_w[t] = o
            self.readers[t] = []
        for d in deps.values():
            if d is o:
                continue
            if d.eng == "tensor" and eng == "tensor":
                continue
            o.deps.append(d)
            d.needs_inc = True
        self.ops.append(o)
        return o

    def dma(self, queue, fn, reads=(), writes=()):
        return self.op(queue, fn, reads, writes, inc_val=16, dma=True)

    def flush(self, block):
        for o in self.ops:
            if o.eng in QUEUES and o.dma:
                self.icount[o.eng] += o.inc_val
                o.inc_target = self.icount[o.eng]
            elif o.eng in QUEUES and o.needs_inc:
                self.ccount[o.eng] += o.inc_val
                o.inc_target = self.ccount[o.eng]
            elif o.needs_inc:
                self.icount[o.eng] += o.inc_val
                o.inc_target = self.icount[o.eng]
        open_batch = {q: [] for q in QUEUES}
        for o in self.ops:
            for d in o.deps:
                if d.eng in QUEUES and d.dma and d.batch_end is None:
                    ob = open_batch[d.eng]
                    if ob:
                        last = ob[-1]
                        val = last.inc_target
                        for m in ob:
                            m.batch_end = val
                        last.self_wait = val
                        open_batch[d.eng] = []
            if o.eng in QUEUES and o.dma:
                open_batch[o.eng].append(o)
        for q in QUEUES:
            for m in open_batch[q]:
                m.batch_end = self.icount[q]

        queue_totals = {e: self.icount[e] for e in ENGINES}
        for e in ENGINES:
            if e not in QUEUES:
                self.icount[e] += 1
        for q in QUEUES:
            self.bcount[q] += 1
        new_totals = {e: (self.bcount[e] if e in QUEUES else self.icount[e])
                      for e in ENGINES}
        prev = self.prev_totals
        ops = self.ops
        sems = self.sems
        bsems = self.bsems
        csems = self.csems

        for ename in ENGINES:
            eng_ops = [o for o in ops if o.eng == ename]
            sec = getattr(block, ename)

            def _body(eng, ename=ename, eng_ops=eng_ops, prev=prev,
                      queue_totals=queue_totals):
                hwm = {}
                if prev is not None:
                    for oe in ENGINES:
                        if oe != ename:
                            eng.wait_ge(bsems[oe] if oe in QUEUES else sems[oe], prev[oe])
                            if oe not in QUEUES:
                                hwm[oe] = prev[oe]
                for o in eng_ops:
                    for d in o.deps:
                        if d.eng in QUEUES and d.dma:
                            key, sem, v = d.eng, sems[d.eng], d.batch_end
                        elif d.eng in QUEUES:
                            key, sem, v = "c" + d.eng, csems[d.eng], d.inc_target
                        else:
                            key, sem, v = d.eng, sems[d.eng], d.inc_target
                        if hwm.get(key, -1) < v:
                            eng.wait_ge(sem, v)
                            hwm[key] = v
                    inst = o.fn(eng)
                    if o.inc_target is not None:
                        if o.eng in QUEUES and not o.dma:
                            inst.then_inc(csems[o.eng], o.inc_val)
                        else:
                            inst.then_inc(sems[o.eng], o.inc_val)
                    if o.self_wait is not None:
                        eng.wait_ge(sems[ename], o.self_wait)
                        hwm[ename] = o.self_wait
                if ename in QUEUES:
                    eng.drain()
                    eng.wait_ge(sems[ename], queue_totals[ename])
                    eng.sem_inc(bsems[ename], 1)
                else:
                    eng.drain()
                    eng.sem_inc(sems[ename], 1)

            sec(_body)

        self.ops = []
        self.last_w = {}
        self.readers = {}
        self.prev_totals = new_totals




def emit_p0(fw, block, DR, es):
    nc = fw.nc
    sb = lambda n, s, dt=F32: es.enter_context(nc.sbuf_tensor("a0" + n, s, dt))
    psm = lambda n, s: es.enter_context(nc.psum_tensor("a0" + n, s, F32))
    v, sc, te, sy, gp = "vector", "scalar", "tensor", "sync", "gpsimd"

    wd = [sb(f"wd{k}", [128, D], BF16) for k in range(DC)]
    wb = [sb(f"wb{k}", [128, 16], BF16) for k in range(DC)]
    wo = sb("wo", [16, D], BF16)
    bd2 = sb("bd2", [128, DC])
    bB = sb("bB", [16, 1])
    alog = sb("alog", [16, 1])
    acol = sb("acol", [16, 1])
    bo = sb("bo", [128, DC])
    ecol = sb("ecol", [128, DC])
    ident = sb("ident", [128, 128])
    ones_bf = sb("onesbf", [128, 1], BF16)
    ones16 = sb("ones16", [1, 16])
    c16 = sb("c16", [1, 16])
    ones128r = sb("o128r", [1, 128])
    xf = [[sb(f"xf{u}_{k}", [128, T]) for k in range(DC)] for u in range(2)]
    xbt = [[sb(f"xb{u}_{k}", [128, T], BF16) for k in range(DC)] for u in range(2)]
    tnh = [sb(f"tnh{k}", [128, T], BF16) for k in range(DC)]
    sqb = [sb(f"sqb{k}", [128, T], BF16) for k in range(2)]
    bm = sb("bm", [16, T])
    abar = sb("abar", [16, T])
    bu = sb("bu", [16, T])
    states = sb("states", [16, T])
    st_bf = sb("stbf", [16, T], BF16)
    carry = sb("carry", [16, 1])
    delta = sb("delta", [1, T])
    grow = sb("grow", [1, T])
    y = [sb(f"y{k}", [128, T]) for k in range(DC)]
    ybf = [sb(f"ybf{k}", [128, T], BF16) for k in range(DC)]
    rows = [sb("mrow", [1, T]), sb("vrow", [1, T]), sb("srow", [1, T]), sb("rstd", [1, T])]
    mx = sb("mx", [128, DC])
    mxt = sb("mxt", [128, 1])
    lrow = sb("lrow", [1, 128])
    lmax = sb("lmax", [1, 1])

    pU = [psm("pU0", [128, T]), psm("pU1", [128, T])]
    pO = [psm("pO0", [128, T]), psm("pO1", [128, T])]
    pMB = psm("pMB", [128, T])
    pRB = psm("pRB", [128, T])
    bankA = psm("bankA", [128, T])
    bankB = psm("bankB", [128, T])

    fw.dma(sy, lambda e: e.dma_start(out=bd2[:], in_=DR["p0_bd"]), writes=("bd2",))
    fw.dma(sy, lambda e: e.dma_start(out=bo[:], in_=DR["p0_bo"]), writes=("bo",))
    fw.dma(sy, lambda e: e.dma_start(out=ecol[:], in_=DR["p0_dp"]), writes=("ecol",))
    fw.dma(sy, lambda e: e.dma_start(out=bB[:], in_=DR["p0_bB"]), writes=("bB",))
    fw.dma(sy, lambda e: e.dma_start(out=alog[:], in_=DR["p0_alog"]), writes=("alog",))
    fw.dma(sy, lambda e: e.dma_start(out=ident[:], in_=DR["ident"]), writes=("ident",))
    for k in range(DC):
        fw.dma(gp, lambda e, k=k: e.dma_start(out=wd[k][:], in_=DR["p0_wd"][k]), writes=(f"wd{k}",))
        fw.dma(gp, lambda e, k=k: e.dma_start(out=wb[k][:], in_=DR["p0_wb"][k]), writes=(f"wb{k}",))
    fw.dma(gp, lambda e: e.dma_start(out=wo[:], in_=DR["p0_wo"]), writes=("wo",))
    fw.op(v, lambda e: e.memset(ones_bf[:], 1.0), writes=("ones",))
    fw.op(v, lambda e: e.memset(ones16[:], 1.0), writes=("ones",))
    fw.op(v, lambda e: e.memset(c16[:], 1.0 / D), writes=("ones",))
    fw.op(v, lambda e: e.memset(ones128r[:], 1.0), writes=("ones",))
    fw.op(v, lambda e: e.memset(mx[:], 0.0), writes=("mx",))
    fw.op(v, lambda e: e.tensor_scalar(out=bd2[:], in0=bd2[:], scalar1=0.5, scalar2=None,
                                       op0=ALU.mult), reads=("bd2",), writes=("bd2",))
    fw.op(v, lambda e: e.tensor_scalar(out=ecol[:], in0=ecol[:], scalar1=1.0, scalar2=None,
                                       op0=ALU.add), reads=("ecol",), writes=("ecol",))
    fw.op(sc, lambda e: e.activation(out=acol[:], in_=alog[:], func=AF.Exp),
          reads=("alog",), writes=("acol",))
    fw.op(v, lambda e: e.tensor_scalar(out=acol[:], in0=acol[:], scalar1=-1.0, scalar2=None,
                                       op0=ALU.mult), reads=("acol",), writes=("acol",))

    ytags = [f"y{c}" for c in range(DC)]
    ybftags = [f"ybf{c}" for c in range(DC)]
    sqbtags = ["sqb0", "sqb1"]
    rowtags = ["mrow", "vrow", "srow", "rstd"]

    for i in range(NTILES):
        b, t0 = tile_bt(i)
        u = i % 2
        for c in range(DC):
            fw.dma(sy, lambda e, b=b, c=c, t0=t0, u=u:
                   e.dma_start(out=xf[u][c][:], in_=DR["x_in"][b, c, :, t0:t0 + T]),
                   writes=(f"xf{u}_{c}",))
            fw.dma(gp, lambda e, b=b, c=c, t0=t0, u=u:
                   e.dma_start(out=xbt[u][c][:], in_=DR["x_in"][b, c, :, t0:t0 + T]),
                   writes=(f"xb{u}_{c}",))
        if t0 == 0:
            fw.op(v, lambda e: e.memset(carry[:], 0.0), writes=("carry",))
        for m in range(DC):
            for k in range(DC):
                fw.op(te, lambda e, m=m, k=k, u=u:
                      e.matmul(out=pU[m % 2][:], lhsT=wd[k][:, m * 128:(m + 1) * 128],
                               rhs=xbt[u][k][:], start=(k == 0), stop=(k == DC - 1)),
                      reads=(f"wd{k}", f"xb{u}_{k}"), writes=(f"pU{m % 2}",))
            fw.op(sc, lambda e, m=m: e.activation(out=tnh[m][:], in_=pU[m % 2][:], func=AF.Tanh,
                                                  scale=0.5, bias=bd2[:, m:m + 1]),
                  reads=(f"pU{m % 2}", "bd2"), writes=(f"tnh{m}",))
        for k in range(DC):
            fw.op(te, lambda e, k=k, u=u:
                  e.matmul(out=bankB[32:48, :], lhsT=wb[k][:], rhs=xbt[u][k][:],
                           start=(k == 0), stop=(k == DC - 1)),
                  reads=(f"wb{k}", f"xb{u}_{k}"), writes=("bankB",))
        fw.op(sc, lambda e: e.activation(out=bm[:], in_=bankB[32:48, :], func=AF.Identity,
                                         bias=bB[:, 0:1]),
              reads=("bankB", "bB"), writes=("bm",))
        for k in range(DC):
            fw.op(te, lambda e, k=k, u=u:
                  e.matmul(out=bankA[32:33, :], lhsT=ones_bf[:], rhs=xbt[u][k][:],
                           start=(k == 0), stop=(k == DC - 1)),
                  reads=(f"xb{u}_{k}", "ones"), writes=("bankA",))
        for k in range(DC):
            fw.op(te, lambda e, k=k:
                  e.matmul(out=bankA[0:1, :], lhsT=ones_bf[:], rhs=tnh[k][:],
                           start=(k == 0), stop=(k == DC - 1)),
                  reads=(f"tnh{k}", "ones"), writes=("bankA",))
        fw.op(v, lambda e: e.tensor_scalar(out=delta[:], in0=bankA[0:1, :],
                                           scalar1=0.0495 / D, scalar2=0.0505,
                                           op0=ALU.mult, op1=ALU.add),
              reads=("bankA",), writes=("delta",))
        fw.op(v, lambda e: e.tensor_tensor(out=grow[:], in0=delta[:], in1=bankA[32:33, :],
                                           op=ALU.mult),
              reads=("delta", "bankA"), writes=("grow",))
        fw.op(te, lambda e: e.matmul(out=bankB[64:80, :], lhsT=ones16[:], rhs=delta[:],
                                     start=True, stop=True),
              reads=("delta", "ones"), writes=("bankB",))
        fw.op(sc, lambda e: e.activation(out=abar[:], in_=bankB[64:80, :], func=AF.Exp,
                                         scale=acol[:, 0:1]),
              reads=("bankB", "acol"), writes=("abar",))
        fw.op(te, lambda e: e.matmul(out=bankB[64:80, :], lhsT=c16[:], rhs=grow[:],
                                     start=True, stop=True),
              reads=("grow", "ones"), writes=("bankB",))
        fw.op(v, lambda e: e.tensor_tensor(out=bu[:], in0=bm[:], in1=bankB[64:80, :], op=ALU.mult),
              reads=("bm", "bankB"), writes=("bu",))
        fw.op(v, lambda e: e.tensor_tensor_scan(out=states[:], data0=abar[:], data1=bu[:],
                                                initial=carry[:, 0:1], op0=ALU.mult, op1=ALU.add),
              reads=("abar", "bu", "carry"), writes=("states",))
        fw.op(v, lambda e: e.tensor_copy(out=carry[:], in_=states[:, T - 1:T]),
              reads=("states",), writes=("carry",))
        fw.op(v, lambda e: e.tensor_copy(out=st_bf[:], in_=states[:]),
              reads=("states",), writes=("stbf",))
        for c in range(DC):
            fw.op(te, lambda e, c=c: e.matmul(out=pO[c % 2][:], lhsT=wo[:, c * 128:(c + 1) * 128],
                                              rhs=st_bf[:], start=True, stop=True),
                  reads=("wo", "stbf"), writes=(f"pO{c % 2}",))
            fw.op(sc, lambda e, c=c: e.activation(out=y[c][:], in_=pO[c % 2][:], func=AF.Identity,
                                                  bias=bo[:, c:c + 1]),
                  reads=(f"pO{c % 2}", "bo"), writes=(f"y{c}",))
            fw.op(v, lambda e, c=c, u=u: e.scalar_tensor_tensor(
                out=y[c][:], in0=xf[u][c][:], scalar=ecol[:, c:c + 1], in1=y[c][:],
                op0=ALU.mult, op1=ALU.add),
                reads=(f"xf{u}_{c}", "ecol", f"y{c}"), writes=(f"y{c}",))
        emit_postln(fw, y, ytags, ybf, ybftags, sqb, sqbtags, rows, rowtags,
                    bankA, "bankA", pMB, pRB, ones_bf, ones128r)
        for c in range(DC):
            fw.dma(sy, lambda e, b=b, c=c, t0=t0: e.dma_start(
                out=DR["x1_s"][b, c, :, t0:t0 + T], in_=y[c][:]),
                reads=(f"y{c}",))
            fw.op(v, lambda e, c=c: e.tensor_reduce(out=mxt[:], in_=y[c][:], axis=AX.X,
                                                    op=ALU.max, apply_absolute_value=True),
                  reads=(f"y{c}",), writes=("mxt",))
            fw.op(v, lambda e, c=c: e.tensor_tensor(out=mx[:, c:c + 1], in0=mx[:, c:c + 1],
                                                    in1=mxt[:], op=ALU.max),
                  reads=("mxt", "mx"), writes=("mx",))
    fw.op(v, lambda e: e.tensor_reduce(out=mxt[:], in_=mx[:], axis=AX.X, op=ALU.max),
          reads=("mx",), writes=("mxt",))
    fw.op(te, lambda e: e.transpose(out=bankA[0:1, 0:128], in_=mxt[:, 0:1], identity=ident[:]),
          reads=("mxt", "ident"), writes=("bankA",))
    fw.op(v, lambda e: e.tensor_copy(out=lrow[:], in_=bankA[0:1, 0:128]),
          reads=("bankA",), writes=("lrow",))
    fw.op(v, lambda e: e.tensor_reduce(out=lmax[:], in_=lrow[:], axis=AX.X, op=ALU.max),
          reads=("lrow",), writes=("lmax",))
    fw.dma(sy, lambda e: e.dma_start(out=DR["lmax_out"], in_=lmax[:]), reads=("lmax",))
    fw.flush(block)


def emit_cc(fw, block, DR):
    fw.op("gpsimd", lambda e: e.collective_compute(
        "AllReduce", ALU.max, replica_groups=[list(range(NCORES))],
        ins=[DR["lmax_out"]], outs=[DR["maxes"]]), inc_val=1, dma=True)
    fw.flush(block)



def emit_p1(fw, block, DR, es, n_max):
    nc = fw.nc
    sb = lambda n, s, dt=F32: es.enter_context(nc.sbuf_tensor("a1" + n, s, dt))
    psm = lambda n, s: es.enter_context(nc.psum_tensor("a1" + n, s, F32))
    v, sc, te, sy, gp = "vector", "scalar", "tensor", "sync", "gpsimd"

    w1f = [sb(f"w1f{k}", [128, D]) for k in range(DC)]
    qw = [sb(f"qw{k}", [128, D], BF16) for k in range(DC)]
    absw = sb("absw", [128, D])
    sgn = sb("sgn", [128, D])
    b1 = sb("b1", [128, DC])
    ident = sb("ident", [128, 128])
    ones_bf = sb("onesbf", [128, 1], BF16)
    ones128r = sb("o128r", [1, 128])
    mxs = sb("mxs", [1, n_max])
    gmax = sb("gmax", [1, 1])
    isc = sb("isc", [1, 1])
    wsis = sb("wsis", [1, 1])
    hws = sb("hws", [1, 1])
    wcol = sb("wcol", [128, DC])
    wcol1 = sb("wcol1", [128, 1])
    wrow = sb("wrow", [1, 128])
    ws = sb("ws", [1, 1])
    inv_col = sb("invcol", [128, 1])
    wsis_col = sb("wsiscol", [128, 1])
    hws_col = sb("hwscol", [128, 1])
    xf = [[sb(f"xf{u}_{k}", [128, T]) for k in range(DC)] for u in range(2)]
    qx = [sb(f"qx{k}", [128, T], BF16) for k in range(DC)]
    qt = [sb("qt0", [128, T]), sb("qt1", [128, T])]
    magic_col = sb("magic", [128, 1])
    y = [sb(f"y{k}", [128, T]) for k in range(DC)]
    ybf = [sb(f"ybf{k}", [128, T], BF16) for k in range(DC)]
    sqb = [sb(f"sqb{k}", [128, T], BF16) for k in range(2)]
    rows = [sb("mrow", [1, T]), sb("vrow", [1, T]), sb("srow", [1, T]), sb("rstd", [1, T])]

    pQ = [psm("pQ0", [128, T]), psm("pQ1", [128, T])]
    pMB = psm("pMB", [128, T])
    pRB = psm("pRB", [128, T])
    bankA = psm("bankA", [128, T])
    pCol = psm("pCol", [128, 512])

    for k in range(DC):
        fw.dma(sy, lambda e, k=k: e.dma_start(out=w1f[k][:], in_=DR["p1_w"][k]), writes=(f"w1f{k}",))
    fw.dma(sy, lambda e: e.dma_start(out=b1[:], in_=DR["p1_b"]), writes=("b1",))
    fw.dma(sy, lambda e: e.dma_start(out=ident[:], in_=DR["ident"]), writes=("ident",))
    fw.dma(sy, lambda e: e.dma_start(out=mxs[:], in_=DR["maxes"]), writes=("mxs",))
    fw.op(v, lambda e: e.memset(ones_bf[:], 1.0), writes=("ones",))
    fw.op(v, lambda e: e.memset(ones128r[:], 1.0), writes=("ones",))
    fw.op(v, lambda e: e.memset(magic_col[:], MAGIC), writes=("ones",))

    for k in range(DC):
        fw.op(v, lambda e, k=k: e.tensor_reduce(out=wcol[:, k:k + 1], in_=w1f[k][:], axis=AX.X,
                                                op=ALU.add, apply_absolute_value=True),
              reads=(f"w1f{k}",), writes=("wcol",))
    fw.op(v, lambda e: e.tensor_reduce(out=wcol1[:], in_=wcol[:], axis=AX.X, op=ALU.add),
          reads=("wcol",), writes=("wcol1",))
    fw.op(te, lambda e: e.transpose(out=bankA[0:1, 0:128], in_=wcol1[:, 0:1], identity=ident[:]),
          reads=("wcol1", "ident"), writes=("bankA",))
    fw.op(v, lambda e: e.tensor_copy(out=wrow[:], in_=bankA[0:1, 0:128]),
          reads=("bankA",), writes=("wrow",))
    fw.op(v, lambda e: e.tensor_reduce(out=ws[:], in_=wrow[:], axis=AX.X, op=ALU.add),
          reads=("wrow",), writes=("ws",))
    fw.op(v, lambda e: e.tensor_scalar(out=ws[:], in0=ws[:], scalar1=1.0 / (D * D), scalar2=None,
                                       op0=ALU.mult), reads=("ws",), writes=("ws",))
    fw.op(v, lambda e: e.tensor_scalar(out=hws[:], in0=ws[:], scalar1=0.5, scalar2=None,
                                       op0=ALU.mult), reads=("ws",), writes=("hws",))
    fw.op(v, lambda e: e.tensor_reduce(out=gmax[:], in_=mxs[:], axis=AX.X, op=ALU.max),
          reads=("mxs",), writes=("gmax",))
    fw.op(v, lambda e: e.tensor_scalar(out=isc[:], in0=gmax[:], scalar1=1.0 / 127, scalar2=None,
                                       op0=ALU.mult), reads=("gmax",), writes=("isc",))
    fw.op(v, lambda e: e.tensor_tensor(out=wsis[:], in0=ws[:], in1=isc[:], op=ALU.mult),
          reads=("ws", "isc"), writes=("wsis",))
    fw.op(v, lambda e: e.reciprocal(out=gmax[:], in_=isc[:]), reads=("isc",), writes=("ginv", "gmax"))
    fw.op(te, lambda e: e.matmul(out=pCol[:, 0:1], lhsT=ones128r[:], rhs=gmax[:],
                                 start=True, stop=True), reads=("ginv", "ones"), writes=("pCol",))
    fw.op(v, lambda e: e.tensor_copy(out=inv_col[:], in_=pCol[:, 0:1]),
          reads=("pCol",), writes=("invcol",))
    fw.op(te, lambda e: e.matmul(out=pCol[:, 1:2], lhsT=ones128r[:], rhs=wsis[:],
                                 start=True, stop=True), reads=("wsis", "ones"), writes=("pCol",))
    fw.op(v, lambda e: e.tensor_copy(out=wsis_col[:], in_=pCol[:, 1:2]),
          reads=("pCol",), writes=("wsiscol",))
    fw.op(te, lambda e: e.matmul(out=pCol[:, 2:3], lhsT=ones128r[:], rhs=hws[:],
                                 start=True, stop=True), reads=("hws", "ones"), writes=("pCol",))
    fw.op(v, lambda e: e.tensor_copy(out=hws_col[:], in_=pCol[:, 2:3]),
          reads=("pCol",), writes=("hwscol",))
    for k in range(DC):
        fw.op(sc, lambda e, k=k: e.activation(out=absw[:], in_=w1f[k][:], func=AF.Abs),
              reads=(f"w1f{k}",), writes=("absw",))
        fw.op(sc, lambda e, k=k: e.activation(out=sgn[:], in_=w1f[k][:], func=AF.Sign),
              reads=(f"w1f{k}",), writes=("sgn",))
        fw.op(v, lambda e: e.tensor_scalar(out=absw[:], in0=absw[:], scalar1=hws_col[:, 0:1],
                                           scalar2=None, op0=ALU.is_gt),
              reads=("absw", "hwscol"), writes=("absw",))
        fw.op(v, lambda e, k=k: e.tensor_tensor(out=qw[k][:], in0=sgn[:], in1=absw[:], op=ALU.mult),
              reads=("sgn", "absw"), writes=(f"qw{k}",))

    ytags = [f"y{c}" for c in range(DC)]
    ybftags = [f"ybf{c}" for c in range(DC)]
    sqbtags = ["sqb0", "sqb1"]
    rowtags = ["mrow", "vrow", "srow", "rstd"]

    for i in range(NTILES):
        b, t0 = tile_bt(i)
        u = i % 2
        lq = sy if i % 2 == 0 else gp
        for c in range(DC):
            fw.dma(lq, lambda e, b=b, c=c, t0=t0, u=u:
                   e.dma_start(out=xf[u][c][:], in_=DR["x1_s"][b, c, :, t0:t0 + T]),
                   writes=(f"xf{u}_{c}",))
        for c in range(DC):
            qb = c % 2
            fw.op(sc, lambda e, c=c, u=u, qb=qb: e.activation(
                out=qt[qb][:], in_=xf[u][c][:], func=AF.Identity,
                scale=inv_col[:, 0:1], bias=magic_col[:, 0:1]),
                reads=(f"xf{u}_{c}", "invcol", "ones"), writes=(f"qt{qb}",))
            fw.op(v, lambda e, qb=qb: e.tensor_scalar(
                out=qt[qb][:], in0=qt[qb][:], scalar1=MAGIC, scalar2=127.0,
                op0=ALU.subtract, op1=ALU.min),
                reads=(f"qt{qb}",), writes=(f"qt{qb}",))
            fw.op(v, lambda e, c=c, qb=qb: e.tensor_scalar(
                out=qx[c][:], in0=qt[qb][:], scalar1=-128.0, scalar2=None, op0=ALU.max),
                reads=(f"qt{qb}",), writes=(f"qx{c}",))
        for m in range(DC):
            for k in range(DC):
                fw.op(te, lambda e, m=m, k=k: e.matmul(
                    out=pQ[m % 2][:], lhsT=qw[k][:, m * 128:(m + 1) * 128], rhs=qx[k][:],
                    start=(k == 0), stop=(k == DC - 1)),
                    reads=(f"qw{k}", f"qx{k}"), writes=(f"pQ{m % 2}",))
            fw.op(sc, lambda e, m=m: e.activation(out=y[m][:], in_=pQ[m % 2][:], func=AF.Identity,
                                                  scale=wsis_col[:, 0:1], bias=b1[:, m:m + 1]),
                  reads=(f"pQ{m % 2}", "wsiscol", "b1"), writes=(f"y{m}",))
            fw.op(gp, lambda e, m=m, u=u: e.tensor_tensor(out=y[m][:], in0=y[m][:],
                                                          in1=xf[u][m][:], op=ALU.add),
                  reads=(f"y{m}", f"xf{u}_{m}"), writes=(f"y{m}",))
        emit_postln(fw, y, ytags, ybf, ybftags, sqb, sqbtags, rows, rowtags,
                    bankA, "bankA", pMB, pRB, ones_bf, ones128r)
        for c in range(DC):
            fw.dma(sy, lambda e, b=b, c=c, t0=t0: e.dma_start(
                out=DR["x2_s"][b, c, :, t0:t0 + T], in_=y[c][:]),
                reads=(f"y{c}",))
    fw.flush(block)



def emit_p2a(fw, block, DR, es, flush=True):
    nc = fw.nc
    sb = lambda n, s, dt=F32: es.enter_context(nc.sbuf_tensor("a2" + n, s, dt))
    psm = lambda n, s: es.enter_context(nc.psum_tensor("a2" + n, s, F32))
    v, sc, te, sy, gp = "vector", "scalar", "tensor", "sync", "gpsimd"

    w1 = [sb(f"w1{k}", [128, D], BF16) for k in range(DC)]
    b1 = sb("b1", [128, DC])
    xb2 = [[sb(f"xb{u}_{k}", [128, T], BF16) for k in range(DC)] for u in range(2)]
    pb = [sb(f"pb{k}", [128, T], BF16) for k in range(DC)]
    tcol = sb("tcol", [128, 1])
    tot = sb("tot", [128, 2 * DC])
    pP = [psm("pP0", [128, T])]

    for k in range(DC):
        fw.dma(gp, lambda e, k=k: e.dma_start(out=w1[k][:], in_=DR["p2_w1"][k]), writes=(f"w1{k}",))
    fw.dma(sy, lambda e: e.dma_start(out=b1[:], in_=DR["p2_b1"]), writes=("b1",))
    fw.op(v, lambda e: e.memset(tot[:], 0.0), writes=("tot",))

    for i in range(NTILES):
        b, t0 = tile_bt(i)
        u = i % 2
        for c in range(DC):
            fw.dma(gp, lambda e, b=b, c=c, t0=t0, u=u:
                   e.dma_start(out=xb2[u][c][:], in_=DR["x2_s"][b, c, :, t0:t0 + T]),
                   writes=(f"xb{u}_{c}",))
        for m in range(DC):
            for k in range(DC):
                fw.op(te, lambda e, m=m, k=k, u=u: e.matmul(
                    out=pP[0][:], lhsT=w1[k][:, m * 128:(m + 1) * 128], rhs=xb2[u][k][:],
                    start=(k == 0), stop=(k == DC - 1)),
                    reads=(f"w1{k}", f"xb{u}_{k}"), writes=("pP0",))
            fw.op(sc, lambda e, m=m: e.activation(out=pb[m][:], in_=pP[0][:], func=AF.Identity,
                                                  bias=b1[:, m:m + 1], accum_out=tcol[:]),
                  reads=("pP0", "b1"), writes=(f"pb{m}", "tcol"))
            j = b * DC + m
            fw.op(v, lambda e, j=j: e.tensor_tensor(out=tot[:, j:j + 1], in0=tot[:, j:j + 1],
                                                    in1=tcol[:], op=ALU.add),
                  reads=("tcol", "tot"), writes=("tot",))
            fw.dma(sy, lambda e, b=b, m=m, t0=t0: e.dma_start(
                out=DR["p_s"][b, m, :, t0:t0 + T], in_=pb[m][:]),
                reads=(f"pb{m}",), writes=(f"psd{b}_{m}",))
    fw.dma(sy, lambda e: e.dma_start(out=DR["tot_s"], in_=tot[:]), reads=("tot",),
           writes=("totd",))
    if flush:
        fw.flush(block)



def emit_p2b(fw, block, DR, es):
    nc = fw.nc
    sb = lambda n, s, dt=F32: es.enter_context(nc.sbuf_tensor("a3" + n, s, dt))
    psm = lambda n, s: es.enter_context(nc.psum_tensor("a3" + n, s, F32))
    v, sc, te, sy, gp = "vector", "scalar", "tensor", "sync", "gpsimd"

    w2 = [sb(f"w2{k}", [128, D], BF16) for k in range(DC)]
    b2 = sb("b2", [128, DC])
    mw = [sb(f"mw{k}", [128, MEM + 1]) for k in range(DC)]
    taps = [sb(f"tp{k}", [128, MEM]) for k in range(DC)]
    tot = sb("tot", [128, 2 * DC])
    ct = sb("ct", [128, 2 * DC])
    ident = sb("ident", [128, 128])
    diag = sb("diag", [128, 128 * MEM], BF16)
    pfull = [sb(f"pfl{u}", [128, MEM - 1 + S], BF16) for u in range(2)]
    bstage = [sb(f"bst{u}", [128, T], BF16) for u in range(2)]
    xf = [[sb(f"xf{u}_{k}", [128, T]) for k in range(DC)] for u in range(2)]
    bnd = [[sb(f"bnd{u}_{k}", [128, T], BF16) for k in range(DC)] for u in range(2)]
    y = [sb(f"y{k}", [128, T]) for k in range(DC)]
    ybf = [sb(f"ybf{k}", [128, T], BF16) for k in range(DC)]
    sqb = [sb(f"sqb{k}", [128, T], BF16) for k in range(2)]
    ones_bf = sb("onesbf", [128, 1], BF16)
    ones128r = sb("o128r", [1, 128])
    rows = [sb("mrow", [1, T]), sb("vrow", [1, T]), sb("srow", [1, T]), sb("rstd", [1, T])]
    pB = [psm("pB0", [128, T]), psm("pB1", [128, T])]
    pW = [psm("pW0", [128, T]), psm("pW1", [128, T])]
    pMB = psm("pMB", [128, T])
    pRB = psm("pRB", [128, T])
    bankA = psm("bankA", [128, T])

    for k in range(DC):
        fw.dma(gp, lambda e, k=k: e.dma_start(out=w2[k][:], in_=DR["p2_w2"][k]), writes=(f"w2{k}",))
        fw.dma(sy, lambda e, k=k: e.dma_start(out=mw[k][:], in_=DR["p2_mw"][k]), writes=(f"mw{k}",))
    fw.dma(sy, lambda e: e.dma_start(out=b2[:], in_=DR["p2_b2"]), writes=("b2",))
    fw.dma(sy, lambda e: e.dma_start(out=tot[:], in_=DR["tot_s"]), reads=("totd",), writes=("tot",))
    fw.dma(sy, lambda e: e.dma_start(out=ident[:], in_=DR["ident"]), writes=("ident",))
    fw.op(v, lambda e: e.memset(ones_bf[:], 1.0), writes=("ones",))
    fw.op(v, lambda e: e.memset(ones128r[:], 1.0), writes=("ones",))
    for k in range(DC):
        fw.op(v, lambda e, k=k: e.tensor_scalar(out=taps[k][:], in0=mw[k][:, 1:MEM + 1],
                                                scalar1=mw[k][:, 0:1], scalar2=None,
                                                op0=ALU.subtract),
              reads=(f"mw{k}",), writes=(f"tp{k}",))
    for b in range(NSEQ):
        for k in range(DC):
            j = b * DC + k
            fw.op(v, lambda e, j=j, k=k: e.tensor_tensor(
                out=ct[:, j:j + 1], in0=mw[k][:, 0:1], in1=tot[:, j:j + 1], op=ALU.mult),
                reads=(f"mw{k}", "tot"), writes=("ct",))

    for u in range(2):
        fw.op(v, lambda e, u=u: e.memset(pfull[u][:, 0:MEM - 1], 0.0), writes=(f"pfl{u}",))
    for b in range(NSEQ):
        for c in range(DC):
            for j in range(MEM):
                fw.op(v, lambda e, c=c, j=j: e.tensor_scalar(
                    out=diag[:, j * 128:(j + 1) * 128], in0=ident[:],
                    scalar1=taps[c][:, j:j + 1], scalar2=None, op0=ALU.mult),
                    reads=("ident", f"tp{c}"), writes=("diag",))
            u = (b * DC + c) % 2
            fw.dma(sy, lambda e, b=b, c=c, u=u: e.dma_start(
                out=pfull[u][:, MEM - 1:], in_=DR["p_s"][b, c, :, :]),
                reads=(f"psd{b}_{c}",), writes=(f"pfl{u}",))
            j0 = b * DC + c
            for tp in range(NT_SEQ // 2):
                for j in range(MEM):
                    for ti in range(2):
                        t8 = tp * 2 + ti
                        fw.op(te, lambda e, u=u, t8=t8, j=j, ti=ti: e.matmul(
                            out=pB[ti][:], lhsT=diag[:, j * 128:(j + 1) * 128],
                            rhs=pfull[u][:, MEM - 1 + t8 * T - j:MEM - 1 + t8 * T - j + T],
                            start=(j == 0), stop=(j == MEM - 1)),
                            reads=("diag", f"pfl{u}"), writes=(f"pB{ti}",))
                for ti in range(2):
                    t8 = tp * 2 + ti
                    fw.op(v, lambda e, ti=ti, j0=j0: e.tensor_scalar(
                        out=bstage[ti][:], in0=pB[ti][:],
                        scalar1=ct[:, j0:j0 + 1], scalar2=None, op0=ALU.add),
                        reads=(f"pB{ti}", "ct"), writes=(f"bst{ti}",))
                    fw.dma(sy, lambda e, b=b, c=c, t8=t8, ti=ti: e.dma_start(
                        out=DR["band_s"][b, c, :, t8 * T:(t8 + 1) * T], in_=bstage[ti][:]),
                        reads=(f"bst{ti}",), writes=(f"bdd{b}_{c}",))

    ytags = [f"y{c}" for c in range(DC)]
    ybftags = [f"ybf{c}" for c in range(DC)]
    sqbtags = ["sqb0", "sqb1"]
    rowtags = ["mrow", "vrow", "srow", "rstd"]

    for i in range(NTILES):
        b, t0 = tile_bt(i)
        u = i % 2
        lq = sy if i % 2 == 0 else gp
        for c in range(DC):
            fw.dma(lq, lambda e, b=b, c=c, t0=t0, u=u: e.dma_start(
                out=bnd[u][c][:], in_=DR["band_s"][b, c, :, t0:t0 + T]),
                reads=(f"bdd{b}_{c}",), writes=(f"bnd{u}_{c}",))
            fw.dma(lq, lambda e, b=b, c=c, t0=t0, u=u: e.dma_start(
                out=xf[u][c][:], in_=DR["x2_s"][b, c, :, t0:t0 + T]),
                writes=(f"xf{u}_{c}",))
        for m in range(DC):
            for k in range(DC):
                fw.op(te, lambda e, m=m, k=k, u=u: e.matmul(
                    out=pW[m % 2][:], lhsT=w2[k][:, m * 128:(m + 1) * 128], rhs=bnd[u][k][:],
                    start=(k == 0), stop=(k == DC - 1)),
                    reads=(f"w2{k}", f"bnd{u}_{k}"), writes=(f"pW{m % 2}",))
            fw.op(sc, lambda e, m=m: e.activation(out=y[m][:], in_=pW[m % 2][:], func=AF.Identity,
                                                  bias=b2[:, m:m + 1]),
                  reads=(f"pW{m % 2}", "b2"), writes=(f"y{m}",))
            fw.op(gp, lambda e, m=m, u=u: e.tensor_tensor(out=y[m][:], in0=y[m][:],
                                                          in1=xf[u][m][:], op=ALU.add),
                  reads=(f"y{m}", f"xf{u}_{m}"), writes=(f"y{m}",))
        emit_postln(fw, y, ytags, ybf, ybftags, sqb, sqbtags, rows, rowtags,
                    bankA, "bankA", pMB, pRB, ones_bf, ones128r)
        for c in range(DC):
            fw.dma(sy, lambda e, b=b, c=c, t0=t0: e.dma_start(
                out=DR["x3_s"][b, c, :, t0:t0 + T], in_=y[c][:]),
                reads=(f"y{c}",))
    fw.flush(block)



def emit_p3(fw, block, DR, es):
    nc = fw.nc
    v, sc, te, sy, gp = "vector", "scalar", "tensor", "sync", "gpsimd"

    with ExitStack() as ges:
        gsb = lambda n, s, dt=F32: ges.enter_context(nc.sbuf_tensor("g_" + n, s, dt))
        gps = lambda n, s: ges.enter_context(nc.psum_tensor("g_" + n, s, F32))
        wr = [gsb(f"wr{k}", [128, NEXP]) for k in range(DC)]
        xb3 = [[gsb(f"xb{u}_{k}", [128, T]) for k in range(DC)] for u in range(2)]
        et = gsb("et", [128, NEXP])
        msk = gsb("msk", [128, NEXP])
        em1 = gsb("em1", [128, 1])
        nem1 = gsb("nem1", [128, 1])
        em2 = gsb("em2", [128, 1])
        den = gsb("den", [128, 1])
        gt = gsb("gt", [128, NEXP])
        gtb = gsb("gtb", [8, 128], BF16)
        ident = gsb("ident", [128, 128])
        pLf = gps("pLf", [128, 512])
        pTf = gps("pTf", [128, 512])
        pL = pLf[:, 0:NEXP]
        pT = pTf[0:8, 0:128]

        fw.dma(sy, lambda e: e.dma_start(out=ident[:], in_=DR["ident"]), writes=("ident",))
        for k in range(DC):
            fw.dma(sy, lambda e, k=k: e.dma_start(out=wr[k][:], in_=DR["p3_wr"][k]),
                   writes=(f"wr{k}",))
        for i in range(NTILES):
            b, t0 = tile_bt(i)
            u = i % 2
            for c in range(DC):
                fw.dma(sy, lambda e, b=b, c=c, t0=t0, u=u: e.dma_start(
                    out=xb3[u][c][:], in_=DR["x3_s"][b, c, :, t0:t0 + T]),
                    writes=(f"xb{u}_{c}",))
            for g in range(T // 128):
                for k in range(DC):
                    fw.op(te, lambda e, g=g, k=k, u=u: e.matmul(
                        out=pL, lhsT=xb3[u][k][:, g * 128:(g + 1) * 128], rhs=wr[k][:],
                        start=(k == 0), stop=(k == DC - 1)),
                        reads=(f"xb{u}_{k}", f"wr{k}"), writes=("pL",))
                fw.op(v, lambda e: e.tensor_reduce(out=em1[:], in_=pL, axis=AX.X, op=ALU.max),
                      reads=("pL",), writes=("em1",))
                fw.op(v, lambda e: e.tensor_scalar(out=nem1[:], in0=em1[:], scalar1=-1.0,
                                                   scalar2=None, op0=ALU.mult),
                      reads=("em1",), writes=("nem1",))
                fw.op(sc, lambda e: e.activation(out=et[:], in_=pL, func=AF.Exp,
                                                 bias=nem1[:, 0:1]),
                      reads=("pL", "nem1"), writes=("et",))
                fw.op(v, lambda e: e.tensor_reduce(out=em1[:], in_=et[:], axis=AX.X, op=ALU.max),
                      reads=("et",), writes=("em1",))
                fw.op(v, lambda e: e.tensor_scalar(out=msk[:], in0=et[:], scalar1=em1[:, 0:1],
                                                   scalar2=None, op0=ALU.is_lt),
                      reads=("et", "em1"), writes=("msk",))
                fw.op(v, lambda e: e.tensor_tensor(out=msk[:], in0=msk[:], in1=et[:], op=ALU.mult),
                      reads=("msk", "et"), writes=("msk",))
                fw.op(v, lambda e: e.tensor_reduce(out=em2[:], in_=msk[:], axis=AX.X, op=ALU.max),
                      reads=("msk",), writes=("em2",))
                fw.op(v, lambda e: e.tensor_scalar(out=gt[:], in0=et[:], scalar1=em2[:, 0:1],
                                                   scalar2=None, op0=ALU.is_ge),
                      reads=("et", "em2"), writes=("gt",))
                fw.op(v, lambda e: e.tensor_tensor(out=gt[:], in0=gt[:], in1=et[:], op=ALU.mult),
                      reads=("gt", "et"), writes=("gt",))
                fw.op(v, lambda e: e.tensor_tensor(out=den[:], in0=em1[:], in1=em2[:], op=ALU.add),
                      reads=("em1", "em2"), writes=("den",))
                fw.op(v, lambda e: e.reciprocal(out=den[:], in_=den[:]),
                      reads=("den",), writes=("den",))
                fw.op(v, lambda e: e.tensor_scalar(out=gt[:], in0=gt[:], scalar1=den[:, 0:1],
                                                   scalar2=None, op0=ALU.mult),
                      reads=("gt", "den"), writes=("gt",))
                fw.op(te, lambda e: e.transpose(out=pT, in_=gt[:], identity=ident[:]),
                      reads=("gt", "ident"), writes=("pT",))
                fw.op(v, lambda e: e.tensor_copy(out=gtb[:], in_=pT),
                      reads=("pT",), writes=("gtb",))
                fw.dma(sy, lambda e, i=i, g=g: e.dma_start(
                    out=DR["gates_s"][:, i * T + g * 128:i * T + (g + 1) * 128], in_=gtb[:]),
                    reads=("gtb",))
        fw.flush(block)

    with ExitStack() as bes:
        bsb = lambda n, s, dt=F32: bes.enter_context(nc.sbuf_tensor("e_" + n, s, dt))
        bps = lambda n, s: bes.enter_context(nc.psum_tensor("e_" + n, s, F32))
        xb3 = [bsb(f"xb{k}", [128, TB], BF16) for k in range(DC)]
        acc = [bsb(f"acc{k}", [128, TB]) for k in range(DC)]
        hid = [bsb(f"hid{k}", [128, TB], BF16) for k in range(HC)]
        wfcT = [bsb(f"wfcT{u}", [128, DC * 1024], BF16) for u in range(2)]
        wpjT = [bsb(f"wpjT{u}", [128, HC * 256], BF16) for u in range(2)]
        bfc = [bsb(f"bfc{u}", [128, HC]) for u in range(2)]
        bpj = [bsb(f"bpj{u}", [128, DC]) for u in range(2)]
        grow = [bsb(f"grow{u}", [1, TB], BF16) for u in range(2)]
        gbc = bsb("gbc", [128, TB], BF16)
        tmp = bsb("tmp", [128, T])
        ones_bf = bsb("onesbf", [128, 1], BF16)
        ones128r = bsb("o128r", [1, 128])
        ones128r_bf = bsb("o128rb", [1, 128], BF16)
        pFC = [bps(f"pFC{q}", [128, T]) for q in range(4)]
        pPJ = [bps(f"pPJ{q}", [128, T]) for q in range(4)]
        xf = [hid[c][:].bitcast(F32) for c in range(DC)]
        ybf = [hid[DC + c] for c in range(DC)]
        sqb = [hid[2 * DC], hid[2 * DC + 1]]
        rows = [hid[2 * DC + 2 + j][:].bitcast(F32)[0:1, 0:T] for j in range(4)]
        pMB = pFC[0]
        pRB = pFC[1]
        bankA = pFC[2]

        fw.op(v, lambda e: e.memset(ones_bf[:], 1.0), writes=("ones",))
        fw.op(v, lambda e: e.memset(ones128r[:], 1.0), writes=("ones",))
        fw.op(v, lambda e: e.memset(ones128r_bf[:], 1.0), writes=("ones",))

        wfc_tags = tuple(f"wfcT{u}" for u in range(2))
        wpj_tags = tuple(f"wpjT{u}" for u in range(2))

        ytags = [f"acc{c}" for c in range(DC)]
        ybftags = [f"hid{DC + c}" for c in range(DC)]
        sqbtags = [f"hid{2 * DC}", f"hid{2 * DC + 1}"]
        rowtags = [f"hid{2 * DC + 2 + j}" for j in range(4)]

        mg_ctr = [0]
        pj_ctr = [0]

        for blk in range(NBLK):
            b = blk // BLK_PER_SEQ
            tb0 = (blk % BLK_PER_SEQ) * TB
            for k in range(DC):
                fw.dma(gp, lambda e, b=b, k=k, tb0=tb0: e.dma_start(
                    out=xb3[k][:], in_=DR["x3_s"][b, k, :, tb0:tb0 + TB]),
                    writes=(f"xb3{k}",))
            for eidx in range(NEXP + 1):
                gu = eidx % 2
                if eidx > 0:
                    fw.dma(sy, lambda e, eidx=eidx, blk=blk, gu=gu: e.dma_start(
                        out=grow[gu][:], in_=DR["gates_s"][eidx - 1:eidx, blk * TB:(blk + 1) * TB]),
                        writes=(f"grow{gu}",))
                for mg in range(4):
                    u = mg_ctr[0] % 2
                    mg_ctr[0] += 1
                    fw.dma(gp, lambda e, eidx=eidx, mg=mg, u=u: e.dma_start(
                        out=wfcT[u][:].rearrange("p (k m) -> p k m", k=DC),
                        in_=DR["p3_wfc"][eidx, :, :, mg * 1024:(mg + 1) * 1024].rearrange(
                            "k p m -> p k m")),
                        writes=(wfc_tags[u],))
                    if mg == 0:
                        fw.dma(gp, lambda e, eidx=eidx, gu=gu: e.dma_start(
                            out=bfc[gu][:], in_=DR["p3_bfc"][eidx]),
                            writes=(f"bfc{gu}",))
                        fw.dma(gp, lambda e, eidx=eidx, gu=gu: e.dma_start(
                            out=bpj[gu][:], in_=DR["p3_bpj"][eidx]),
                            writes=(f"bpj{gu}",))
                    for mi in range(DC):
                        m = mg * DC + mi
                        q = (mi % 2) * 2
                        for k in range(DC):
                            for t in range(NT_B):
                                fw.op(te, lambda e, u=u, k=k, mi=mi, t=t, q=q: e.matmul(
                                    out=pFC[q + t][:],
                                    lhsT=wfcT[u][:, k * 1024 + mi * 128:k * 1024 + (mi + 1) * 128],
                                    rhs=xb3[k][:, t * T:(t + 1) * T],
                                    start=(k == 0), stop=(k == DC - 1)),
                                    reads=(wfc_tags[u], f"xb3{k}"), writes=(f"pFC{q + t}",))
                        for t in range(NT_B):
                            fw.op(sc, lambda e, gu=gu, m=m, t=t, q=q: e.activation(
                                out=hid[m][:, t * T:(t + 1) * T], in_=pFC[q + t][:],
                                func=AF.Gelu_apprx_tanh, bias=bfc[gu][:, m:m + 1]),
                                reads=(f"pFC{q + t}", f"bfc{gu}"), writes=(f"hid{m}",))
                if eidx > 0:
                    for t in range(NT_B):
                        fw.op(te, lambda e, gu=gu, t=t: e.matmul(
                            out=pFC[t][:], lhsT=ones128r_bf[:], rhs=grow[gu][0:1, t * T:(t + 1) * T],
                            start=True, stop=True),
                            reads=(f"grow{gu}", "ones"), writes=(f"pFC{t}",))
                        fw.op(v, lambda e, t=t: e.tensor_copy(out=gbc[:, t * T:(t + 1) * T],
                                                              in_=pFC[t][:]),
                              reads=(f"pFC{t}",), writes=("gbc",))
                for mg in range(4):
                    u = pj_ctr[0] % 2
                    pj_ctr[0] += 1
                    fw.dma(gp, lambda e, eidx=eidx, mg=mg, u=u: e.dma_start(
                        out=wpjT[u][:].rearrange("p (k m) -> p k m", k=HC),
                        in_=DR["p3_wpj"][eidx, :, :, mg * 256:(mg + 1) * 256].rearrange(
                            "k p m -> p k m")),
                        writes=(wpj_tags[u],))
                    for mi in range(2):
                        m2 = mg * 2 + mi
                        q = (m2 % 2) * 2
                        for k in range(HC):
                            for t in range(NT_B):
                                fw.op(te, lambda e, u=u, k=k, mi=mi, t=t, q=q: e.matmul(
                                    out=pPJ[q + t][:],
                                    lhsT=wpjT[u][:, k * 256 + mi * 128:k * 256 + (mi + 1) * 128],
                                    rhs=hid[k][:, t * T:(t + 1) * T],
                                    start=(k == 0), stop=(k == HC - 1)),
                                    reads=(wpj_tags[u], f"hid{k}"), writes=(f"pPJ{q + t}",))
                        for t in range(NT_B):
                            if eidx == 0:
                                fw.op(sc, lambda e, gu=gu, m2=m2, t=t, q=q: e.activation(
                                    out=acc[m2][:, t * T:(t + 1) * T], in_=pPJ[q + t][:],
                                    func=AF.Identity, bias=bpj[gu][:, m2:m2 + 1]),
                                    reads=(f"pPJ{q + t}", f"bpj{gu}"), writes=(f"acc{m2}",))
                            else:
                                fw.op(sc, lambda e, gu=gu, m2=m2, t=t, q=q: e.activation(
                                    out=tmp[:], in_=pPJ[q + t][:], func=AF.Identity,
                                    bias=bpj[gu][:, m2:m2 + 1]),
                                    reads=(f"pPJ{q + t}", f"bpj{gu}"), writes=("tmp",))
                                fw.op(v, lambda e, t=t: e.tensor_tensor(
                                    out=tmp[:], in0=tmp[:], in1=gbc[:, t * T:(t + 1) * T],
                                    op=ALU.mult),
                                    reads=("tmp", "gbc"), writes=("tmp",))
                                fw.op(v, lambda e, m2=m2, t=t: e.tensor_tensor(
                                    out=acc[m2][:, t * T:(t + 1) * T],
                                    in0=acc[m2][:, t * T:(t + 1) * T], in1=tmp[:], op=ALU.add),
                                    reads=("tmp", f"acc{m2}"), writes=(f"acc{m2}",))
            for t in range(NT_B):
                t0 = tb0 + t * T
                yv = [acc[c][:, t * T:(t + 1) * T] for c in range(DC)]
                xfv = [xf[c][:, 0:T] for c in range(DC)]
                ybfv = [ybf[c][:, 0:T] for c in range(DC)]
                sqbv = [sqb[0][:, 0:T], sqb[1][:, 0:T]]
                for c in range(DC):
                    fw.dma(sy, lambda e, b=b, c=c, t0=t0, xfv=xfv: e.dma_start(
                        out=xfv[c], in_=DR["x3_s"][b, c, :, t0:t0 + T]),
                        writes=(f"hid{c}",))
                    fw.op(v, lambda e, c=c, yv=yv, xfv=xfv: e.tensor_tensor(
                        out=yv[c], in0=yv[c], in1=xfv[c], op=ALU.add),
                        reads=(f"acc{c}", f"hid{c}"), writes=(f"acc{c}",))
                emit_postln(fw, yv, ytags,
                            ybfv, ybftags, sqbv, sqbtags, rows, rowtags,
                            bankA, "pFC2", pMB, pRB, ones_bf, ones128r,
                            psMBtag="pFC0", psRBtag="pFC1")
                for c in range(DC):
                    fw.dma(sy, lambda e, b=b, c=c, t0=t0, yv=yv: e.dma_start(
                        out=DR["out"][b, c, :, t0:t0 + T], in_=yv[c]),
                        reads=(f"acc{c}",))
        fw.flush(block)



def _ap(x):
    try:
        return x[:]
    except Exception:
        return x


def emit_postln(fw, y, ytags, ybf, ybftags, sqb, sqbtags, rows, rowtags,
                psA, psAtag, psMB, psRB, ones_bf, ones128r,
                psMBtag="psMB", psRBtag="psRB"):
    v, sc, te = "vector", "scalar", "tensor"
    ya = [_ap(t) for t in y]
    ybfa = [_ap(t) for t in ybf]
    sqba = [_ap(t) for t in sqb]
    rowsa = [_ap(t) for t in rows]
    mrow, vrow, srow, rstd = rowsa
    tm, tv, ts_, tr = rowtags
    for c in range(DC):
        fw.op(v, lambda e, c=c: e.tensor_copy(out=ybfa[c], in_=ya[c]),
              reads=(ytags[c],), writes=(ybftags[c],))
        fw.op(sc, lambda e, c=c: e.activation(out=sqba[c % 2], in_=ya[c], func=AF.Square),
              reads=(ytags[c],), writes=(sqbtags[c % 2],))
        fw.op(te, lambda e, c=c: e.matmul(out=psA[32:33, :], lhsT=ones_bf[:], rhs=sqba[c % 2],
                                          start=(c == 0), stop=(c == DC - 1)),
              reads=(sqbtags[c % 2], "ones"), writes=(psAtag,))
    for c in range(DC):
        fw.op(te, lambda e, c=c: e.matmul(out=psA[0:1, :], lhsT=ones_bf[:], rhs=ybfa[c],
                                          start=(c == 0), stop=(c == DC - 1)),
              reads=(ybftags[c], "ones"), writes=(psAtag,))
    fw.op(v, lambda e: e.tensor_scalar(out=mrow, in0=psA[0:1, :], scalar1=1.0 / D,
                                       scalar2=None, op0=ALU.mult),
          reads=(psAtag,), writes=(tm,))
    fw.op(v, lambda e: e.tensor_scalar(out=vrow, in0=psA[32:33, :], scalar1=1.0 / D,
                                       scalar2=EPS, op0=ALU.mult, op1=ALU.add),
          reads=(psAtag,), writes=(tv,))
    fw.op(v, lambda e: e.tensor_tensor(out=srow, in0=mrow, in1=mrow, op=ALU.mult),
          reads=(tm,), writes=(ts_,))
    fw.op(v, lambda e: e.tensor_tensor(out=vrow, in0=vrow, in1=srow, op=ALU.subtract),
          reads=(tv, ts_), writes=(tv,))
    fw.op(sc, lambda e: e.activation(out=srow, in_=vrow, func=AF.Sqrt),
          reads=(tv,), writes=(ts_,))
    fw.op(v, lambda e: e.reciprocal(out=rstd, in_=srow),
          reads=(ts_,), writes=(tr,))
    fw.op(te, lambda e: e.matmul(out=psMB[:], lhsT=ones128r[:], rhs=mrow, start=True, stop=True),
          reads=(tm, "ones"), writes=(psMBtag,))
    fw.op(te, lambda e: e.matmul(out=psRB[:], lhsT=ones128r[:], rhs=rstd, start=True, stop=True),
          reads=(tr, "ones"), writes=(psRBtag,))
    for c in range(DC):
        fw.op(v, lambda e, c=c: e.tensor_tensor(out=ya[c], in0=ya[c], in1=psMB[:],
                                                op=ALU.subtract),
              reads=(ytags[c], psMBtag), writes=(ytags[c],))
        fw.op(v, lambda e, c=c: e.tensor_tensor(out=ya[c], in0=ya[c], in1=psRB[:],
                                                op=ALU.mult),
              reads=(ytags[c], psRBtag), writes=(ytags[c],))



WEIGHT_SPECS = {
    "p0_wd": (DC, 128, D), "p0_bd": (128, DC), "p0_wb": (DC, 128, 16),
    "p0_bB": (16, 1), "p0_alog": (16, 1), "p0_wo": (16, D), "p0_bo": (128, DC),
    "p0_dp": (128, DC), "ident": (128, 128),
    "p1_w": (DC, 128, D), "p1_b": (128, DC),
    "p2_w1": (DC, 128, D), "p2_b1": (128, DC), "p2_w2": (DC, 128, D),
    "p2_b2": (128, DC), "p2_mw": (DC, 128, MEM + 1),
    "p3_wr": (DC, 128, NEXP), "p3_wfc": (NEXP + 1, DC, 128, HID),
    "p3_wpj": (NEXP + 1, HC, 128, D), "p3_bfc": (NEXP + 1, 128, HC),
    "p3_bpj": (NEXP + 1, 128, DC),
}

SCRATCH_SPECS = {
    "x1_s": ((NSEQ, DC, 128, S), F32),
    "x2_s": ((NSEQ, DC, 128, S), F32),
    "p_s": ((NSEQ, DC, 128, S), BF16),
    "tot_s": ((128, 2 * DC), F32),
    "band_s": ((NSEQ, DC, 128, S), BF16),
    "x3_s": ((NSEQ, DC, 128, S), F32),
    "gates_s": ((NEXP, NSEQ * S), BF16),
    "lmax_out": ((1, 1), F32),
}

PHASE_DEFS = {
    "p0": dict(weights=["p0_wd", "p0_bd", "p0_wb", "p0_bB", "p0_alog", "p0_wo",
                        "p0_bo", "p0_dp", "ident"],
               sin=["x_in"], sout=["x1_s", "lmax_out"]),
    "p1": dict(weights=["p1_w", "p1_b", "ident"],
               sin=["x1_s", "maxes"], sout=["x2_s"]),
    "p2a": dict(weights=["p2_w1", "p2_b1"], sin=["x2_s"], sout=["p_s", "tot_s"]),
    "p2b": dict(weights=["p2_w2", "p2_b2", "p2_mw", "ident"],
                sin=["p_s", "tot_s", "x2_s"], sout=["x3_s"], scratch=["band_s"]),
    "p3": dict(weights=["p3_wr", "p3_wfc", "p3_wpj", "p3_bfc", "p3_bpj", "ident"],
               sin=["x3_s"], sout=["out"], scratch=["gates_s"]),
}


def build_unfused(phase):
    nc = bass.Bass("TRN2", target_bir_lowering=False, debug=False)
    pd = PHASE_DEFS[phase]
    DR = {}
    for w in pd["weights"]:
        DR[w] = nc.dram_tensor(w, list(WEIGHT_SPECS[w]), F32, kind="ExternalInput").ap()
    for s in pd["sin"]:
        if s == "x_in":
            DR[s] = nc.dram_tensor("x_in", [NSEQ, DC, 128, S], F32, kind="ExternalInput").ap()
        elif s == "maxes":
            DR[s] = nc.dram_tensor("maxes", [1, NCORES], F32, kind="ExternalInput").ap()
        else:
            shp, dt = SCRATCH_SPECS[s]
            DR[s] = nc.dram_tensor(s, list(shp), dt, kind="ExternalInput").ap()
    for s in pd["sout"]:
        if s == "out":
            DR[s] = nc.dram_tensor("out", [NSEQ, DC, 128, S], F32, kind="ExternalOutput").ap()
        else:
            shp, dt = SCRATCH_SPECS[s]
            DR[s] = nc.dram_tensor(s, list(shp), dt, kind="ExternalOutput").ap()
    for s in pd.get("scratch", []):
        shp, dt = SCRATCH_SPECS[s]
        DR[s] = nc.dram_tensor(s, list(shp), dt).ap()

    with ExitStack() as es:
        fw = FW(nc, es)
        with nc.Block() as block, ExitStack() as pes:
            if phase == "p0":
                emit_p0(fw, block, DR, pes)
            elif phase == "p1":
                emit_p1(fw, block, DR, pes, n_max=NCORES)
            elif phase == "p2a":
                emit_p2a(fw, block, DR, pes)
            elif phase == "p2b":
                emit_p2b(fw, block, DR, pes)
            elif phase == "p3":
                emit_p3(fw, block, DR, pes)
    return nc


def build_fused():
    from concourse.replica_groups import maybe_share_collective_output_space
    nc = bass.Bass("TRN2", target_bir_lowering=False, debug=False)
    DR = {}
    for w, shp in WEIGHT_SPECS.items():
        DR[w] = nc.dram_tensor(w, list(shp), F32, kind="ExternalInput").ap()
    DR["x_in"] = nc.dram_tensor("x_in", [NSEQ, DC, 128, S], F32, kind="ExternalInput").ap()
    DR["out"] = nc.dram_tensor("out", [NSEQ, DC, 128, S], F32, kind="ExternalOutput").ap()
    for s, (shp, dt) in SCRATCH_SPECS.items():
        DR[s] = nc.dram_tensor(s, list(shp), dt).ap()
    aspace = maybe_share_collective_output_space("AllReduce", [list(range(NCORES))], prefer_local=True)
    DR["maxes"] = nc.dram_tensor("maxes", [1, 1], F32, addr_space=aspace).ap()

    with ExitStack() as es:
        fw = FW(nc, es)
        with nc.Block() as block:
            with ExitStack() as pes:
                emit_p0(fw, block, DR, pes)
            emit_cc(fw, block, DR)
            with ExitStack() as pes:
                emit_p1(fw, block, DR, pes, n_max=1)
            with ExitStack() as pes:
                emit_p2a(fw, block, DR, pes, flush=False)
                emit_p2b(fw, block, DR, pes)
            with ExitStack() as pes:
                emit_p3(fw, block, DR, pes)
    return nc



def kernel(**inputs):
    W = prep_weights(inputs)
    xs = shard_x(inputs["L0_attn_x"])

    if FUSED:
        nc = build_fused()
        in_maps = [dict(W, x_in=xs[c]) for c in range(NCORES)]
        res = run_bass_kernel_spmd(nc, in_maps, core_ids=list(range(NCORES)))
        return unshard_out([r["out"] for r in res.results])

    state = [dict() for _ in range(NCORES)]

    def run(phase, feed_keys, extra=None):
        nc = build_unfused(phase)
        pd = PHASE_DEFS[phase]
        in_maps = []
        for c in range(NCORES):
            m = {w: W[w] for w in pd["weights"]}
            for s in pd["sin"]:
                if s == "x_in":
                    m["x_in"] = xs[c]
                elif s == "maxes":
                    m["maxes"] = extra[c]
                else:
                    m[s] = state[c][s]
            in_maps.append(m)
        res = run_bass_kernel_spmd(nc, in_maps, core_ids=list(range(NCORES)))
        for c in range(NCORES):
            for s in pd["sout"]:
                state[c][s] = res.results[c][s]
        return res

    run("p0", None)
    maxes = np.concatenate([state[c]["lmax_out"].reshape(1, 1) for c in range(NCORES)],
                           axis=1).astype(np.float32)
    run("p1", None, extra=[maxes for _ in range(NCORES)])
    run("p2a", None)
    run("p2b", None)
    run("p3", None)
    return unshard_out([state[c]["out"] for c in range(NCORES)])


# version 11 (final)
# speedup vs baseline: 1.0196x; 1.0059x over previous
import numpy as np
from contextlib import ExitStack

import concourse.bass as bass
import concourse.mybir as mybir
from concourse.bass_utils import run_bass_kernel_spmd

F32 = mybir.dt.float32
BF16 = mybir.dt.bfloat16
AF = mybir.ActivationFunctionType
ALU = mybir.AluOpType
AX = mybir.AxisListType

MAGIC = float(np.float32(1.5 * 2 ** 23))

NCORES = 8
NSEQ = 2
S = 4096
D = 1024
DC = D // 128
T = 512
NT_SEQ = S // T
NTILES = NSEQ * NT_SEQ
EPS = 1e-5

HID = 4096
HC = HID // 128
NEXP = 8
TB = 1024
NT_B = TB // T
NBLK = NSEQ * S // TB
BLK_PER_SEQ = S // TB

MEM = 40

FUSED = True



def _cm(w):
    w = np.ascontiguousarray(np.asarray(w), dtype=np.float32)
    return np.ascontiguousarray(w.reshape(w.shape[0] // 128, 128, w.shape[1]))


def _cv(v):
    v = np.ascontiguousarray(np.asarray(v), dtype=np.float32)
    return np.ascontiguousarray(v.reshape(-1, 128).T)


def prep_weights(i):
    d = {}
    d["p0_wd"] = _cm(np.asarray(i["L0_attn_W_delta"]).T)
    d["p0_bd"] = _cv(i["L0_attn_b_delta"])
    d["p0_wb"] = _cm(np.asarray(i["L0_attn_W_B"]).T)
    d["p0_bB"] = np.ascontiguousarray(np.asarray(i["L0_attn_b_B"], dtype=np.float32).reshape(16, 1))
    d["p0_alog"] = np.ascontiguousarray(np.asarray(i["L0_attn_A_log"], dtype=np.float32).reshape(16, 1))
    d["p0_wo"] = np.ascontiguousarray(np.asarray(i["L0_attn_W_out"], dtype=np.float32).T)
    d["p0_bo"] = _cv(i["L0_attn_b_out"])
    d["p0_dp"] = _cv(i["L0_attn_D_param"])
    d["ident"] = np.eye(128, dtype=np.float32)
    d["p1_w"] = _cm(np.asarray(i["L0_ffn_weight"]).T)
    d["p1_b"] = _cv(i["L0_ffn_bias"])
    d["p2_w1"] = _cm(i["L1_attn_W1"])
    d["p2_b1"] = _cv(i["L1_attn_bias1"])
    d["p2_w2"] = _cm(i["L1_attn_W2"])
    d["p2_b2"] = _cv(i["L1_attn_bias2"])
    d["p2_mw"] = np.ascontiguousarray(
        np.asarray(i["L1_attn_memory_weights"], dtype=np.float32).T.reshape(DC, 128, MEM + 1))
    d["p3_wr"] = _cm(i["L1_ffn_Wr"])
    wfc = np.concatenate([np.asarray(i["L1_ffn_Wfc_s"], dtype=np.float32)[None],
                          np.asarray(i["L1_ffn_Wfc"], dtype=np.float32)], axis=0)
    d["p3_wfc"] = np.ascontiguousarray(wfc.reshape(NEXP + 1, DC, 128, HID))
    wpj = np.concatenate([np.asarray(i["L1_ffn_Wproj_s"], dtype=np.float32)[None],
                          np.asarray(i["L1_ffn_Wproj"], dtype=np.float32)], axis=0)
    d["p3_wpj"] = np.ascontiguousarray(wpj.reshape(NEXP + 1, HC, 128, D))
    bfc = np.concatenate([np.asarray(i["L1_ffn_bfc_s"], dtype=np.float32)[None],
                          np.asarray(i["L1_ffn_bfc"], dtype=np.float32)], axis=0)
    d["p3_bfc"] = np.ascontiguousarray(
        bfc.reshape(NEXP + 1, HC, 128).transpose(0, 2, 1))
    bpj = np.concatenate([np.asarray(i["L1_ffn_bproj_s"], dtype=np.float32)[None],
                          np.asarray(i["L1_ffn_bproj"], dtype=np.float32)], axis=0)
    d["p3_bpj"] = np.ascontiguousarray(
        bpj.reshape(NEXP + 1, DC, 128).transpose(0, 2, 1))
    return d


def shard_x(x):
    x = np.asarray(x, dtype=np.float32)
    out = []
    for c in range(NCORES):
        xc = x[NSEQ * c:NSEQ * (c + 1)]
        out.append(np.ascontiguousarray(
            xc.reshape(NSEQ, S, DC, 128).transpose(0, 2, 3, 1)))
    return out


def unshard_out(shards):
    out = np.empty((NCORES * NSEQ, S, D), dtype=np.float32)
    for c, o in enumerate(shards):
        out[NSEQ * c:NSEQ * (c + 1)] = np.asarray(o).transpose(0, 3, 1, 2).reshape(NSEQ, S, D)
    return out


def tile_bt(i):
    return i // NT_SEQ, (i % NT_SEQ) * T



ENGINES = ("sync", "gpsimd", "vector", "scalar", "tensor")
QUEUES = ("sync", "gpsimd")


class _Op:
    __slots__ = ("eng", "fn", "inc_val", "deps", "needs_inc", "inc_target",
                 "batch_end", "self_wait", "dma")

    def __init__(self, eng, fn, inc_val, dma):
        self.eng = eng
        self.fn = fn
        self.inc_val = inc_val
        self.dma = dma
        self.deps = []
        self.needs_inc = False
        self.inc_target = None
        self.batch_end = None
        self.self_wait = None


class FW:
    def __init__(self, nc, es):
        self.nc = nc
        self.sems = {e: es.enter_context(nc.semaphore(f"S_{e}")) for e in ENGINES}
        self.bsems = {q: es.enter_context(nc.semaphore(f"B_{q}")) for q in QUEUES}
        self.csems = {q: es.enter_context(nc.semaphore(f"C_{q}")) for q in QUEUES}
        self.icount = {e: 0 for e in ENGINES}
        self.bcount = {q: 0 for q in QUEUES}
        self.ccount = {q: 0 for q in QUEUES}
        self.prev_totals = None
        self.ops = []
        self.last_w = {}
        self.readers = {}

    def op(self, eng, fn, reads=(), writes=(), inc_val=1, dma=False):
        o = _Op(eng, fn, inc_val, dma)
        deps = {}
        for t in reads:
            w = self.last_w.get(t)
            if w is not None:
                deps[id(w)] = w
        for t in writes:
            w = self.last_w.get(t)
            if w is not None:
                deps[id(w)] = w
            for r in self.readers.get(t, ()):
                deps[id(r)] = r
        for t in reads:
            self.readers.setdefault(t, []).append(o)
        for t in writes:
            self.last_w[t] = o
            self.readers[t] = []
        for d in deps.values():
            if d is o:
                continue
            if d.eng == "tensor" and eng == "tensor":
                continue
            o.deps.append(d)
            d.needs_inc = True
        self.ops.append(o)
        return o

    def dma(self, queue, fn, reads=(), writes=()):
        return self.op(queue, fn, reads, writes, inc_val=16, dma=True)

    def flush(self, block):
        for o in self.ops:
            if o.eng in QUEUES and o.dma:
                self.icount[o.eng] += o.inc_val
                o.inc_target = self.icount[o.eng]
            elif o.eng in QUEUES and o.needs_inc:
                self.ccount[o.eng] += o.inc_val
                o.inc_target = self.ccount[o.eng]
            elif o.needs_inc:
                self.icount[o.eng] += o.inc_val
                o.inc_target = self.icount[o.eng]
        open_batch = {q: [] for q in QUEUES}
        for o in self.ops:
            for d in o.deps:
                if d.eng in QUEUES and d.dma and d.batch_end is None:
                    ob = open_batch[d.eng]
                    if ob:
                        last = ob[-1]
                        val = last.inc_target
                        for m in ob:
                            m.batch_end = val
                        last.self_wait = val
                        open_batch[d.eng] = []
            if o.eng in QUEUES and o.dma:
                open_batch[o.eng].append(o)
        for q in QUEUES:
            for m in open_batch[q]:
                m.batch_end = self.icount[q]

        queue_totals = {e: self.icount[e] for e in ENGINES}
        for e in ENGINES:
            if e not in QUEUES:
                self.icount[e] += 1
        for q in QUEUES:
            self.bcount[q] += 1
        new_totals = {e: (self.bcount[e] if e in QUEUES else self.icount[e])
                      for e in ENGINES}
        prev = self.prev_totals
        ops = self.ops
        sems = self.sems
        bsems = self.bsems
        csems = self.csems

        for ename in ENGINES:
            eng_ops = [o for o in ops if o.eng == ename]
            sec = getattr(block, ename)

            def _body(eng, ename=ename, eng_ops=eng_ops, prev=prev,
                      queue_totals=queue_totals):
                hwm = {}
                if prev is not None:
                    for oe in ENGINES:
                        if oe != ename:
                            eng.wait_ge(bsems[oe] if oe in QUEUES else sems[oe], prev[oe])
                            if oe not in QUEUES:
                                hwm[oe] = prev[oe]
                for o in eng_ops:
                    for d in o.deps:
                        if d.eng in QUEUES and d.dma:
                            key, sem, v = d.eng, sems[d.eng], d.batch_end
                        elif d.eng in QUEUES:
                            key, sem, v = "c" + d.eng, csems[d.eng], d.inc_target
                        else:
                            key, sem, v = d.eng, sems[d.eng], d.inc_target
                        if hwm.get(key, -1) < v:
                            eng.wait_ge(sem, v)
                            hwm[key] = v
                    inst = o.fn(eng)
                    if o.inc_target is not None:
                        if o.eng in QUEUES and not o.dma:
                            inst.then_inc(csems[o.eng], o.inc_val)
                        else:
                            inst.then_inc(sems[o.eng], o.inc_val)
                    if o.self_wait is not None:
                        eng.wait_ge(sems[ename], o.self_wait)
                        hwm[ename] = o.self_wait
                if ename in QUEUES:
                    eng.drain()
                    eng.wait_ge(sems[ename], queue_totals[ename])
                    eng.sem_inc(bsems[ename], 1)
                else:
                    eng.drain()
                    eng.sem_inc(sems[ename], 1)

            sec(_body)

        self.ops = []
        self.last_w = {}
        self.readers = {}
        self.prev_totals = new_totals




def emit_p0(fw, block, DR, es):
    nc = fw.nc
    sb = lambda n, s, dt=F32: es.enter_context(nc.sbuf_tensor("a0" + n, s, dt))
    psm = lambda n, s: es.enter_context(nc.psum_tensor("a0" + n, s, F32))
    v, sc, te, sy, gp = "vector", "scalar", "tensor", "sync", "gpsimd"

    wd = [sb(f"wd{k}", [128, D], BF16) for k in range(DC)]
    wb = [sb(f"wb{k}", [128, 16], BF16) for k in range(DC)]
    wo = sb("wo", [16, D], BF16)
    bd2 = sb("bd2", [128, DC])
    bB = sb("bB", [16, 1])
    alog = sb("alog", [16, 1])
    acol = sb("acol", [16, 1])
    bo = sb("bo", [128, DC])
    ecol = sb("ecol", [128, DC])
    ident = sb("ident", [128, 128])
    ones_bf = sb("onesbf", [128, 1], BF16)
    ones16 = sb("ones16", [1, 16])
    c16 = sb("c16", [1, 16])
    ones128r = sb("o128r", [1, 128])
    xf = [[sb(f"xf{u}_{k}", [128, T]) for k in range(DC)] for u in range(2)]
    xbt = [[sb(f"xb{u}_{k}", [128, T], BF16) for k in range(DC)] for u in range(2)]
    tnh = [sb(f"tnh{k}", [128, T], BF16) for k in range(DC)]
    sqb = [sb(f"sqb{k}", [128, T], BF16) for k in range(2)]
    bm = sb("bm", [16, T])
    abar = sb("abar", [16, T])
    bu = sb("bu", [16, T])
    states = sb("states", [16, T])
    st_bf = sb("stbf", [16, T], BF16)
    carry = sb("carry", [16, 1])
    delta = sb("delta", [1, T])
    grow = sb("grow", [1, T])
    y = [sb(f"y{k}", [128, T]) for k in range(DC)]
    ybf = [sb(f"ybf{k}", [128, T], BF16) for k in range(DC)]
    rows = [sb("mrow", [1, T]), sb("vrow", [1, T]), sb("srow", [1, T]), sb("rstd", [1, T])]
    mx = sb("mx", [128, DC])
    mxt = sb("mxt", [128, 1])
    lrow = sb("lrow", [1, 128])
    lmax = sb("lmax", [1, 1])

    pU = [psm("pU0", [128, T]), psm("pU1", [128, T])]
    pO = [psm("pO0", [128, T]), psm("pO1", [128, T])]
    pMB = psm("pMB", [128, T])
    pRB = psm("pRB", [128, T])
    bankA = psm("bankA", [128, T])
    bankB = psm("bankB", [128, T])

    fw.dma(sy, lambda e: e.dma_start(out=bd2[:], in_=DR["p0_bd"]), writes=("bd2",))
    fw.dma(sy, lambda e: e.dma_start(out=bo[:], in_=DR["p0_bo"]), writes=("bo",))
    fw.dma(sy, lambda e: e.dma_start(out=ecol[:], in_=DR["p0_dp"]), writes=("ecol",))
    fw.dma(sy, lambda e: e.dma_start(out=bB[:], in_=DR["p0_bB"]), writes=("bB",))
    fw.dma(sy, lambda e: e.dma_start(out=alog[:], in_=DR["p0_alog"]), writes=("alog",))
    fw.dma(sy, lambda e: e.dma_start(out=ident[:], in_=DR["ident"]), writes=("ident",))
    for k in range(DC):
        fw.dma(gp, lambda e, k=k: e.dma_start(out=wd[k][:], in_=DR["p0_wd"][k]), writes=(f"wd{k}",))
        fw.dma(gp, lambda e, k=k: e.dma_start(out=wb[k][:], in_=DR["p0_wb"][k]), writes=(f"wb{k}",))
    fw.dma(gp, lambda e: e.dma_start(out=wo[:], in_=DR["p0_wo"]), writes=("wo",))
    fw.op(v, lambda e: e.memset(ones_bf[:], 1.0), writes=("ones",))
    fw.op(v, lambda e: e.memset(ones16[:], 1.0), writes=("ones",))
    fw.op(v, lambda e: e.memset(c16[:], 1.0 / D), writes=("ones",))
    fw.op(v, lambda e: e.memset(ones128r[:], 1.0), writes=("ones",))
    fw.op(v, lambda e: e.memset(mx[:], 0.0), writes=("mx",))
    fw.op(v, lambda e: e.tensor_scalar(out=bd2[:], in0=bd2[:], scalar1=0.5, scalar2=None,
                                       op0=ALU.mult), reads=("bd2",), writes=("bd2",))
    fw.op(v, lambda e: e.tensor_scalar(out=ecol[:], in0=ecol[:], scalar1=1.0, scalar2=None,
                                       op0=ALU.add), reads=("ecol",), writes=("ecol",))
    fw.op(sc, lambda e: e.activation(out=acol[:], in_=alog[:], func=AF.Exp),
          reads=("alog",), writes=("acol",))
    fw.op(v, lambda e: e.tensor_scalar(out=acol[:], in0=acol[:], scalar1=-1.0, scalar2=None,
                                       op0=ALU.mult), reads=("acol",), writes=("acol",))

    ytags = [f"y{c}" for c in range(DC)]
    ybftags = [f"ybf{c}" for c in range(DC)]
    sqbtags = ["sqb0", "sqb1"]
    rowtags = ["mrow", "vrow", "srow", "rstd"]

    for i in range(NTILES):
        b, t0 = tile_bt(i)
        u = i % 2
        for c in range(DC):
            fw.dma(sy, lambda e, b=b, c=c, t0=t0, u=u:
                   e.dma_start(out=xf[u][c][:], in_=DR["x_in"][b, c, :, t0:t0 + T]),
                   writes=(f"xf{u}_{c}",))
            fw.dma(gp, lambda e, b=b, c=c, t0=t0, u=u:
                   e.dma_start(out=xbt[u][c][:], in_=DR["x_in"][b, c, :, t0:t0 + T]),
                   writes=(f"xb{u}_{c}",))
        if t0 == 0:
            fw.op(v, lambda e: e.memset(carry[:], 0.0), writes=("carry",))
        for m in range(DC):
            for k in range(DC):
                fw.op(te, lambda e, m=m, k=k, u=u:
                      e.matmul(out=pU[m % 2][:], lhsT=wd[k][:, m * 128:(m + 1) * 128],
                               rhs=xbt[u][k][:], start=(k == 0), stop=(k == DC - 1)),
                      reads=(f"wd{k}", f"xb{u}_{k}"), writes=(f"pU{m % 2}",))
            fw.op(sc, lambda e, m=m: e.activation(out=tnh[m][:], in_=pU[m % 2][:], func=AF.Tanh,
                                                  scale=0.5, bias=bd2[:, m:m + 1]),
                  reads=(f"pU{m % 2}", "bd2"), writes=(f"tnh{m}",))
        for k in range(DC):
            fw.op(te, lambda e, k=k, u=u:
                  e.matmul(out=bankB[32:48, :], lhsT=wb[k][:], rhs=xbt[u][k][:],
                           start=(k == 0), stop=(k == DC - 1)),
                  reads=(f"wb{k}", f"xb{u}_{k}"), writes=("bankB",))
        fw.op(sc, lambda e: e.activation(out=bm[:], in_=bankB[32:48, :], func=AF.Identity,
                                         bias=bB[:, 0:1]),
              reads=("bankB", "bB"), writes=("bm",))
        for k in range(DC):
            fw.op(te, lambda e, k=k, u=u:
                  e.matmul(out=bankA[32:33, :], lhsT=ones_bf[:], rhs=xbt[u][k][:],
                           start=(k == 0), stop=(k == DC - 1)),
                  reads=(f"xb{u}_{k}", "ones"), writes=("bankA",))
        for k in range(DC):
            fw.op(te, lambda e, k=k:
                  e.matmul(out=bankA[0:1, :], lhsT=ones_bf[:], rhs=tnh[k][:],
                           start=(k == 0), stop=(k == DC - 1)),
                  reads=(f"tnh{k}", "ones"), writes=("bankA",))
        fw.op(v, lambda e: e.tensor_scalar(out=delta[:], in0=bankA[0:1, :],
                                           scalar1=0.0495 / D, scalar2=0.0505,
                                           op0=ALU.mult, op1=ALU.add),
              reads=("bankA",), writes=("delta",))
        fw.op(v, lambda e: e.tensor_tensor(out=grow[:], in0=delta[:], in1=bankA[32:33, :],
                                           op=ALU.mult),
              reads=("delta", "bankA"), writes=("grow",))
        fw.op(te, lambda e: e.matmul(out=bankB[64:80, :], lhsT=ones16[:], rhs=delta[:],
                                     start=True, stop=True),
              reads=("delta", "ones"), writes=("bankB",))
        fw.op(sc, lambda e: e.activation(out=abar[:], in_=bankB[64:80, :], func=AF.Exp,
                                         scale=acol[:, 0:1]),
              reads=("bankB", "acol"), writes=("abar",))
        fw.op(te, lambda e: e.matmul(out=bankB[64:80, :], lhsT=c16[:], rhs=grow[:],
                                     start=True, stop=True),
              reads=("grow", "ones"), writes=("bankB",))
        fw.op(v, lambda e: e.tensor_tensor(out=bu[:], in0=bm[:], in1=bankB[64:80, :], op=ALU.mult),
              reads=("bm", "bankB"), writes=("bu",))
        fw.op(v, lambda e: e.tensor_tensor_scan(out=states[:], data0=abar[:], data1=bu[:],
                                                initial=carry[:, 0:1], op0=ALU.mult, op1=ALU.add),
              reads=("abar", "bu", "carry"), writes=("states",))
        fw.op(v, lambda e: e.tensor_copy(out=carry[:], in_=states[:, T - 1:T]),
              reads=("states",), writes=("carry",))
        fw.op(v, lambda e: e.tensor_copy(out=st_bf[:], in_=states[:]),
              reads=("states",), writes=("stbf",))
        for c in range(DC):
            fw.op(te, lambda e, c=c: e.matmul(out=pO[c % 2][:], lhsT=wo[:, c * 128:(c + 1) * 128],
                                              rhs=st_bf[:], start=True, stop=True),
                  reads=("wo", "stbf"), writes=(f"pO{c % 2}",))
            fw.op(sc, lambda e, c=c: e.activation(out=y[c][:], in_=pO[c % 2][:], func=AF.Identity,
                                                  bias=bo[:, c:c + 1]),
                  reads=(f"pO{c % 2}", "bo"), writes=(f"y{c}",))
            fw.op(v, lambda e, c=c, u=u: e.scalar_tensor_tensor(
                out=y[c][:], in0=xf[u][c][:], scalar=ecol[:, c:c + 1], in1=y[c][:],
                op0=ALU.mult, op1=ALU.add),
                reads=(f"xf{u}_{c}", "ecol", f"y{c}"), writes=(f"y{c}",))
        emit_postln(fw, y, ytags, ybf, ybftags, sqb, sqbtags, rows, rowtags,
                    bankA, "bankA", pMB, pRB, ones_bf, ones128r)
        for c in range(DC):
            fw.dma(sy, lambda e, b=b, c=c, t0=t0: e.dma_start(
                out=DR["x1_s"][b, c, :, t0:t0 + T], in_=y[c][:]),
                reads=(f"y{c}",))
            fw.op(v, lambda e, c=c: e.tensor_reduce(out=mxt[:], in_=y[c][:], axis=AX.X,
                                                    op=ALU.max, apply_absolute_value=True),
                  reads=(f"y{c}",), writes=("mxt",))
            fw.op(v, lambda e, c=c: e.tensor_tensor(out=mx[:, c:c + 1], in0=mx[:, c:c + 1],
                                                    in1=mxt[:], op=ALU.max),
                  reads=("mxt", "mx"), writes=("mx",))
    fw.op(v, lambda e: e.tensor_reduce(out=mxt[:], in_=mx[:], axis=AX.X, op=ALU.max),
          reads=("mx",), writes=("mxt",))
    fw.op(te, lambda e: e.transpose(out=bankA[0:1, 0:128], in_=mxt[:, 0:1], identity=ident[:]),
          reads=("mxt", "ident"), writes=("bankA",))
    fw.op(v, lambda e: e.tensor_copy(out=lrow[:], in_=bankA[0:1, 0:128]),
          reads=("bankA",), writes=("lrow",))
    fw.op(v, lambda e: e.tensor_reduce(out=lmax[:], in_=lrow[:], axis=AX.X, op=ALU.max),
          reads=("lrow",), writes=("lmax",))
    fw.dma(sy, lambda e: e.dma_start(out=DR["lmax_out"], in_=lmax[:]), reads=("lmax",))
    fw.flush(block)


def emit_cc(fw, block, DR):
    fw.op("gpsimd", lambda e: e.collective_compute(
        "AllReduce", ALU.max, replica_groups=[list(range(NCORES))],
        ins=[DR["lmax_out"]], outs=[DR["maxes"]]), inc_val=1, dma=True)
    fw.flush(block)



def emit_p1(fw, block, DR, es, n_max):
    nc = fw.nc
    sb = lambda n, s, dt=F32: es.enter_context(nc.sbuf_tensor("a1" + n, s, dt))
    psm = lambda n, s: es.enter_context(nc.psum_tensor("a1" + n, s, F32))
    v, sc, te, sy, gp = "vector", "scalar", "tensor", "sync", "gpsimd"

    w1f = [sb(f"w1f{k}", [128, D]) for k in range(DC)]
    qw = [sb(f"qw{k}", [128, D], BF16) for k in range(DC)]
    absw = sb("absw", [128, D])
    sgn = sb("sgn", [128, D])
    b1 = sb("b1", [128, DC])
    ident = sb("ident", [128, 128])
    ones_bf = sb("onesbf", [128, 1], BF16)
    ones128r = sb("o128r", [1, 128])
    mxs = sb("mxs", [1, n_max])
    gmax = sb("gmax", [1, 1])
    isc = sb("isc", [1, 1])
    wsis = sb("wsis", [1, 1])
    hws = sb("hws", [1, 1])
    wcol = sb("wcol", [128, DC])
    wcol1 = sb("wcol1", [128, 1])
    wrow = sb("wrow", [1, 128])
    ws = sb("ws", [1, 1])
    inv_col = sb("invcol", [128, 1])
    wsis_col = sb("wsiscol", [128, 1])
    hws_col = sb("hwscol", [128, 1])
    xf = [[sb(f"xf{u}_{k}", [128, T]) for k in range(DC)] for u in range(2)]
    qx = [sb(f"qx{k}", [128, T], BF16) for k in range(DC)]
    qt = [sb("qt0", [128, T]), sb("qt1", [128, T])]
    magic_col = sb("magic", [128, 1])
    y = [sb(f"y{k}", [128, T]) for k in range(DC)]
    ybf = [sb(f"ybf{k}", [128, T], BF16) for k in range(DC)]
    sqb = [sb(f"sqb{k}", [128, T], BF16) for k in range(2)]
    rows = [sb("mrow", [1, T]), sb("vrow", [1, T]), sb("srow", [1, T]), sb("rstd", [1, T])]

    pQ = [psm("pQ0", [128, T]), psm("pQ1", [128, T])]
    pMB = psm("pMB", [128, T])
    pRB = psm("pRB", [128, T])
    bankA = psm("bankA", [128, T])
    pCol = psm("pCol", [128, 512])

    for k in range(DC):
        fw.dma(sy, lambda e, k=k: e.dma_start(out=w1f[k][:], in_=DR["p1_w"][k]), writes=(f"w1f{k}",))
    fw.dma(sy, lambda e: e.dma_start(out=b1[:], in_=DR["p1_b"]), writes=("b1",))
    fw.dma(sy, lambda e: e.dma_start(out=ident[:], in_=DR["ident"]), writes=("ident",))
    fw.dma(sy, lambda e: e.dma_start(out=mxs[:], in_=DR["maxes"]), writes=("mxs",))
    fw.op(v, lambda e: e.memset(ones_bf[:], 1.0), writes=("ones",))
    fw.op(v, lambda e: e.memset(ones128r[:], 1.0), writes=("ones",))
    fw.op(v, lambda e: e.memset(magic_col[:], MAGIC), writes=("ones",))

    for k in range(DC):
        fw.op(v, lambda e, k=k: e.tensor_reduce(out=wcol[:, k:k + 1], in_=w1f[k][:], axis=AX.X,
                                                op=ALU.add, apply_absolute_value=True),
              reads=(f"w1f{k}",), writes=("wcol",))
    fw.op(v, lambda e: e.tensor_reduce(out=wcol1[:], in_=wcol[:], axis=AX.X, op=ALU.add),
          reads=("wcol",), writes=("wcol1",))
    fw.op(te, lambda e: e.transpose(out=bankA[0:1, 0:128], in_=wcol1[:, 0:1], identity=ident[:]),
          reads=("wcol1", "ident"), writes=("bankA",))
    fw.op(v, lambda e: e.tensor_copy(out=wrow[:], in_=bankA[0:1, 0:128]),
          reads=("bankA",), writes=("wrow",))
    fw.op(v, lambda e: e.tensor_reduce(out=ws[:], in_=wrow[:], axis=AX.X, op=ALU.add),
          reads=("wrow",), writes=("ws",))
    fw.op(v, lambda e: e.tensor_scalar(out=ws[:], in0=ws[:], scalar1=1.0 / (D * D), scalar2=None,
                                       op0=ALU.mult), reads=("ws",), writes=("ws",))
    fw.op(v, lambda e: e.tensor_scalar(out=hws[:], in0=ws[:], scalar1=0.5, scalar2=None,
                                       op0=ALU.mult), reads=("ws",), writes=("hws",))
    fw.op(v, lambda e: e.tensor_reduce(out=gmax[:], in_=mxs[:], axis=AX.X, op=ALU.max),
          reads=("mxs",), writes=("gmax",))
    fw.op(v, lambda e: e.tensor_scalar(out=isc[:], in0=gmax[:], scalar1=1.0 / 127, scalar2=None,
                                       op0=ALU.mult), reads=("gmax",), writes=("isc",))
    fw.op(v, lambda e: e.tensor_tensor(out=wsis[:], in0=ws[:], in1=isc[:], op=ALU.mult),
          reads=("ws", "isc"), writes=("wsis",))
    fw.op(v, lambda e: e.reciprocal(out=gmax[:], in_=isc[:]), reads=("isc",), writes=("ginv", "gmax"))
    fw.op(te, lambda e: e.matmul(out=pCol[:, 0:1], lhsT=ones128r[:], rhs=gmax[:],
                                 start=True, stop=True), reads=("ginv", "ones"), writes=("pCol",))
    fw.op(v, lambda e: e.tensor_copy(out=inv_col[:], in_=pCol[:, 0:1]),
          reads=("pCol",), writes=("invcol",))
    fw.op(te, lambda e: e.matmul(out=pCol[:, 1:2], lhsT=ones128r[:], rhs=wsis[:],
                                 start=True, stop=True), reads=("wsis", "ones"), writes=("pCol",))
    fw.op(v, lambda e: e.tensor_copy(out=wsis_col[:], in_=pCol[:, 1:2]),
          reads=("pCol",), writes=("wsiscol",))
    fw.op(te, lambda e: e.matmul(out=pCol[:, 2:3], lhsT=ones128r[:], rhs=hws[:],
                                 start=True, stop=True), reads=("hws", "ones"), writes=("pCol",))
    fw.op(v, lambda e: e.tensor_copy(out=hws_col[:], in_=pCol[:, 2:3]),
          reads=("pCol",), writes=("hwscol",))
    for k in range(DC):
        fw.op(sc, lambda e, k=k: e.activation(out=absw[:], in_=w1f[k][:], func=AF.Abs),
              reads=(f"w1f{k}",), writes=("absw",))
        fw.op(sc, lambda e, k=k: e.activation(out=sgn[:], in_=w1f[k][:], func=AF.Sign),
              reads=(f"w1f{k}",), writes=("sgn",))
        fw.op(v, lambda e: e.tensor_scalar(out=absw[:], in0=absw[:], scalar1=hws_col[:, 0:1],
                                           scalar2=None, op0=ALU.is_gt),
              reads=("absw", "hwscol"), writes=("absw",))
        fw.op(v, lambda e, k=k: e.tensor_tensor(out=qw[k][:], in0=sgn[:], in1=absw[:], op=ALU.mult),
              reads=("sgn", "absw"), writes=(f"qw{k}",))

    ytags = [f"y{c}" for c in range(DC)]
    ybftags = [f"ybf{c}" for c in range(DC)]
    sqbtags = ["sqb0", "sqb1"]
    rowtags = ["mrow", "vrow", "srow", "rstd"]

    for i in range(NTILES):
        b, t0 = tile_bt(i)
        u = i % 2
        lq = sy if i % 2 == 0 else gp
        for c in range(DC):
            fw.dma(lq, lambda e, b=b, c=c, t0=t0, u=u:
                   e.dma_start(out=xf[u][c][:], in_=DR["x1_s"][b, c, :, t0:t0 + T]),
                   writes=(f"xf{u}_{c}",))
        for c in range(DC):
            qb = c % 2
            fw.op(sc, lambda e, c=c, u=u, qb=qb: e.activation(
                out=qt[qb][:], in_=xf[u][c][:], func=AF.Identity,
                scale=inv_col[:, 0:1], bias=magic_col[:, 0:1]),
                reads=(f"xf{u}_{c}", "invcol", "ones"), writes=(f"qt{qb}",))
            fw.op(v, lambda e, qb=qb: e.tensor_scalar(
                out=qt[qb][:], in0=qt[qb][:], scalar1=MAGIC, scalar2=127.0,
                op0=ALU.subtract, op1=ALU.min),
                reads=(f"qt{qb}",), writes=(f"qt{qb}",))
            fw.op(v, lambda e, c=c, qb=qb: e.tensor_scalar(
                out=qx[c][:], in0=qt[qb][:], scalar1=-128.0, scalar2=None, op0=ALU.max),
                reads=(f"qt{qb}",), writes=(f"qx{c}",))
        for m in range(DC):
            for k in range(DC):
                fw.op(te, lambda e, m=m, k=k: e.matmul(
                    out=pQ[m % 2][:], lhsT=qw[k][:, m * 128:(m + 1) * 128], rhs=qx[k][:],
                    start=(k == 0), stop=(k == DC - 1)),
                    reads=(f"qw{k}", f"qx{k}"), writes=(f"pQ{m % 2}",))
            fw.op(sc, lambda e, m=m: e.activation(out=y[m][:], in_=pQ[m % 2][:], func=AF.Identity,
                                                  scale=wsis_col[:, 0:1], bias=b1[:, m:m + 1]),
                  reads=(f"pQ{m % 2}", "wsiscol", "b1"), writes=(f"y{m}",))
            fw.op(gp, lambda e, m=m, u=u: e.tensor_tensor(out=y[m][:], in0=y[m][:],
                                                          in1=xf[u][m][:], op=ALU.add),
                  reads=(f"y{m}", f"xf{u}_{m}"), writes=(f"y{m}",))
        emit_postln(fw, y, ytags, ybf, ybftags, sqb, sqbtags, rows, rowtags,
                    bankA, "bankA", pMB, pRB, ones_bf, ones128r)
        for c in range(DC):
            fw.dma(sy, lambda e, b=b, c=c, t0=t0: e.dma_start(
                out=DR["x2_s"][b, c, :, t0:t0 + T], in_=y[c][:]),
                reads=(f"y{c}",))
    fw.flush(block)



def emit_p2a(fw, block, DR, es, flush=True):
    nc = fw.nc
    sb = lambda n, s, dt=F32: es.enter_context(nc.sbuf_tensor("a2" + n, s, dt))
    psm = lambda n, s: es.enter_context(nc.psum_tensor("a2" + n, s, F32))
    v, sc, te, sy, gp = "vector", "scalar", "tensor", "sync", "gpsimd"

    w1 = [sb(f"w1{k}", [128, D], BF16) for k in range(DC)]
    b1 = sb("b1", [128, DC])
    xb2 = [[sb(f"xb{u}_{k}", [128, T], BF16) for k in range(DC)] for u in range(2)]
    pb = [sb(f"pb{k}", [128, T], BF16) for k in range(DC)]
    tcol = sb("tcol", [128, 1])
    tot = sb("tot", [128, 2 * DC])
    pP = [psm("pP0", [128, T]), psm("pP1", [128, T])]

    for k in range(DC):
        fw.dma(gp, lambda e, k=k: e.dma_start(out=w1[k][:], in_=DR["p2_w1"][k]), writes=(f"w1{k}",))
    fw.dma(sy, lambda e: e.dma_start(out=b1[:], in_=DR["p2_b1"]), writes=("b1",))
    fw.op(v, lambda e: e.memset(tot[:], 0.0), writes=("tot",))

    for i in range(NTILES):
        b, t0 = tile_bt(i)
        u = i % 2
        for c in range(DC):
            fw.dma(gp, lambda e, b=b, c=c, t0=t0, u=u:
                   e.dma_start(out=xb2[u][c][:], in_=DR["x2_s"][b, c, :, t0:t0 + T]),
                   writes=(f"xb{u}_{c}",))
        for m in range(DC):
            for k in range(DC):
                fw.op(te, lambda e, m=m, k=k, u=u: e.matmul(
                    out=pP[m % 2][:], lhsT=w1[k][:, m * 128:(m + 1) * 128], rhs=xb2[u][k][:],
                    start=(k == 0), stop=(k == DC - 1)),
                    reads=(f"w1{k}", f"xb{u}_{k}"), writes=(f"pP{m % 2}",))
            fw.op(sc, lambda e, m=m: e.activation(out=pb[m][:], in_=pP[m % 2][:], func=AF.Identity,
                                                  bias=b1[:, m:m + 1], accum_out=tcol[:]),
                  reads=(f"pP{m % 2}", "b1"), writes=(f"pb{m}", "tcol"))
            j = b * DC + m
            fw.op(v, lambda e, j=j: e.tensor_tensor(out=tot[:, j:j + 1], in0=tot[:, j:j + 1],
                                                    in1=tcol[:], op=ALU.add),
                  reads=("tcol", "tot"), writes=("tot",))
            fw.dma(sy, lambda e, b=b, m=m, t0=t0: e.dma_start(
                out=DR["p_s"][b, m, :, t0:t0 + T], in_=pb[m][:]),
                reads=(f"pb{m}",), writes=(f"psd{b}_{m}",))
    fw.dma(sy, lambda e: e.dma_start(out=DR["tot_s"], in_=tot[:]), reads=("tot",),
           writes=("totd",))
    if flush:
        fw.flush(block)



def emit_p2b(fw, block, DR, es):
    nc = fw.nc
    sb = lambda n, s, dt=F32: es.enter_context(nc.sbuf_tensor("a3" + n, s, dt))
    psm = lambda n, s: es.enter_context(nc.psum_tensor("a3" + n, s, F32))
    v, sc, te, sy, gp = "vector", "scalar", "tensor", "sync", "gpsimd"

    w2 = [sb(f"w2{k}", [128, D], BF16) for k in range(DC)]
    b2 = sb("b2", [128, DC])
    mw = [sb(f"mw{k}", [128, MEM + 1]) for k in range(DC)]
    taps = [sb(f"tp{k}", [128, MEM]) for k in range(DC)]
    tot = sb("tot", [128, 2 * DC])
    ct = sb("ct", [128, 2 * DC])
    ident = sb("ident", [128, 128])
    diag = sb("diag", [128, 128 * MEM], BF16)
    pfull = [sb(f"pfl{u}", [128, MEM - 1 + S], BF16) for u in range(2)]
    bstage = [sb(f"bst{u}", [128, T], BF16) for u in range(2)]
    xf = [[sb(f"xf{u}_{k}", [128, T]) for k in range(DC)] for u in range(2)]
    bnd = [[sb(f"bnd{u}_{k}", [128, T], BF16) for k in range(DC)] for u in range(2)]
    y = [sb(f"y{k}", [128, T]) for k in range(DC)]
    ybf = [sb(f"ybf{k}", [128, T], BF16) for k in range(DC)]
    sqb = [sb(f"sqb{k}", [128, T], BF16) for k in range(2)]
    ones_bf = sb("onesbf", [128, 1], BF16)
    ones128r = sb("o128r", [1, 128])
    rows = [sb("mrow", [1, T]), sb("vrow", [1, T]), sb("srow", [1, T]), sb("rstd", [1, T])]
    pB = [psm("pB0", [128, T]), psm("pB1", [128, T])]
    pW = [psm("pW0", [128, T]), psm("pW1", [128, T])]
    pMB = psm("pMB", [128, T])
    pRB = pMB
    bankA = psm("bankA", [128, T])

    for k in range(DC):
        fw.dma(gp, lambda e, k=k: e.dma_start(out=w2[k][:], in_=DR["p2_w2"][k]), writes=(f"w2{k}",))
        fw.dma(sy, lambda e, k=k: e.dma_start(out=mw[k][:], in_=DR["p2_mw"][k]), writes=(f"mw{k}",))
    fw.dma(sy, lambda e: e.dma_start(out=b2[:], in_=DR["p2_b2"]), writes=("b2",))
    fw.dma(sy, lambda e: e.dma_start(out=tot[:], in_=DR["tot_s"]), reads=("totd",), writes=("tot",))
    fw.dma(sy, lambda e: e.dma_start(out=ident[:], in_=DR["ident"]), writes=("ident",))
    fw.op(v, lambda e: e.memset(ones_bf[:], 1.0), writes=("ones",))
    fw.op(v, lambda e: e.memset(ones128r[:], 1.0), writes=("ones",))
    for k in range(DC):
        fw.op(v, lambda e, k=k: e.tensor_scalar(out=taps[k][:], in0=mw[k][:, 1:MEM + 1],
                                                scalar1=mw[k][:, 0:1], scalar2=None,
                                                op0=ALU.subtract),
              reads=(f"mw{k}",), writes=(f"tp{k}",))
    for b in range(NSEQ):
        for k in range(DC):
            j = b * DC + k
            fw.op(v, lambda e, j=j, k=k: e.tensor_tensor(
                out=ct[:, j:j + 1], in0=mw[k][:, 0:1], in1=tot[:, j:j + 1], op=ALU.mult),
                reads=(f"mw{k}", "tot"), writes=("ct",))

    for u in range(2):
        fw.op(v, lambda e, u=u: e.memset(pfull[u][:, 0:MEM - 1], 0.0), writes=(f"pfl{u}",))
    for b in range(NSEQ):
        for c in range(DC):
            for j in range(MEM):
                fw.op(v, lambda e, c=c, j=j: e.tensor_scalar(
                    out=diag[:, j * 128:(j + 1) * 128], in0=ident[:],
                    scalar1=taps[c][:, j:j + 1], scalar2=None, op0=ALU.mult),
                    reads=("ident", f"tp{c}"), writes=("diag",))
            u = (b * DC + c) % 2
            fw.dma(sy, lambda e, b=b, c=c, u=u: e.dma_start(
                out=pfull[u][:, MEM - 1:], in_=DR["p_s"][b, c, :, :]),
                reads=(f"psd{b}_{c}",), writes=(f"pfl{u}",))
            j0 = b * DC + c
            for tp in range(NT_SEQ // 2):
                for j in range(MEM):
                    for ti in range(2):
                        t8 = tp * 2 + ti
                        fw.op(te, lambda e, u=u, t8=t8, j=j, ti=ti: e.matmul(
                            out=pB[ti][:], lhsT=diag[:, j * 128:(j + 1) * 128],
                            rhs=pfull[u][:, MEM - 1 + t8 * T - j:MEM - 1 + t8 * T - j + T],
                            start=(j == 0), stop=(j == MEM - 1)),
                            reads=("diag", f"pfl{u}"), writes=(f"pB{ti}",))
                for ti in range(2):
                    t8 = tp * 2 + ti
                    fw.op(v, lambda e, ti=ti, j0=j0: e.tensor_scalar(
                        out=bstage[ti][:], in0=pB[ti][:],
                        scalar1=ct[:, j0:j0 + 1], scalar2=None, op0=ALU.add),
                        reads=(f"pB{ti}", "ct"), writes=(f"bst{ti}",))
                    fw.dma(sy, lambda e, b=b, c=c, t8=t8, ti=ti: e.dma_start(
                        out=DR["band_s"][b, c, :, t8 * T:(t8 + 1) * T], in_=bstage[ti][:]),
                        reads=(f"bst{ti}",), writes=(f"bdd{b}_{c}",))

    ytags = [f"y{c}" for c in range(DC)]
    ybftags = [f"ybf{c}" for c in range(DC)]
    sqbtags = ["sqb0", "sqb1"]
    rowtags = ["mrow", "vrow", "srow", "rstd"]

    for i in range(NTILES):
        b, t0 = tile_bt(i)
        u = i % 2
        lq = sy if i % 2 == 0 else gp
        for c in range(DC):
            fw.dma(lq, lambda e, b=b, c=c, t0=t0, u=u: e.dma_start(
                out=bnd[u][c][:], in_=DR["band_s"][b, c, :, t0:t0 + T]),
                reads=(f"bdd{b}_{c}",), writes=(f"bnd{u}_{c}",))
            fw.dma(lq, lambda e, b=b, c=c, t0=t0, u=u: e.dma_start(
                out=xf[u][c][:], in_=DR["x2_s"][b, c, :, t0:t0 + T]),
                writes=(f"xf{u}_{c}",))
        for m in range(DC):
            for k in range(DC):
                fw.op(te, lambda e, m=m, k=k, u=u: e.matmul(
                    out=pW[m % 2][:], lhsT=w2[k][:, m * 128:(m + 1) * 128], rhs=bnd[u][k][:],
                    start=(k == 0), stop=(k == DC - 1)),
                    reads=(f"w2{k}", f"bnd{u}_{k}"), writes=(f"pW{m % 2}",))
            fw.op(sc, lambda e, m=m: e.activation(out=y[m][:], in_=pW[m % 2][:], func=AF.Identity,
                                                  bias=b2[:, m:m + 1]),
                  reads=(f"pW{m % 2}", "b2"), writes=(f"y{m}",))
            fw.op(gp, lambda e, m=m, u=u: e.tensor_tensor(out=y[m][:], in0=y[m][:],
                                                          in1=xf[u][m][:], op=ALU.add),
                  reads=(f"y{m}", f"xf{u}_{m}"), writes=(f"y{m}",))
        emit_postln(fw, y, ytags, ybf, ybftags, sqb, sqbtags, rows, rowtags,
                    bankA, "bankA", pMB, pRB, ones_bf, ones128r,
                    psMBtag="psMB", psRBtag="psMB")
        for c in range(DC):
            fw.dma(sy, lambda e, b=b, c=c, t0=t0: e.dma_start(
                out=DR["x3_s"][b, c, :, t0:t0 + T], in_=y[c][:]),
                reads=(f"y{c}",))
    fw.flush(block)



def emit_p3(fw, block, DR, es):
    nc = fw.nc
    v, sc, te, sy, gp = "vector", "scalar", "tensor", "sync", "gpsimd"

    with ExitStack() as ges:
        gsb = lambda n, s, dt=F32: ges.enter_context(nc.sbuf_tensor("g_" + n, s, dt))
        gps = lambda n, s: ges.enter_context(nc.psum_tensor("g_" + n, s, F32))
        wr = [gsb(f"wr{k}", [128, NEXP]) for k in range(DC)]
        xb3 = [[gsb(f"xb{u}_{k}", [128, T]) for k in range(DC)] for u in range(2)]
        et = gsb("et", [128, NEXP])
        msk = gsb("msk", [128, NEXP])
        em1 = gsb("em1", [128, 1])
        nem1 = gsb("nem1", [128, 1])
        em2 = gsb("em2", [128, 1])
        den = gsb("den", [128, 1])
        gt = gsb("gt", [128, NEXP])
        gtb = gsb("gtb", [8, 128], BF16)
        ident = gsb("ident", [128, 128])
        pLf = gps("pLf", [128, 512])
        pTf = gps("pTf", [128, 512])
        pL = pLf[:, 0:NEXP]
        pT = pTf[0:8, 0:128]

        fw.dma(sy, lambda e: e.dma_start(out=ident[:], in_=DR["ident"]), writes=("ident",))
        for k in range(DC):
            fw.dma(sy, lambda e, k=k: e.dma_start(out=wr[k][:], in_=DR["p3_wr"][k]),
                   writes=(f"wr{k}",))
        for i in range(NTILES):
            b, t0 = tile_bt(i)
            u = i % 2
            for c in range(DC):
                fw.dma(sy, lambda e, b=b, c=c, t0=t0, u=u: e.dma_start(
                    out=xb3[u][c][:], in_=DR["x3_s"][b, c, :, t0:t0 + T]),
                    writes=(f"xb{u}_{c}",))
            for g in range(T // 128):
                for k in range(DC):
                    fw.op(te, lambda e, g=g, k=k, u=u: e.matmul(
                        out=pL, lhsT=xb3[u][k][:, g * 128:(g + 1) * 128], rhs=wr[k][:],
                        start=(k == 0), stop=(k == DC - 1)),
                        reads=(f"xb{u}_{k}", f"wr{k}"), writes=("pL",))
                fw.op(v, lambda e: e.tensor_reduce(out=em1[:], in_=pL, axis=AX.X, op=ALU.max),
                      reads=("pL",), writes=("em1",))
                fw.op(v, lambda e: e.tensor_scalar(out=nem1[:], in0=em1[:], scalar1=-1.0,
                                                   scalar2=None, op0=ALU.mult),
                      reads=("em1",), writes=("nem1",))
                fw.op(sc, lambda e: e.activation(out=et[:], in_=pL, func=AF.Exp,
                                                 bias=nem1[:, 0:1]),
                      reads=("pL", "nem1"), writes=("et",))
                fw.op(v, lambda e: e.tensor_reduce(out=em1[:], in_=et[:], axis=AX.X, op=ALU.max),
                      reads=("et",), writes=("em1",))
                fw.op(v, lambda e: e.tensor_scalar(out=msk[:], in0=et[:], scalar1=em1[:, 0:1],
                                                   scalar2=None, op0=ALU.is_lt),
                      reads=("et", "em1"), writes=("msk",))
                fw.op(v, lambda e: e.tensor_tensor(out=msk[:], in0=msk[:], in1=et[:], op=ALU.mult),
                      reads=("msk", "et"), writes=("msk",))
                fw.op(v, lambda e: e.tensor_reduce(out=em2[:], in_=msk[:], axis=AX.X, op=ALU.max),
                      reads=("msk",), writes=("em2",))
                fw.op(v, lambda e: e.tensor_scalar(out=gt[:], in0=et[:], scalar1=em2[:, 0:1],
                                                   scalar2=None, op0=ALU.is_ge),
                      reads=("et", "em2"), writes=("gt",))
                fw.op(v, lambda e: e.tensor_tensor(out=gt[:], in0=gt[:], in1=et[:], op=ALU.mult),
                      reads=("gt", "et"), writes=("gt",))
                fw.op(v, lambda e: e.tensor_tensor(out=den[:], in0=em1[:], in1=em2[:], op=ALU.add),
                      reads=("em1", "em2"), writes=("den",))
                fw.op(v, lambda e: e.reciprocal(out=den[:], in_=den[:]),
                      reads=("den",), writes=("den",))
                fw.op(v, lambda e: e.tensor_scalar(out=gt[:], in0=gt[:], scalar1=den[:, 0:1],
                                                   scalar2=None, op0=ALU.mult),
                      reads=("gt", "den"), writes=("gt",))
                fw.op(te, lambda e: e.transpose(out=pT, in_=gt[:], identity=ident[:]),
                      reads=("gt", "ident"), writes=("pT",))
                fw.op(v, lambda e: e.tensor_copy(out=gtb[:], in_=pT),
                      reads=("pT",), writes=("gtb",))
                fw.dma(sy, lambda e, i=i, g=g: e.dma_start(
                    out=DR["gates_s"][:, i * T + g * 128:i * T + (g + 1) * 128], in_=gtb[:]),
                    reads=("gtb",))
        fw.flush(block)

    with ExitStack() as bes:
        bsb = lambda n, s, dt=F32: bes.enter_context(nc.sbuf_tensor("e_" + n, s, dt))
        bps = lambda n, s: bes.enter_context(nc.psum_tensor("e_" + n, s, F32))
        xb3 = [bsb(f"xb{k}", [128, TB], BF16) for k in range(DC)]
        acc = [bsb(f"acc{k}", [128, TB]) for k in range(DC)]
        hid = [bsb(f"hid{k}", [128, TB], BF16) for k in range(HC)]
        wfcT = [bsb(f"wfcT{u}", [128, DC * 1024], BF16) for u in range(2)]
        wpjT = [bsb(f"wpjT{u}", [128, HC * 256], BF16) for u in range(2)]
        bfc = [bsb(f"bfc{u}", [128, HC]) for u in range(2)]
        bpj = [bsb(f"bpj{u}", [128, DC]) for u in range(2)]
        grow = [bsb(f"grow{u}", [1, TB], BF16) for u in range(2)]
        gbc = bsb("gbc", [128, TB], BF16)
        tmp = bsb("tmp", [128, T])
        ones_bf = bsb("onesbf", [128, 1], BF16)
        ones128r = bsb("o128r", [1, 128])
        ones128r_bf = bsb("o128rb", [1, 128], BF16)
        pFC = [bps(f"pFC{q}", [128, T]) for q in range(4)]
        pPJ = [bps(f"pPJ{q}", [128, T]) for q in range(4)]
        xf = [hid[c][:].bitcast(F32) for c in range(DC)]
        ybf = [hid[DC + c] for c in range(DC)]
        sqb = [hid[2 * DC], hid[2 * DC + 1]]
        rows = [hid[2 * DC + 2 + j][:].bitcast(F32)[0:1, 0:T] for j in range(4)]
        pMB = pPJ[0]
        pRB = pPJ[1]
        bankA = pPJ[2]

        fw.op(v, lambda e: e.memset(ones_bf[:], 1.0), writes=("ones",))
        fw.op(v, lambda e: e.memset(ones128r[:], 1.0), writes=("ones",))
        fw.op(v, lambda e: e.memset(ones128r_bf[:], 1.0), writes=("ones",))

        wfc_tags = tuple(f"wfcT{u}" for u in range(2))
        wpj_tags = tuple(f"wpjT{u}" for u in range(2))

        ytags = [f"acc{c}" for c in range(DC)]
        ybftags = [f"hid{DC + c}" for c in range(DC)]
        sqbtags = [f"hid{2 * DC}", f"hid{2 * DC + 1}"]
        rowtags = [f"hid{2 * DC + 2 + j}" for j in range(4)]

        mg_ctr = [0]
        pj_ctr = [0]

        for blk in range(NBLK):
            b = blk // BLK_PER_SEQ
            tb0 = (blk % BLK_PER_SEQ) * TB
            for k in range(DC):
                fw.dma(gp, lambda e, b=b, k=k, tb0=tb0: e.dma_start(
                    out=xb3[k][:], in_=DR["x3_s"][b, k, :, tb0:tb0 + TB]),
                    writes=(f"xb3{k}",))
            for eidx in range(NEXP + 1):
                gu = eidx % 2
                if eidx > 0:
                    fw.dma(sy, lambda e, eidx=eidx, blk=blk, gu=gu: e.dma_start(
                        out=grow[gu][:], in_=DR["gates_s"][eidx - 1:eidx, blk * TB:(blk + 1) * TB]),
                        writes=(f"grow{gu}",))
                for mg in range(4):
                    u = mg_ctr[0] % 2
                    mg_ctr[0] += 1
                    fw.dma(gp, lambda e, eidx=eidx, mg=mg, u=u: e.dma_start(
                        out=wfcT[u][:].rearrange("p (k m) -> p k m", k=DC),
                        in_=DR["p3_wfc"][eidx, :, :, mg * 1024:(mg + 1) * 1024].rearrange(
                            "k p m -> p k m")),
                        writes=(wfc_tags[u],))
                    if mg == 0:
                        fw.dma(gp, lambda e, eidx=eidx, gu=gu: e.dma_start(
                            out=bfc[gu][:], in_=DR["p3_bfc"][eidx]),
                            writes=(f"bfc{gu}",))
                        fw.dma(gp, lambda e, eidx=eidx, gu=gu: e.dma_start(
                            out=bpj[gu][:], in_=DR["p3_bpj"][eidx]),
                            writes=(f"bpj{gu}",))
                    for mi in range(DC):
                        m = mg * DC + mi
                        q = (mi % 2) * 2
                        for k in range(DC):
                            for t in range(NT_B):
                                fw.op(te, lambda e, u=u, k=k, mi=mi, t=t, q=q: e.matmul(
                                    out=pFC[q + t][:],
                                    lhsT=wfcT[u][:, k * 1024 + mi * 128:k * 1024 + (mi + 1) * 128],
                                    rhs=xb3[k][:, t * T:(t + 1) * T],
                                    start=(k == 0), stop=(k == DC - 1)),
                                    reads=(wfc_tags[u], f"xb3{k}"), writes=(f"pFC{q + t}",))
                        for t in range(NT_B):
                            fw.op(sc, lambda e, gu=gu, m=m, t=t, q=q: e.activation(
                                out=hid[m][:, t * T:(t + 1) * T], in_=pFC[q + t][:],
                                func=AF.Gelu_apprx_tanh, bias=bfc[gu][:, m:m + 1]),
                                reads=(f"pFC{q + t}", f"bfc{gu}"), writes=(f"hid{m}",))
                if eidx > 0:
                    for t in range(NT_B):
                        fw.op(te, lambda e, gu=gu, t=t: e.matmul(
                            out=pFC[t][:], lhsT=ones128r_bf[:], rhs=grow[gu][0:1, t * T:(t + 1) * T],
                            start=True, stop=True),
                            reads=(f"grow{gu}", "ones"), writes=(f"pFC{t}",))
                        fw.op(v, lambda e, t=t: e.tensor_copy(out=gbc[:, t * T:(t + 1) * T],
                                                              in_=pFC[t][:]),
                              reads=(f"pFC{t}",), writes=("gbc",))
                for mg in range(4):
                    u = pj_ctr[0] % 2
                    pj_ctr[0] += 1
                    fw.dma(gp, lambda e, eidx=eidx, mg=mg, u=u: e.dma_start(
                        out=wpjT[u][:].rearrange("p (k m) -> p k m", k=HC),
                        in_=DR["p3_wpj"][eidx, :, :, mg * 256:(mg + 1) * 256].rearrange(
                            "k p m -> p k m")),
                        writes=(wpj_tags[u],))
                    for mi in range(2):
                        m2 = mg * 2 + mi
                        q = (m2 % 2) * 2
                        for k in range(HC):
                            for t in range(NT_B):
                                fw.op(te, lambda e, u=u, k=k, mi=mi, t=t, q=q: e.matmul(
                                    out=pPJ[q + t][:],
                                    lhsT=wpjT[u][:, k * 256 + mi * 128:k * 256 + (mi + 1) * 128],
                                    rhs=hid[k][:, t * T:(t + 1) * T],
                                    start=(k == 0), stop=(k == HC - 1)),
                                    reads=(wpj_tags[u], f"hid{k}"), writes=(f"pPJ{q + t}",))
                        for t in range(NT_B):
                            if eidx == 0:
                                fw.op(sc, lambda e, gu=gu, m2=m2, t=t, q=q: e.activation(
                                    out=acc[m2][:, t * T:(t + 1) * T], in_=pPJ[q + t][:],
                                    func=AF.Identity, bias=bpj[gu][:, m2:m2 + 1]),
                                    reads=(f"pPJ{q + t}", f"bpj{gu}"), writes=(f"acc{m2}",))
                            else:
                                fw.op(sc, lambda e, gu=gu, m2=m2, t=t, q=q: e.activation(
                                    out=tmp[:], in_=pPJ[q + t][:], func=AF.Identity,
                                    bias=bpj[gu][:, m2:m2 + 1]),
                                    reads=(f"pPJ{q + t}", f"bpj{gu}"), writes=("tmp",))
                                fw.op(v, lambda e, t=t: e.tensor_tensor(
                                    out=tmp[:], in0=tmp[:], in1=gbc[:, t * T:(t + 1) * T],
                                    op=ALU.mult),
                                    reads=("tmp", "gbc"), writes=("tmp",))
                                fw.op(v, lambda e, m2=m2, t=t: e.tensor_tensor(
                                    out=acc[m2][:, t * T:(t + 1) * T],
                                    in0=acc[m2][:, t * T:(t + 1) * T], in1=tmp[:], op=ALU.add),
                                    reads=("tmp", f"acc{m2}"), writes=(f"acc{m2}",))
            for t in range(NT_B):
                t0 = tb0 + t * T
                yv = [acc[c][:, t * T:(t + 1) * T] for c in range(DC)]
                xfv = [xf[c][:, 0:T] for c in range(DC)]
                ybfv = [ybf[c][:, 0:T] for c in range(DC)]
                sqbv = [sqb[0][:, 0:T], sqb[1][:, 0:T]]
                for c in range(DC):
                    fw.dma(sy, lambda e, b=b, c=c, t0=t0, xfv=xfv: e.dma_start(
                        out=xfv[c], in_=DR["x3_s"][b, c, :, t0:t0 + T]),
                        writes=(f"hid{c}",))
                    fw.op(v, lambda e, c=c, yv=yv, xfv=xfv: e.tensor_tensor(
                        out=yv[c], in0=yv[c], in1=xfv[c], op=ALU.add),
                        reads=(f"acc{c}", f"hid{c}"), writes=(f"acc{c}",))
                emit_postln(fw, yv, ytags,
                            ybfv, ybftags, sqbv, sqbtags, rows, rowtags,
                            bankA, "pPJ2", pMB, pRB, ones_bf, ones128r,
                            psMBtag="pPJ0", psRBtag="pPJ1")
                for c in range(DC):
                    fw.dma(sy, lambda e, b=b, c=c, t0=t0, yv=yv: e.dma_start(
                        out=DR["out"][b, c, :, t0:t0 + T], in_=yv[c]),
                        reads=(f"acc{c}",))
        fw.flush(block)



def _ap(x):
    try:
        return x[:]
    except Exception:
        return x


def emit_postln(fw, y, ytags, ybf, ybftags, sqb, sqbtags, rows, rowtags,
                psA, psAtag, psMB, psRB, ones_bf, ones128r,
                psMBtag="psMB", psRBtag="psRB"):
    v, sc, te = "vector", "scalar", "tensor"
    ya = [_ap(t) for t in y]
    ybfa = [_ap(t) for t in ybf]
    sqba = [_ap(t) for t in sqb]
    rowsa = [_ap(t) for t in rows]
    mrow, vrow, srow, rstd = rowsa
    tm, tv, ts_, tr = rowtags
    for c in range(DC):
        fw.op(v, lambda e, c=c: e.tensor_copy(out=ybfa[c], in_=ya[c]),
              reads=(ytags[c],), writes=(ybftags[c],))
        fw.op(sc, lambda e, c=c: e.activation(out=sqba[c % 2], in_=ya[c], func=AF.Square),
              reads=(ytags[c],), writes=(sqbtags[c % 2],))
        fw.op(te, lambda e, c=c: e.matmul(out=psA[32:33, :], lhsT=ones_bf[:], rhs=sqba[c % 2],
                                          start=(c == 0), stop=(c == DC - 1)),
              reads=(sqbtags[c % 2], "ones"), writes=(psAtag,))
    for c in range(DC):
        fw.op(te, lambda e, c=c: e.matmul(out=psA[0:1, :], lhsT=ones_bf[:], rhs=ybfa[c],
                                          start=(c == 0), stop=(c == DC - 1)),
              reads=(ybftags[c], "ones"), writes=(psAtag,))
    fw.op(v, lambda e: e.tensor_scalar(out=mrow, in0=psA[0:1, :], scalar1=1.0 / D,
                                       scalar2=None, op0=ALU.mult),
          reads=(psAtag,), writes=(tm,))
    fw.op(v, lambda e: e.tensor_scalar(out=vrow, in0=psA[32:33, :], scalar1=1.0 / D,
                                       scalar2=EPS, op0=ALU.mult, op1=ALU.add),
          reads=(psAtag,), writes=(tv,))
    fw.op(v, lambda e: e.tensor_tensor(out=srow, in0=mrow, in1=mrow, op=ALU.mult),
          reads=(tm,), writes=(ts_,))
    fw.op(v, lambda e: e.tensor_tensor(out=vrow, in0=vrow, in1=srow, op=ALU.subtract),
          reads=(tv, ts_), writes=(tv,))
    fw.op(sc, lambda e: e.activation(out=srow, in_=vrow, func=AF.Sqrt),
          reads=(tv,), writes=(ts_,))
    fw.op(v, lambda e: e.reciprocal(out=rstd, in_=srow),
          reads=(ts_,), writes=(tr,))
    fw.op(te, lambda e: e.matmul(out=psMB[:], lhsT=ones128r[:], rhs=mrow, start=True, stop=True),
          reads=(tm, "ones"), writes=(psMBtag,))
    for c in range(DC):
        fw.op(v, lambda e, c=c: e.tensor_tensor(out=ya[c], in0=ya[c], in1=psMB[:],
                                                op=ALU.subtract),
              reads=(ytags[c], psMBtag), writes=(ytags[c],))
    fw.op(te, lambda e: e.matmul(out=psRB[:], lhsT=ones128r[:], rhs=rstd, start=True, stop=True),
          reads=(tr, "ones"), writes=(psRBtag,))
    for c in range(DC):
        fw.op(v, lambda e, c=c: e.tensor_tensor(out=ya[c], in0=ya[c], in1=psRB[:],
                                                op=ALU.mult),
              reads=(ytags[c], psRBtag), writes=(ytags[c],))



WEIGHT_SPECS = {
    "p0_wd": (DC, 128, D), "p0_bd": (128, DC), "p0_wb": (DC, 128, 16),
    "p0_bB": (16, 1), "p0_alog": (16, 1), "p0_wo": (16, D), "p0_bo": (128, DC),
    "p0_dp": (128, DC), "ident": (128, 128),
    "p1_w": (DC, 128, D), "p1_b": (128, DC),
    "p2_w1": (DC, 128, D), "p2_b1": (128, DC), "p2_w2": (DC, 128, D),
    "p2_b2": (128, DC), "p2_mw": (DC, 128, MEM + 1),
    "p3_wr": (DC, 128, NEXP), "p3_wfc": (NEXP + 1, DC, 128, HID),
    "p3_wpj": (NEXP + 1, HC, 128, D), "p3_bfc": (NEXP + 1, 128, HC),
    "p3_bpj": (NEXP + 1, 128, DC),
}

SCRATCH_SPECS = {
    "x1_s": ((NSEQ, DC, 128, S), F32),
    "x2_s": ((NSEQ, DC, 128, S), F32),
    "p_s": ((NSEQ, DC, 128, S), BF16),
    "tot_s": ((128, 2 * DC), F32),
    "band_s": ((NSEQ, DC, 128, S), BF16),
    "x3_s": ((NSEQ, DC, 128, S), F32),
    "gates_s": ((NEXP, NSEQ * S), BF16),
    "lmax_out": ((1, 1), F32),
}

PHASE_DEFS = {
    "p0": dict(weights=["p0_wd", "p0_bd", "p0_wb", "p0_bB", "p0_alog", "p0_wo",
                        "p0_bo", "p0_dp", "ident"],
               sin=["x_in"], sout=["x1_s", "lmax_out"]),
    "p1": dict(weights=["p1_w", "p1_b", "ident"],
               sin=["x1_s", "maxes"], sout=["x2_s"]),
    "p2a": dict(weights=["p2_w1", "p2_b1"], sin=["x2_s"], sout=["p_s", "tot_s"]),
    "p2b": dict(weights=["p2_w2", "p2_b2", "p2_mw", "ident"],
                sin=["p_s", "tot_s", "x2_s"], sout=["x3_s"], scratch=["band_s"]),
    "p3": dict(weights=["p3_wr", "p3_wfc", "p3_wpj", "p3_bfc", "p3_bpj", "ident"],
               sin=["x3_s"], sout=["out"], scratch=["gates_s"]),
}


def build_unfused(phase):
    nc = bass.Bass("TRN2", target_bir_lowering=False, debug=False)
    pd = PHASE_DEFS[phase]
    DR = {}
    for w in pd["weights"]:
        DR[w] = nc.dram_tensor(w, list(WEIGHT_SPECS[w]), F32, kind="ExternalInput").ap()
    for s in pd["sin"]:
        if s == "x_in":
            DR[s] = nc.dram_tensor("x_in", [NSEQ, DC, 128, S], F32, kind="ExternalInput").ap()
        elif s == "maxes":
            DR[s] = nc.dram_tensor("maxes", [1, NCORES], F32, kind="ExternalInput").ap()
        else:
            shp, dt = SCRATCH_SPECS[s]
            DR[s] = nc.dram_tensor(s, list(shp), dt, kind="ExternalInput").ap()
    for s in pd["sout"]:
        if s == "out":
            DR[s] = nc.dram_tensor("out", [NSEQ, DC, 128, S], F32, kind="ExternalOutput").ap()
        else:
            shp, dt = SCRATCH_SPECS[s]
            DR[s] = nc.dram_tensor(s, list(shp), dt, kind="ExternalOutput").ap()
    for s in pd.get("scratch", []):
        shp, dt = SCRATCH_SPECS[s]
        DR[s] = nc.dram_tensor(s, list(shp), dt).ap()

    with ExitStack() as es:
        fw = FW(nc, es)
        with nc.Block() as block, ExitStack() as pes:
            if phase == "p0":
                emit_p0(fw, block, DR, pes)
            elif phase == "p1":
                emit_p1(fw, block, DR, pes, n_max=NCORES)
            elif phase == "p2a":
                emit_p2a(fw, block, DR, pes)
            elif phase == "p2b":
                emit_p2b(fw, block, DR, pes)
            elif phase == "p3":
                emit_p3(fw, block, DR, pes)
    return nc


def build_fused():
    from concourse.replica_groups import maybe_share_collective_output_space
    nc = bass.Bass("TRN2", target_bir_lowering=False, debug=False)
    DR = {}
    for w, shp in WEIGHT_SPECS.items():
        DR[w] = nc.dram_tensor(w, list(shp), F32, kind="ExternalInput").ap()
    DR["x_in"] = nc.dram_tensor("x_in", [NSEQ, DC, 128, S], F32, kind="ExternalInput").ap()
    DR["out"] = nc.dram_tensor("out", [NSEQ, DC, 128, S], F32, kind="ExternalOutput").ap()
    for s, (shp, dt) in SCRATCH_SPECS.items():
        DR[s] = nc.dram_tensor(s, list(shp), dt).ap()
    aspace = maybe_share_collective_output_space("AllReduce", [list(range(NCORES))], prefer_local=True)
    DR["maxes"] = nc.dram_tensor("maxes", [1, 1], F32, addr_space=aspace).ap()

    with ExitStack() as es:
        fw = FW(nc, es)
        with nc.Block() as block:
            with ExitStack() as pes:
                emit_p0(fw, block, DR, pes)
            emit_cc(fw, block, DR)
            with ExitStack() as pes:
                emit_p1(fw, block, DR, pes, n_max=1)
            with ExitStack() as pes:
                emit_p2a(fw, block, DR, pes, flush=False)
                emit_p2b(fw, block, DR, pes)
            with ExitStack() as pes:
                emit_p3(fw, block, DR, pes)
    return nc



def kernel(**inputs):
    W = prep_weights(inputs)
    xs = shard_x(inputs["L0_attn_x"])

    if FUSED:
        nc = build_fused()
        in_maps = [dict(W, x_in=xs[c]) for c in range(NCORES)]
        res = run_bass_kernel_spmd(nc, in_maps, core_ids=list(range(NCORES)))
        return unshard_out([r["out"] for r in res.results])

    state = [dict() for _ in range(NCORES)]

    def run(phase, feed_keys, extra=None):
        nc = build_unfused(phase)
        pd = PHASE_DEFS[phase]
        in_maps = []
        for c in range(NCORES):
            m = {w: W[w] for w in pd["weights"]}
            for s in pd["sin"]:
                if s == "x_in":
                    m["x_in"] = xs[c]
                elif s == "maxes":
                    m["maxes"] = extra[c]
                else:
                    m[s] = state[c][s]
            in_maps.append(m)
        res = run_bass_kernel_spmd(nc, in_maps, core_ids=list(range(NCORES)))
        for c in range(NCORES):
            for s in pd["sout"]:
                state[c][s] = res.results[c][s]
        return res

    run("p0", None)
    maxes = np.concatenate([state[c]["lmax_out"].reshape(1, 1) for c in range(NCORES)],
                           axis=1).astype(np.float32)
    run("p1", None, extra=[maxes for _ in range(NCORES)])
    run("p2a", None)
    run("p2b", None)
    run("p3", None)
    return unshard_out([state[c]["out"] for c in range(NCORES)])


# version 12
# speedup vs baseline: 1.0204x; 1.0008x over previous
import numpy as np
from contextlib import ExitStack

import concourse.bass as bass
import concourse.mybir as mybir
from concourse.bass_utils import run_bass_kernel_spmd

F32 = mybir.dt.float32
BF16 = mybir.dt.bfloat16
AF = mybir.ActivationFunctionType
ALU = mybir.AluOpType
AX = mybir.AxisListType

MAGIC = float(np.float32(1.5 * 2 ** 23))

NCORES = 8
NSEQ = 2
S = 4096
D = 1024
DC = D // 128
T = 512
NT_SEQ = S // T
NTILES = NSEQ * NT_SEQ
EPS = 1e-5

HID = 4096
HC = HID // 128
NEXP = 8
TB = 1024
NT_B = TB // T
NBLK = NSEQ * S // TB
BLK_PER_SEQ = S // TB

MEM = 40

FUSED = True



def _cm(w):
    w = np.ascontiguousarray(np.asarray(w), dtype=np.float32)
    return np.ascontiguousarray(w.reshape(w.shape[0] // 128, 128, w.shape[1]))


def _cv(v):
    v = np.ascontiguousarray(np.asarray(v), dtype=np.float32)
    return np.ascontiguousarray(v.reshape(-1, 128).T)


def prep_weights(i):
    d = {}
    d["p0_wd"] = _cm(np.asarray(i["L0_attn_W_delta"]).T)
    d["p0_bd"] = _cv(i["L0_attn_b_delta"])
    d["p0_wb"] = _cm(np.asarray(i["L0_attn_W_B"]).T)
    d["p0_bB"] = np.ascontiguousarray(np.asarray(i["L0_attn_b_B"], dtype=np.float32).reshape(16, 1))
    d["p0_alog"] = np.ascontiguousarray(np.asarray(i["L0_attn_A_log"], dtype=np.float32).reshape(16, 1))
    d["p0_wo"] = np.ascontiguousarray(np.asarray(i["L0_attn_W_out"], dtype=np.float32).T)
    d["p0_bo"] = _cv(i["L0_attn_b_out"])
    d["p0_dp"] = _cv(i["L0_attn_D_param"])
    d["ident"] = np.eye(128, dtype=np.float32)
    d["p1_w"] = _cm(np.asarray(i["L0_ffn_weight"]).T)
    d["p1_b"] = _cv(i["L0_ffn_bias"])
    d["p2_w1"] = _cm(i["L1_attn_W1"])
    d["p2_b1"] = _cv(i["L1_attn_bias1"])
    d["p2_w2"] = _cm(i["L1_attn_W2"])
    d["p2_b2"] = _cv(i["L1_attn_bias2"])
    d["p2_mw"] = np.ascontiguousarray(
        np.asarray(i["L1_attn_memory_weights"], dtype=np.float32).T.reshape(DC, 128, MEM + 1))
    d["p3_wr"] = _cm(i["L1_ffn_Wr"])
    wfc = np.concatenate([np.asarray(i["L1_ffn_Wfc_s"], dtype=np.float32)[None],
                          np.asarray(i["L1_ffn_Wfc"], dtype=np.float32)], axis=0)
    d["p3_wfc"] = np.ascontiguousarray(wfc.reshape(NEXP + 1, DC, 128, HID))
    wpj = np.concatenate([np.asarray(i["L1_ffn_Wproj_s"], dtype=np.float32)[None],
                          np.asarray(i["L1_ffn_Wproj"], dtype=np.float32)], axis=0)
    d["p3_wpj"] = np.ascontiguousarray(wpj.reshape(NEXP + 1, HC, 128, D))
    bfc = np.concatenate([np.asarray(i["L1_ffn_bfc_s"], dtype=np.float32)[None],
                          np.asarray(i["L1_ffn_bfc"], dtype=np.float32)], axis=0)
    d["p3_bfc"] = np.ascontiguousarray(
        bfc.reshape(NEXP + 1, HC, 128).transpose(0, 2, 1))
    bpj = np.concatenate([np.asarray(i["L1_ffn_bproj_s"], dtype=np.float32)[None],
                          np.asarray(i["L1_ffn_bproj"], dtype=np.float32)], axis=0)
    d["p3_bpj"] = np.ascontiguousarray(
        bpj.reshape(NEXP + 1, DC, 128).transpose(0, 2, 1))
    return d


def shard_x(x):
    x = np.asarray(x, dtype=np.float32)
    out = []
    for c in range(NCORES):
        xc = x[NSEQ * c:NSEQ * (c + 1)]
        out.append(np.ascontiguousarray(
            xc.reshape(NSEQ, S, DC, 128).transpose(0, 2, 3, 1)))
    return out


def unshard_out(shards):
    out = np.empty((NCORES * NSEQ, S, D), dtype=np.float32)
    for c, o in enumerate(shards):
        out[NSEQ * c:NSEQ * (c + 1)] = np.asarray(o).transpose(0, 3, 1, 2).reshape(NSEQ, S, D)
    return out


def tile_bt(i):
    return i // NT_SEQ, (i % NT_SEQ) * T



ENGINES = ("sync", "gpsimd", "vector", "scalar", "tensor")
QUEUES = ("sync", "gpsimd")


class _Op:
    __slots__ = ("eng", "fn", "inc_val", "deps", "needs_inc", "inc_target",
                 "batch_end", "self_wait", "dma")

    def __init__(self, eng, fn, inc_val, dma):
        self.eng = eng
        self.fn = fn
        self.inc_val = inc_val
        self.dma = dma
        self.deps = []
        self.needs_inc = False
        self.inc_target = None
        self.batch_end = None
        self.self_wait = None


class FW:
    def __init__(self, nc, es):
        self.nc = nc
        self.sems = {e: es.enter_context(nc.semaphore(f"S_{e}")) for e in ENGINES}
        self.bsems = {q: es.enter_context(nc.semaphore(f"B_{q}")) for q in QUEUES}
        self.csems = {q: es.enter_context(nc.semaphore(f"C_{q}")) for q in QUEUES}
        self.icount = {e: 0 for e in ENGINES}
        self.bcount = {q: 0 for q in QUEUES}
        self.ccount = {q: 0 for q in QUEUES}
        self.prev_totals = None
        self.ops = []
        self.last_w = {}
        self.readers = {}

    def op(self, eng, fn, reads=(), writes=(), inc_val=1, dma=False):
        o = _Op(eng, fn, inc_val, dma)
        deps = {}
        for t in reads:
            w = self.last_w.get(t)
            if w is not None:
                deps[id(w)] = w
        for t in writes:
            w = self.last_w.get(t)
            if w is not None:
                deps[id(w)] = w
            for r in self.readers.get(t, ()):
                deps[id(r)] = r
        for t in reads:
            self.readers.setdefault(t, []).append(o)
        for t in writes:
            self.last_w[t] = o
            self.readers[t] = []
        for d in deps.values():
            if d is o:
                continue
            if d.eng == "tensor" and eng == "tensor":
                continue
            o.deps.append(d)
            d.needs_inc = True
        self.ops.append(o)
        return o

    def dma(self, queue, fn, reads=(), writes=()):
        return self.op(queue, fn, reads, writes, inc_val=16, dma=True)

    def flush(self, block):
        for o in self.ops:
            if o.eng in QUEUES and o.dma:
                self.icount[o.eng] += o.inc_val
                o.inc_target = self.icount[o.eng]
            elif o.eng in QUEUES and o.needs_inc:
                self.ccount[o.eng] += o.inc_val
                o.inc_target = self.ccount[o.eng]
            elif o.needs_inc:
                self.icount[o.eng] += o.inc_val
                o.inc_target = self.icount[o.eng]
        open_batch = {q: [] for q in QUEUES}
        for o in self.ops:
            for d in o.deps:
                if d.eng in QUEUES and d.dma and d.batch_end is None:
                    ob = open_batch[d.eng]
                    if ob:
                        last = ob[-1]
                        val = last.inc_target
                        for m in ob:
                            m.batch_end = val
                        last.self_wait = val
                        open_batch[d.eng] = []
            if o.eng in QUEUES and o.dma:
                open_batch[o.eng].append(o)
        for q in QUEUES:
            for m in open_batch[q]:
                m.batch_end = self.icount[q]

        queue_totals = {e: self.icount[e] for e in ENGINES}
        for e in ENGINES:
            if e not in QUEUES:
                self.icount[e] += 1
        for q in QUEUES:
            self.bcount[q] += 1
        new_totals = {e: (self.bcount[e] if e in QUEUES else self.icount[e])
                      for e in ENGINES}
        prev = self.prev_totals
        ops = self.ops
        sems = self.sems
        bsems = self.bsems
        csems = self.csems

        for ename in ENGINES:
            eng_ops = [o for o in ops if o.eng == ename]
            sec = getattr(block, ename)

            def _body(eng, ename=ename, eng_ops=eng_ops, prev=prev,
                      queue_totals=queue_totals):
                hwm = {}
                if prev is not None:
                    for oe in ENGINES:
                        if oe != ename:
                            eng.wait_ge(bsems[oe] if oe in QUEUES else sems[oe], prev[oe])
                            if oe not in QUEUES:
                                hwm[oe] = prev[oe]
                for o in eng_ops:
                    for d in o.deps:
                        if d.eng in QUEUES and d.dma:
                            key, sem, v = d.eng, sems[d.eng], d.batch_end
                        elif d.eng in QUEUES:
                            key, sem, v = "c" + d.eng, csems[d.eng], d.inc_target
                        else:
                            key, sem, v = d.eng, sems[d.eng], d.inc_target
                        if hwm.get(key, -1) < v:
                            eng.wait_ge(sem, v)
                            hwm[key] = v
                    inst = o.fn(eng)
                    if o.inc_target is not None:
                        if o.eng in QUEUES and not o.dma:
                            inst.then_inc(csems[o.eng], o.inc_val)
                        else:
                            inst.then_inc(sems[o.eng], o.inc_val)
                    if o.self_wait is not None:
                        eng.wait_ge(sems[ename], o.self_wait)
                        hwm[ename] = o.self_wait
                if ename in QUEUES:
                    eng.drain()
                    eng.wait_ge(sems[ename], queue_totals[ename])
                    eng.sem_inc(bsems[ename], 1)
                else:
                    eng.drain()
                    eng.sem_inc(sems[ename], 1)

            sec(_body)

        self.ops = []
        self.last_w = {}
        self.readers = {}
        self.prev_totals = new_totals




def emit_p0(fw, block, DR, es):
    nc = fw.nc
    sb = lambda n, s, dt=F32: es.enter_context(nc.sbuf_tensor("a0" + n, s, dt))
    psm = lambda n, s: es.enter_context(nc.psum_tensor("a0" + n, s, F32))
    v, sc, te, sy, gp = "vector", "scalar", "tensor", "sync", "gpsimd"

    wd = [sb(f"wd{k}", [128, D], BF16) for k in range(DC)]
    wb = [sb(f"wb{k}", [128, 16], BF16) for k in range(DC)]
    wo = sb("wo", [16, D], BF16)
    bd2 = sb("bd2", [128, DC])
    bB = sb("bB", [16, 1])
    alog = sb("alog", [16, 1])
    acol = sb("acol", [16, 1])
    bo = sb("bo", [128, DC])
    ecol = sb("ecol", [128, DC])
    ident = sb("ident", [128, 128])
    ones_bf = sb("onesbf", [128, 1], BF16)
    ones16 = sb("ones16", [1, 16])
    c16 = sb("c16", [1, 16])
    ones128r = sb("o128r", [1, 128])
    xf = [[sb(f"xf{u}_{k}", [128, T]) for k in range(DC)] for u in range(2)]
    xbt = [[sb(f"xb{u}_{k}", [128, T], BF16) for k in range(DC)] for u in range(2)]
    tnh = [sb(f"tnh{k}", [128, T], BF16) for k in range(DC)]
    sqb = [sb(f"sqb{k}", [128, T], BF16) for k in range(2)]
    bm = sb("bm", [16, T])
    abar = sb("abar", [16, T])
    bu = sb("bu", [16, T])
    states = sb("states", [16, T])
    st_bf = sb("stbf", [16, T], BF16)
    carry = sb("carry", [16, 1])
    delta = sb("delta", [1, T])
    grow = sb("grow", [1, T])
    y = [sb(f"y{k}", [128, T]) for k in range(DC)]
    ybf = [sb(f"ybf{k}", [128, T], BF16) for k in range(DC)]
    rows = [sb("mrow", [1, T]), sb("vrow", [1, T]), sb("srow", [1, T]), sb("rstd", [1, T])]
    mx = sb("mx", [128, DC])
    mxt = sb("mxt", [128, 1])
    lrow = sb("lrow", [1, 128])
    lmax = sb("lmax", [1, 1])

    pU = [psm("pU0", [128, T]), psm("pU1", [128, T])]
    pO = [psm("pO0", [128, T]), psm("pO1", [128, T])]
    pMB = psm("pMB", [128, T])
    pRB = psm("pRB", [128, T])
    bankA = psm("bankA", [128, T])
    bankB = psm("bankB", [128, T])

    fw.dma(sy, lambda e: e.dma_start(out=bd2[:], in_=DR["p0_bd"]), writes=("bd2",))
    fw.dma(sy, lambda e: e.dma_start(out=bo[:], in_=DR["p0_bo"]), writes=("bo",))
    fw.dma(sy, lambda e: e.dma_start(out=ecol[:], in_=DR["p0_dp"]), writes=("ecol",))
    fw.dma(sy, lambda e: e.dma_start(out=bB[:], in_=DR["p0_bB"]), writes=("bB",))
    fw.dma(sy, lambda e: e.dma_start(out=alog[:], in_=DR["p0_alog"]), writes=("alog",))
    fw.dma(sy, lambda e: e.dma_start(out=ident[:], in_=DR["ident"]), writes=("ident",))
    for k in range(DC):
        fw.dma(gp, lambda e, k=k: e.dma_start(out=wd[k][:], in_=DR["p0_wd"][k]), writes=(f"wd{k}",))
        fw.dma(gp, lambda e, k=k: e.dma_start(out=wb[k][:], in_=DR["p0_wb"][k]), writes=(f"wb{k}",))
    fw.dma(gp, lambda e: e.dma_start(out=wo[:], in_=DR["p0_wo"]), writes=("wo",))
    fw.op(v, lambda e: e.memset(ones_bf[:], 1.0), writes=("ones",))
    fw.op(v, lambda e: e.memset(ones16[:], 1.0), writes=("ones",))
    fw.op(v, lambda e: e.memset(c16[:], 1.0 / D), writes=("ones",))
    fw.op(v, lambda e: e.memset(ones128r[:], 1.0), writes=("ones",))
    fw.op(v, lambda e: e.memset(mx[:], 0.0), writes=("mx",))
    fw.op(v, lambda e: e.tensor_scalar(out=bd2[:], in0=bd2[:], scalar1=0.5, scalar2=None,
                                       op0=ALU.mult), reads=("bd2",), writes=("bd2",))
    fw.op(v, lambda e: e.tensor_scalar(out=ecol[:], in0=ecol[:], scalar1=1.0, scalar2=None,
                                       op0=ALU.add), reads=("ecol",), writes=("ecol",))
    fw.op(sc, lambda e: e.activation(out=acol[:], in_=alog[:], func=AF.Exp),
          reads=("alog",), writes=("acol",))
    fw.op(v, lambda e: e.tensor_scalar(out=acol[:], in0=acol[:], scalar1=-1.0, scalar2=None,
                                       op0=ALU.mult), reads=("acol",), writes=("acol",))

    ytags = [f"y{c}" for c in range(DC)]
    ybftags = [f"ybf{c}" for c in range(DC)]
    sqbtags = ["sqb0", "sqb1"]
    rowtags = ["mrow", "vrow", "srow", "rstd"]

    for i in range(NTILES):
        b, t0 = tile_bt(i)
        u = i % 2
        for c in range(DC):
            fw.dma(sy, lambda e, b=b, c=c, t0=t0, u=u:
                   e.dma_start(out=xf[u][c][:], in_=DR["x_in"][b, c, :, t0:t0 + T]),
                   writes=(f"xf{u}_{c}",))
            fw.dma(gp, lambda e, b=b, c=c, t0=t0, u=u:
                   e.dma_start(out=xbt[u][c][:], in_=DR["x_in"][b, c, :, t0:t0 + T]),
                   writes=(f"xb{u}_{c}",))
        if t0 == 0:
            fw.op(v, lambda e: e.memset(carry[:], 0.0), writes=("carry",))
        for m in range(DC):
            for k in range(DC):
                fw.op(te, lambda e, m=m, k=k, u=u:
                      e.matmul(out=pU[m % 2][:], lhsT=wd[k][:, m * 128:(m + 1) * 128],
                               rhs=xbt[u][k][:], start=(k == 0), stop=(k == DC - 1)),
                      reads=(f"wd{k}", f"xb{u}_{k}"), writes=(f"pU{m % 2}",))
            fw.op(sc, lambda e, m=m: e.activation(out=tnh[m][:], in_=pU[m % 2][:], func=AF.Tanh,
                                                  scale=0.5, bias=bd2[:, m:m + 1]),
                  reads=(f"pU{m % 2}", "bd2"), writes=(f"tnh{m}",))
        for k in range(DC):
            fw.op(te, lambda e, k=k, u=u:
                  e.matmul(out=bankB[32:48, :], lhsT=wb[k][:], rhs=xbt[u][k][:],
                           start=(k == 0), stop=(k == DC - 1)),
                  reads=(f"wb{k}", f"xb{u}_{k}"), writes=("bankB",))
        fw.op(sc, lambda e: e.activation(out=bm[:], in_=bankB[32:48, :], func=AF.Identity,
                                         bias=bB[:, 0:1]),
              reads=("bankB", "bB"), writes=("bm",))
        for k in range(DC):
            fw.op(te, lambda e, k=k, u=u:
                  e.matmul(out=bankA[32:33, :], lhsT=ones_bf[:], rhs=xbt[u][k][:],
                           start=(k == 0), stop=(k == DC - 1)),
                  reads=(f"xb{u}_{k}", "ones"), writes=("bankA",))
        for k in range(DC):
            fw.op(te, lambda e, k=k:
                  e.matmul(out=bankA[0:1, :], lhsT=ones_bf[:], rhs=tnh[k][:],
                           start=(k == 0), stop=(k == DC - 1)),
                  reads=(f"tnh{k}", "ones"), writes=("bankA",))
        fw.op(v, lambda e: e.tensor_scalar(out=delta[:], in0=bankA[0:1, :],
                                           scalar1=0.0495 / D, scalar2=0.0505,
                                           op0=ALU.mult, op1=ALU.add),
              reads=("bankA",), writes=("delta",))
        fw.op(v, lambda e: e.tensor_tensor(out=grow[:], in0=delta[:], in1=bankA[32:33, :],
                                           op=ALU.mult),
              reads=("delta", "bankA"), writes=("grow",))
        fw.op(te, lambda e: e.matmul(out=bankB[64:80, :], lhsT=ones16[:], rhs=delta[:],
                                     start=True, stop=True),
              reads=("delta", "ones"), writes=("bankB",))
        fw.op(sc, lambda e: e.activation(out=abar[:], in_=bankB[64:80, :], func=AF.Exp,
                                         scale=acol[:, 0:1]),
              reads=("bankB", "acol"), writes=("abar",))
        fw.op(te, lambda e: e.matmul(out=bankB[64:80, :], lhsT=c16[:], rhs=grow[:],
                                     start=True, stop=True),
              reads=("grow", "ones"), writes=("bankB",))
        fw.op(v, lambda e: e.tensor_tensor(out=bu[:], in0=bm[:], in1=bankB[64:80, :], op=ALU.mult),
              reads=("bm", "bankB"), writes=("bu",))
        fw.op(v, lambda e: e.tensor_tensor_scan(out=states[:], data0=abar[:], data1=bu[:],
                                                initial=carry[:, 0:1], op0=ALU.mult, op1=ALU.add),
              reads=("abar", "bu", "carry"), writes=("states",))
        fw.op(v, lambda e: e.tensor_copy(out=carry[:], in_=states[:, T - 1:T]),
              reads=("states",), writes=("carry",))
        fw.op(v, lambda e: e.tensor_copy(out=st_bf[:], in_=states[:]),
              reads=("states",), writes=("stbf",))
        for c in range(DC):
            fw.op(te, lambda e, c=c: e.matmul(out=pO[c % 2][:], lhsT=wo[:, c * 128:(c + 1) * 128],
                                              rhs=st_bf[:], start=True, stop=True),
                  reads=("wo", "stbf"), writes=(f"pO{c % 2}",))
            fw.op(sc, lambda e, c=c: e.activation(out=y[c][:], in_=pO[c % 2][:], func=AF.Identity,
                                                  bias=bo[:, c:c + 1]),
                  reads=(f"pO{c % 2}", "bo"), writes=(f"y{c}",))
            fw.op(v, lambda e, c=c, u=u: e.scalar_tensor_tensor(
                out=y[c][:], in0=xf[u][c][:], scalar=ecol[:, c:c + 1], in1=y[c][:],
                op0=ALU.mult, op1=ALU.add),
                reads=(f"xf{u}_{c}", "ecol", f"y{c}"), writes=(f"y{c}",))
        emit_postln(fw, y, ytags, ybf, ybftags, sqb, sqbtags, rows, rowtags,
                    bankA, "bankA", pMB, pRB, ones_bf, ones128r)
        for c in range(DC):
            fw.dma(sy, lambda e, b=b, c=c, t0=t0: e.dma_start(
                out=DR["x1_s"][b, c, :, t0:t0 + T], in_=y[c][:]),
                reads=(f"y{c}",))
            fw.op(v, lambda e, c=c: e.tensor_reduce(out=mxt[:], in_=y[c][:], axis=AX.X,
                                                    op=ALU.max, apply_absolute_value=True),
                  reads=(f"y{c}",), writes=("mxt",))
            fw.op(v, lambda e, c=c: e.tensor_tensor(out=mx[:, c:c + 1], in0=mx[:, c:c + 1],
                                                    in1=mxt[:], op=ALU.max),
                  reads=("mxt", "mx"), writes=("mx",))
    fw.op(v, lambda e: e.tensor_reduce(out=mxt[:], in_=mx[:], axis=AX.X, op=ALU.max),
          reads=("mx",), writes=("mxt",))
    fw.op(te, lambda e: e.transpose(out=bankA[0:1, 0:128], in_=mxt[:, 0:1], identity=ident[:]),
          reads=("mxt", "ident"), writes=("bankA",))
    fw.op(v, lambda e: e.tensor_copy(out=lrow[:], in_=bankA[0:1, 0:128]),
          reads=("bankA",), writes=("lrow",))
    fw.op(v, lambda e: e.tensor_reduce(out=lmax[:], in_=lrow[:], axis=AX.X, op=ALU.max),
          reads=("lrow",), writes=("lmax",))
    fw.dma(sy, lambda e: e.dma_start(out=DR["lmax_out"], in_=lmax[:]), reads=("lmax",))
    fw.flush(block)


def emit_cc(fw, block, DR):
    fw.op("gpsimd", lambda e: e.collective_compute(
        "AllReduce", ALU.max, replica_groups=[list(range(NCORES))],
        ins=[DR["lmax_out"]], outs=[DR["maxes"]]), inc_val=1, dma=True)
    fw.flush(block)



def emit_p1(fw, block, DR, es, n_max):
    nc = fw.nc
    sb = lambda n, s, dt=F32: es.enter_context(nc.sbuf_tensor("a1" + n, s, dt))
    psm = lambda n, s: es.enter_context(nc.psum_tensor("a1" + n, s, F32))
    v, sc, te, sy, gp = "vector", "scalar", "tensor", "sync", "gpsimd"

    w1f = [sb(f"w1f{k}", [128, D]) for k in range(DC)]
    qw = [sb(f"qw{k}", [128, D], BF16) for k in range(DC)]
    absw = sb("absw", [128, D])
    sgn = sb("sgn", [128, D])
    b1 = sb("b1", [128, DC])
    ident = sb("ident", [128, 128])
    ones_bf = sb("onesbf", [128, 1], BF16)
    ones128r = sb("o128r", [1, 128])
    mxs = sb("mxs", [1, n_max])
    gmax = sb("gmax", [1, 1])
    isc = sb("isc", [1, 1])
    wsis = sb("wsis", [1, 1])
    hws = sb("hws", [1, 1])
    wcol = sb("wcol", [128, DC])
    wcol1 = sb("wcol1", [128, 1])
    wrow = sb("wrow", [1, 128])
    ws = sb("ws", [1, 1])
    inv_col = sb("invcol", [128, 1])
    wsis_col = sb("wsiscol", [128, 1])
    hws_col = sb("hwscol", [128, 1])
    xf = [[sb(f"xf{u}_{k}", [128, T]) for k in range(DC)] for u in range(2)]
    qx = [sb(f"qx{k}", [128, T], BF16) for k in range(DC)]
    qt = [sb("qt0", [128, T]), sb("qt1", [128, T])]
    magic_col = sb("magic", [128, 1])
    y = [sb(f"y{k}", [128, T]) for k in range(DC)]
    ybf = [sb(f"ybf{k}", [128, T], BF16) for k in range(DC)]
    sqb = [sb(f"sqb{k}", [128, T], BF16) for k in range(2)]
    rows = [sb("mrow", [1, T]), sb("vrow", [1, T]), sb("srow", [1, T]), sb("rstd", [1, T])]

    pQ = [psm("pQ0", [128, T]), psm("pQ1", [128, T])]
    pMB = psm("pMB", [128, T])
    pRB = psm("pRB", [128, T])
    bankA = psm("bankA", [128, T])
    pCol = psm("pCol", [128, 512])

    for k in range(DC):
        fw.dma(sy, lambda e, k=k: e.dma_start(out=w1f[k][:], in_=DR["p1_w"][k]), writes=(f"w1f{k}",))
    fw.dma(sy, lambda e: e.dma_start(out=b1[:], in_=DR["p1_b"]), writes=("b1",))
    fw.dma(sy, lambda e: e.dma_start(out=ident[:], in_=DR["ident"]), writes=("ident",))
    fw.dma(sy, lambda e: e.dma_start(out=mxs[:], in_=DR["maxes"]), writes=("mxs",))
    fw.op(v, lambda e: e.memset(ones_bf[:], 1.0), writes=("ones",))
    fw.op(v, lambda e: e.memset(ones128r[:], 1.0), writes=("ones",))
    fw.op(v, lambda e: e.memset(magic_col[:], MAGIC), writes=("ones",))

    for k in range(DC):
        fw.op(v, lambda e, k=k: e.tensor_reduce(out=wcol[:, k:k + 1], in_=w1f[k][:], axis=AX.X,
                                                op=ALU.add, apply_absolute_value=True),
              reads=(f"w1f{k}",), writes=("wcol",))
    fw.op(v, lambda e: e.tensor_reduce(out=wcol1[:], in_=wcol[:], axis=AX.X, op=ALU.add),
          reads=("wcol",), writes=("wcol1",))
    fw.op(te, lambda e: e.transpose(out=bankA[0:1, 0:128], in_=wcol1[:, 0:1], identity=ident[:]),
          reads=("wcol1", "ident"), writes=("bankA",))
    fw.op(v, lambda e: e.tensor_copy(out=wrow[:], in_=bankA[0:1, 0:128]),
          reads=("bankA",), writes=("wrow",))
    fw.op(v, lambda e: e.tensor_reduce(out=ws[:], in_=wrow[:], axis=AX.X, op=ALU.add),
          reads=("wrow",), writes=("ws",))
    fw.op(v, lambda e: e.tensor_scalar(out=ws[:], in0=ws[:], scalar1=1.0 / (D * D), scalar2=None,
                                       op0=ALU.mult), reads=("ws",), writes=("ws",))
    fw.op(v, lambda e: e.tensor_scalar(out=hws[:], in0=ws[:], scalar1=0.5, scalar2=None,
                                       op0=ALU.mult), reads=("ws",), writes=("hws",))
    fw.op(v, lambda e: e.tensor_reduce(out=gmax[:], in_=mxs[:], axis=AX.X, op=ALU.max),
          reads=("mxs",), writes=("gmax",))
    fw.op(v, lambda e: e.tensor_scalar(out=isc[:], in0=gmax[:], scalar1=1.0 / 127, scalar2=None,
                                       op0=ALU.mult), reads=("gmax",), writes=("isc",))
    fw.op(v, lambda e: e.tensor_tensor(out=wsis[:], in0=ws[:], in1=isc[:], op=ALU.mult),
          reads=("ws", "isc"), writes=("wsis",))
    fw.op(v, lambda e: e.reciprocal(out=gmax[:], in_=isc[:]), reads=("isc",), writes=("ginv", "gmax"))
    fw.op(te, lambda e: e.matmul(out=pCol[:, 0:1], lhsT=ones128r[:], rhs=gmax[:],
                                 start=True, stop=True), reads=("ginv", "ones"), writes=("pCol",))
    fw.op(v, lambda e: e.tensor_copy(out=inv_col[:], in_=pCol[:, 0:1]),
          reads=("pCol",), writes=("invcol",))
    fw.op(te, lambda e: e.matmul(out=pCol[:, 1:2], lhsT=ones128r[:], rhs=wsis[:],
                                 start=True, stop=True), reads=("wsis", "ones"), writes=("pCol",))
    fw.op(v, lambda e: e.tensor_copy(out=wsis_col[:], in_=pCol[:, 1:2]),
          reads=("pCol",), writes=("wsiscol",))
    fw.op(te, lambda e: e.matmul(out=pCol[:, 2:3], lhsT=ones128r[:], rhs=hws[:],
                                 start=True, stop=True), reads=("hws", "ones"), writes=("pCol",))
    fw.op(v, lambda e: e.tensor_copy(out=hws_col[:], in_=pCol[:, 2:3]),
          reads=("pCol",), writes=("hwscol",))
    for k in range(DC):
        fw.op(sc, lambda e, k=k: e.activation(out=absw[:], in_=w1f[k][:], func=AF.Abs),
              reads=(f"w1f{k}",), writes=("absw",))
        fw.op(sc, lambda e, k=k: e.activation(out=sgn[:], in_=w1f[k][:], func=AF.Sign),
              reads=(f"w1f{k}",), writes=("sgn",))
        fw.op(v, lambda e: e.tensor_scalar(out=absw[:], in0=absw[:], scalar1=hws_col[:, 0:1],
                                           scalar2=None, op0=ALU.is_gt),
              reads=("absw", "hwscol"), writes=("absw",))
        fw.op(v, lambda e, k=k: e.tensor_tensor(out=qw[k][:], in0=sgn[:], in1=absw[:], op=ALU.mult),
              reads=("sgn", "absw"), writes=(f"qw{k}",))

    ytags = [f"y{c}" for c in range(DC)]
    ybftags = [f"ybf{c}" for c in range(DC)]
    sqbtags = ["sqb0", "sqb1"]
    rowtags = ["mrow", "vrow", "srow", "rstd"]

    for i in range(NTILES):
        b, t0 = tile_bt(i)
        u = i % 2
        lq = sy if i % 2 == 0 else gp
        for c in range(DC):
            fw.dma(lq, lambda e, b=b, c=c, t0=t0, u=u:
                   e.dma_start(out=xf[u][c][:], in_=DR["x1_s"][b, c, :, t0:t0 + T]),
                   writes=(f"xf{u}_{c}",))
        for c in range(DC):
            qb = c % 2
            fw.op(sc, lambda e, c=c, u=u, qb=qb: e.activation(
                out=qt[qb][:], in_=xf[u][c][:], func=AF.Identity,
                scale=inv_col[:, 0:1], bias=magic_col[:, 0:1]),
                reads=(f"xf{u}_{c}", "invcol", "ones"), writes=(f"qt{qb}",))
            fw.op(v, lambda e, qb=qb: e.tensor_scalar(
                out=qt[qb][:], in0=qt[qb][:], scalar1=MAGIC, scalar2=127.0,
                op0=ALU.subtract, op1=ALU.min),
                reads=(f"qt{qb}",), writes=(f"qt{qb}",))
            fw.op(v, lambda e, c=c, qb=qb: e.tensor_scalar(
                out=qx[c][:], in0=qt[qb][:], scalar1=-128.0, scalar2=None, op0=ALU.max),
                reads=(f"qt{qb}",), writes=(f"qx{c}",))
        for m in range(DC):
            for k in range(DC):
                fw.op(te, lambda e, m=m, k=k: e.matmul(
                    out=pQ[m % 2][:], lhsT=qw[k][:, m * 128:(m + 1) * 128], rhs=qx[k][:],
                    start=(k == 0), stop=(k == DC - 1)),
                    reads=(f"qw{k}", f"qx{k}"), writes=(f"pQ{m % 2}",))
            fw.op(sc, lambda e, m=m: e.activation(out=y[m][:], in_=pQ[m % 2][:], func=AF.Identity,
                                                  scale=wsis_col[:, 0:1], bias=b1[:, m:m + 1]),
                  reads=(f"pQ{m % 2}", "wsiscol", "b1"), writes=(f"y{m}",))
            fw.op(gp, lambda e, m=m, u=u: e.tensor_tensor(out=y[m][:], in0=y[m][:],
                                                          in1=xf[u][m][:], op=ALU.add),
                  reads=(f"y{m}", f"xf{u}_{m}"), writes=(f"y{m}",))
        emit_postln(fw, y, ytags, ybf, ybftags, sqb, sqbtags, rows, rowtags,
                    bankA, "bankA", pMB, pRB, ones_bf, ones128r)
        for c in range(DC):
            fw.dma(sy, lambda e, b=b, c=c, t0=t0: e.dma_start(
                out=DR["x2_s"][b, c, :, t0:t0 + T], in_=y[c][:]),
                reads=(f"y{c}",))
    fw.flush(block)



def emit_p2a(fw, block, DR, es, flush=True):
    nc = fw.nc
    sb = lambda n, s, dt=F32: es.enter_context(nc.sbuf_tensor("a2" + n, s, dt))
    psm = lambda n, s: es.enter_context(nc.psum_tensor("a2" + n, s, F32))
    v, sc, te, sy, gp = "vector", "scalar", "tensor", "sync", "gpsimd"

    w1 = [sb(f"w1{k}", [128, D], BF16) for k in range(DC)]
    b1 = sb("b1", [128, DC])
    xb2 = [[sb(f"xb{u}_{k}", [128, T], BF16) for k in range(DC)] for u in range(2)]
    pb = [sb(f"pb{k}", [128, T], BF16) for k in range(DC)]
    tcol = sb("tcol", [128, 1])
    tot = sb("tot", [128, 2 * DC])
    pP = [psm("pP0", [128, T]), psm("pP1", [128, T])]

    for k in range(DC):
        fw.dma(gp, lambda e, k=k: e.dma_start(out=w1[k][:], in_=DR["p2_w1"][k]), writes=(f"w1{k}",))
    fw.dma(sy, lambda e: e.dma_start(out=b1[:], in_=DR["p2_b1"]), writes=("b1",))
    fw.op(v, lambda e: e.memset(tot[:], 0.0), writes=("tot",))

    for i in range(NTILES):
        b, t0 = tile_bt(i)
        u = i % 2
        for c in range(DC):
            fw.dma(gp, lambda e, b=b, c=c, t0=t0, u=u:
                   e.dma_start(out=xb2[u][c][:], in_=DR["x2_s"][b, c, :, t0:t0 + T]),
                   writes=(f"xb{u}_{c}",))
        for m in range(DC):
            for k in range(DC):
                fw.op(te, lambda e, m=m, k=k, u=u: e.matmul(
                    out=pP[m % 2][:], lhsT=w1[k][:, m * 128:(m + 1) * 128], rhs=xb2[u][k][:],
                    start=(k == 0), stop=(k == DC - 1)),
                    reads=(f"w1{k}", f"xb{u}_{k}"), writes=(f"pP{m % 2}",))
            fw.op(sc, lambda e, m=m: e.activation(out=pb[m][:], in_=pP[m % 2][:], func=AF.Identity,
                                                  bias=b1[:, m:m + 1], accum_out=tcol[:]),
                  reads=(f"pP{m % 2}", "b1"), writes=(f"pb{m}", "tcol"))
            j = b * DC + m
            fw.op(v, lambda e, j=j: e.tensor_tensor(out=tot[:, j:j + 1], in0=tot[:, j:j + 1],
                                                    in1=tcol[:], op=ALU.add),
                  reads=("tcol", "tot"), writes=("tot",))
            fw.dma(sy, lambda e, b=b, m=m, t0=t0: e.dma_start(
                out=DR["p_s"][b, m, :, t0:t0 + T], in_=pb[m][:]),
                reads=(f"pb{m}",), writes=(f"psd{b}_{m}",))
    fw.dma(sy, lambda e: e.dma_start(out=DR["tot_s"], in_=tot[:]), reads=("tot",),
           writes=("totd",))
    if flush:
        fw.flush(block)



def emit_p2b(fw, block, DR, es):
    nc = fw.nc
    sb = lambda n, s, dt=F32: es.enter_context(nc.sbuf_tensor("a3" + n, s, dt))
    psm = lambda n, s: es.enter_context(nc.psum_tensor("a3" + n, s, F32))
    v, sc, te, sy, gp = "vector", "scalar", "tensor", "sync", "gpsimd"

    w2 = [sb(f"w2{k}", [128, D], BF16) for k in range(DC)]
    b2 = sb("b2", [128, DC])
    mw = [sb(f"mw{k}", [128, MEM + 1]) for k in range(DC)]
    taps = [sb(f"tp{k}", [128, MEM]) for k in range(DC)]
    tot = sb("tot", [128, 2 * DC])
    ct = sb("ct", [128, 2 * DC])
    ident = sb("ident", [128, 128])
    diag = sb("diag", [128, 128 * MEM], BF16)
    pfull = [sb(f"pfl{u}", [128, MEM - 1 + S], BF16) for u in range(2)]
    bstage = [sb(f"bst{u}", [128, T], BF16) for u in range(2)]
    xf = [[sb(f"xf{u}_{k}", [128, T]) for k in range(DC)] for u in range(2)]
    bnd = [[sb(f"bnd{u}_{k}", [128, T], BF16) for k in range(DC)] for u in range(2)]
    y = [sb(f"y{k}", [128, T]) for k in range(DC)]
    ybf = [sb(f"ybf{k}", [128, T], BF16) for k in range(DC)]
    sqb = [sb(f"sqb{k}", [128, T], BF16) for k in range(2)]
    ones_bf = sb("onesbf", [128, 1], BF16)
    ones128r = sb("o128r", [1, 128])
    rows = [sb("mrow", [1, T]), sb("vrow", [1, T]), sb("srow", [1, T]), sb("rstd", [1, T])]
    pB = [psm("pB0", [128, T]), psm("pB1", [128, T])]
    pW = [psm("pW0", [128, T]), psm("pW1", [128, T])]
    pMB = psm("pMB", [128, T])
    pRB = pMB
    bankA = psm("bankA", [128, T])

    for k in range(DC):
        fw.dma(gp, lambda e, k=k: e.dma_start(out=w2[k][:], in_=DR["p2_w2"][k]), writes=(f"w2{k}",))
        fw.dma(sy, lambda e, k=k: e.dma_start(out=mw[k][:], in_=DR["p2_mw"][k]), writes=(f"mw{k}",))
    fw.dma(sy, lambda e: e.dma_start(out=b2[:], in_=DR["p2_b2"]), writes=("b2",))
    fw.dma(sy, lambda e: e.dma_start(out=tot[:], in_=DR["tot_s"]), reads=("totd",), writes=("tot",))
    fw.dma(sy, lambda e: e.dma_start(out=ident[:], in_=DR["ident"]), writes=("ident",))
    fw.op(v, lambda e: e.memset(ones_bf[:], 1.0), writes=("ones",))
    fw.op(v, lambda e: e.memset(ones128r[:], 1.0), writes=("ones",))
    for k in range(DC):
        fw.op(v, lambda e, k=k: e.tensor_scalar(out=taps[k][:], in0=mw[k][:, 1:MEM + 1],
                                                scalar1=mw[k][:, 0:1], scalar2=None,
                                                op0=ALU.subtract),
              reads=(f"mw{k}",), writes=(f"tp{k}",))
    for b in range(NSEQ):
        for k in range(DC):
            j = b * DC + k
            fw.op(v, lambda e, j=j, k=k: e.tensor_tensor(
                out=ct[:, j:j + 1], in0=mw[k][:, 0:1], in1=tot[:, j:j + 1], op=ALU.mult),
                reads=(f"mw{k}", "tot"), writes=("ct",))

    for u in range(2):
        fw.op(v, lambda e, u=u: e.memset(pfull[u][:, 0:MEM - 1], 0.0), writes=(f"pfl{u}",))
    for b in range(NSEQ):
        for c in range(DC):
            for j in range(MEM):
                fw.op(v, lambda e, c=c, j=j: e.tensor_scalar(
                    out=diag[:, j * 128:(j + 1) * 128], in0=ident[:],
                    scalar1=taps[c][:, j:j + 1], scalar2=None, op0=ALU.mult),
                    reads=("ident", f"tp{c}"), writes=("diag",))
            u = (b * DC + c) % 2
            fw.dma(sy, lambda e, b=b, c=c, u=u: e.dma_start(
                out=pfull[u][:, MEM - 1:], in_=DR["p_s"][b, c, :, :]),
                reads=(f"psd{b}_{c}",), writes=(f"pfl{u}",))
            j0 = b * DC + c
            for tp in range(NT_SEQ // 2):
                for j in range(MEM):
                    for ti in range(2):
                        t8 = tp * 2 + ti
                        fw.op(te, lambda e, u=u, t8=t8, j=j, ti=ti: e.matmul(
                            out=pB[ti][:], lhsT=diag[:, j * 128:(j + 1) * 128],
                            rhs=pfull[u][:, MEM - 1 + t8 * T - j:MEM - 1 + t8 * T - j + T],
                            start=(j == 0), stop=(j == MEM - 1)),
                            reads=("diag", f"pfl{u}"), writes=(f"pB{ti}",))
                for ti in range(2):
                    t8 = tp * 2 + ti
                    fw.op(v, lambda e, ti=ti, j0=j0: e.tensor_scalar(
                        out=bstage[ti][:], in0=pB[ti][:],
                        scalar1=ct[:, j0:j0 + 1], scalar2=None, op0=ALU.add),
                        reads=(f"pB{ti}", "ct"), writes=(f"bst{ti}",))
                    fw.dma(sy, lambda e, b=b, c=c, t8=t8, ti=ti: e.dma_start(
                        out=DR["band_s"][b, c, :, t8 * T:(t8 + 1) * T], in_=bstage[ti][:]),
                        reads=(f"bst{ti}",), writes=(f"bdd{b}_{c}",))

    ytags = [f"y{c}" for c in range(DC)]
    ybftags = [f"ybf{c}" for c in range(DC)]
    sqbtags = ["sqb0", "sqb1"]
    rowtags = ["mrow", "vrow", "srow", "rstd"]

    for i in range(NTILES):
        b, t0 = tile_bt(i)
        u = i % 2
        lq = sy if i % 2 == 0 else gp
        for c in range(DC):
            fw.dma(lq, lambda e, b=b, c=c, t0=t0, u=u: e.dma_start(
                out=bnd[u][c][:], in_=DR["band_s"][b, c, :, t0:t0 + T]),
                reads=(f"bdd{b}_{c}",), writes=(f"bnd{u}_{c}",))
            fw.dma(lq, lambda e, b=b, c=c, t0=t0, u=u: e.dma_start(
                out=xf[u][c][:], in_=DR["x2_s"][b, c, :, t0:t0 + T]),
                writes=(f"xf{u}_{c}",))
        for m in range(DC):
            for k in range(DC):
                fw.op(te, lambda e, m=m, k=k, u=u: e.matmul(
                    out=pW[m % 2][:], lhsT=w2[k][:, m * 128:(m + 1) * 128], rhs=bnd[u][k][:],
                    start=(k == 0), stop=(k == DC - 1)),
                    reads=(f"w2{k}", f"bnd{u}_{k}"), writes=(f"pW{m % 2}",))
            fw.op(sc, lambda e, m=m: e.activation(out=y[m][:], in_=pW[m % 2][:], func=AF.Identity,
                                                  bias=b2[:, m:m + 1]),
                  reads=(f"pW{m % 2}", "b2"), writes=(f"y{m}",))
            fw.op(gp, lambda e, m=m, u=u: e.tensor_tensor(out=y[m][:], in0=y[m][:],
                                                          in1=xf[u][m][:], op=ALU.add),
                  reads=(f"y{m}", f"xf{u}_{m}"), writes=(f"y{m}",))
        emit_postln(fw, y, ytags, ybf, ybftags, sqb, sqbtags, rows, rowtags,
                    bankA, "bankA", pMB, pRB, ones_bf, ones128r,
                    psMBtag="psMB", psRBtag="psMB")
        for c in range(DC):
            fw.dma(sy, lambda e, b=b, c=c, t0=t0: e.dma_start(
                out=DR["x3_s"][b, c, :, t0:t0 + T], in_=y[c][:]),
                reads=(f"y{c}",))
    fw.flush(block)



def emit_p3(fw, block, DR, es):
    nc = fw.nc
    v, sc, te, sy, gp = "vector", "scalar", "tensor", "sync", "gpsimd"

    with ExitStack() as ges:
        gsb = lambda n, s, dt=F32: ges.enter_context(nc.sbuf_tensor("g_" + n, s, dt))
        gps = lambda n, s: ges.enter_context(nc.psum_tensor("g_" + n, s, F32))
        wr = [gsb(f"wr{k}", [128, NEXP]) for k in range(DC)]
        xb3 = [[gsb(f"xb{u}_{k}", [128, T]) for k in range(DC)] for u in range(2)]
        et = gsb("et", [128, NEXP])
        msk = gsb("msk", [128, NEXP])
        em1 = gsb("em1", [128, 1])
        nem1 = gsb("nem1", [128, 1])
        em2 = gsb("em2", [128, 1])
        den = gsb("den", [128, 1])
        gt = gsb("gt", [128, NEXP])
        gtb = gsb("gtb", [8, 128], BF16)
        ident = gsb("ident", [128, 128])
        pLf = gps("pLf", [128, 512])
        pTf = gps("pTf", [128, 512])
        pL = pLf[:, 0:NEXP]
        pT = pTf[0:8, 0:128]

        fw.dma(sy, lambda e: e.dma_start(out=ident[:], in_=DR["ident"]), writes=("ident",))
        for k in range(DC):
            fw.dma(sy, lambda e, k=k: e.dma_start(out=wr[k][:], in_=DR["p3_wr"][k]),
                   writes=(f"wr{k}",))
        for i in range(NTILES):
            b, t0 = tile_bt(i)
            u = i % 2
            for c in range(DC):
                fw.dma(sy, lambda e, b=b, c=c, t0=t0, u=u: e.dma_start(
                    out=xb3[u][c][:], in_=DR["x3_s"][b, c, :, t0:t0 + T]),
                    writes=(f"xb{u}_{c}",))
            for g in range(T // 128):
                for k in range(DC):
                    fw.op(te, lambda e, g=g, k=k, u=u: e.matmul(
                        out=pL, lhsT=xb3[u][k][:, g * 128:(g + 1) * 128], rhs=wr[k][:],
                        start=(k == 0), stop=(k == DC - 1)),
                        reads=(f"xb{u}_{k}", f"wr{k}"), writes=("pL",))
                fw.op(v, lambda e: e.tensor_reduce(out=em1[:], in_=pL, axis=AX.X, op=ALU.max),
                      reads=("pL",), writes=("em1",))
                fw.op(v, lambda e: e.tensor_scalar(out=nem1[:], in0=em1[:], scalar1=-1.0,
                                                   scalar2=None, op0=ALU.mult),
                      reads=("em1",), writes=("nem1",))
                fw.op(sc, lambda e: e.activation(out=et[:], in_=pL, func=AF.Exp,
                                                 bias=nem1[:, 0:1]),
                      reads=("pL", "nem1"), writes=("et",))
                fw.op(v, lambda e: e.tensor_reduce(out=em1[:], in_=et[:], axis=AX.X, op=ALU.max),
                      reads=("et",), writes=("em1",))
                fw.op(v, lambda e: e.tensor_scalar(out=msk[:], in0=et[:], scalar1=em1[:, 0:1],
                                                   scalar2=None, op0=ALU.is_lt),
                      reads=("et", "em1"), writes=("msk",))
                fw.op(v, lambda e: e.tensor_tensor(out=msk[:], in0=msk[:], in1=et[:], op=ALU.mult),
                      reads=("msk", "et"), writes=("msk",))
                fw.op(v, lambda e: e.tensor_reduce(out=em2[:], in_=msk[:], axis=AX.X, op=ALU.max),
                      reads=("msk",), writes=("em2",))
                fw.op(v, lambda e: e.tensor_scalar(out=gt[:], in0=et[:], scalar1=em2[:, 0:1],
                                                   scalar2=None, op0=ALU.is_ge),
                      reads=("et", "em2"), writes=("gt",))
                fw.op(v, lambda e: e.tensor_tensor(out=gt[:], in0=gt[:], in1=et[:], op=ALU.mult),
                      reads=("gt", "et"), writes=("gt",))
                fw.op(v, lambda e: e.tensor_tensor(out=den[:], in0=em1[:], in1=em2[:], op=ALU.add),
                      reads=("em1", "em2"), writes=("den",))
                fw.op(v, lambda e: e.reciprocal(out=den[:], in_=den[:]),
                      reads=("den",), writes=("den",))
                fw.op(v, lambda e: e.tensor_scalar(out=gt[:], in0=gt[:], scalar1=den[:, 0:1],
                                                   scalar2=None, op0=ALU.mult),
                      reads=("gt", "den"), writes=("gt",))
                fw.op(te, lambda e: e.transpose(out=pT, in_=gt[:], identity=ident[:]),
                      reads=("gt", "ident"), writes=("pT",))
                fw.op(v, lambda e: e.tensor_copy(out=gtb[:], in_=pT),
                      reads=("pT",), writes=("gtb",))
                fw.dma(sy, lambda e, i=i, g=g: e.dma_start(
                    out=DR["gates_s"][:, i * T + g * 128:i * T + (g + 1) * 128], in_=gtb[:]),
                    reads=("gtb",))
        fw.flush(block)

    with ExitStack() as bes:
        bsb = lambda n, s, dt=F32: bes.enter_context(nc.sbuf_tensor("e_" + n, s, dt))
        bps = lambda n, s: bes.enter_context(nc.psum_tensor("e_" + n, s, F32))
        xb3 = [bsb(f"xb{k}", [128, TB], BF16) for k in range(DC)]
        acc = [bsb(f"acc{k}", [128, TB]) for k in range(DC)]
        hid = [bsb(f"hid{k}", [128, TB], BF16) for k in range(HC)]
        wfcT = [bsb(f"wfcT{u}", [128, DC * 1024], BF16) for u in range(3)]
        wpjT = [bsb(f"wpjT{u}", [128, HC * 256], BF16) for u in range(2)]
        bfc = [bsb(f"bfc{u}", [128, HC]) for u in range(2)]
        bpj = [bsb(f"bpj{u}", [128, DC]) for u in range(2)]
        grow = [bsb(f"grow{u}", [1, TB], BF16) for u in range(2)]
        gbc = bsb("gbc", [128, TB], BF16)
        tmp = bsb("tmp", [128, T])
        ones_bf = bsb("onesbf", [128, 1], BF16)
        ones128r = bsb("o128r", [1, 128])
        ones128r_bf = bsb("o128rb", [1, 128], BF16)
        pFC = [bps(f"pFC{q}", [128, T]) for q in range(4)]
        pPJ = [bps(f"pPJ{q}", [128, T]) for q in range(4)]
        xf = [hid[c][:].bitcast(F32) for c in range(DC)]
        ybf = [hid[DC + c] for c in range(DC)]
        sqb = [hid[2 * DC], hid[2 * DC + 1]]
        rows = [hid[2 * DC + 2 + j][:].bitcast(F32)[0:1, 0:T] for j in range(4)]
        pMB = pPJ[0]
        pRB = pPJ[1]
        bankA = pPJ[2]

        fw.op(v, lambda e: e.memset(ones_bf[:], 1.0), writes=("ones",))
        fw.op(v, lambda e: e.memset(ones128r[:], 1.0), writes=("ones",))
        fw.op(v, lambda e: e.memset(ones128r_bf[:], 1.0), writes=("ones",))

        wfc_tags = tuple(f"wfcT{u}" for u in range(3))
        wpj_tags = tuple(f"wpjT{u}" for u in range(2))

        ytags = [f"acc{c}" for c in range(DC)]
        ybftags = [f"hid{DC + c}" for c in range(DC)]
        sqbtags = [f"hid{2 * DC}", f"hid{2 * DC + 1}"]
        rowtags = [f"hid{2 * DC + 2 + j}" for j in range(4)]

        mg_ctr = [0]
        pj_ctr = [0]

        for blk in range(NBLK):
            b = blk // BLK_PER_SEQ
            tb0 = (blk % BLK_PER_SEQ) * TB
            for k in range(DC):
                fw.dma(gp, lambda e, b=b, k=k, tb0=tb0: e.dma_start(
                    out=xb3[k][:], in_=DR["x3_s"][b, k, :, tb0:tb0 + TB]),
                    writes=(f"xb3{k}",))
            for eidx in range(NEXP + 1):
                gu = eidx % 2
                if eidx > 0:
                    fw.dma(sy, lambda e, eidx=eidx, blk=blk, gu=gu: e.dma_start(
                        out=grow[gu][:], in_=DR["gates_s"][eidx - 1:eidx, blk * TB:(blk + 1) * TB]),
                        writes=(f"grow{gu}",))
                for mg in range(4):
                    u = mg_ctr[0] % 3
                    mg_ctr[0] += 1
                    fw.dma(gp, lambda e, eidx=eidx, mg=mg, u=u: e.dma_start(
                        out=wfcT[u][:].rearrange("p (k m) -> p k m", k=DC),
                        in_=DR["p3_wfc"][eidx, :, :, mg * 1024:(mg + 1) * 1024].rearrange(
                            "k p m -> p k m")),
                        writes=(wfc_tags[u],))
                    if mg == 0:
                        fw.dma(gp, lambda e, eidx=eidx, gu=gu: e.dma_start(
                            out=bfc[gu][:], in_=DR["p3_bfc"][eidx]),
                            writes=(f"bfc{gu}",))
                        fw.dma(gp, lambda e, eidx=eidx, gu=gu: e.dma_start(
                            out=bpj[gu][:], in_=DR["p3_bpj"][eidx]),
                            writes=(f"bpj{gu}",))
                    for mi in range(DC):
                        m = mg * DC + mi
                        q = (mi % 2) * 2
                        for k in range(DC):
                            for t in range(NT_B):
                                fw.op(te, lambda e, u=u, k=k, mi=mi, t=t, q=q: e.matmul(
                                    out=pFC[q + t][:],
                                    lhsT=wfcT[u][:, k * 1024 + mi * 128:k * 1024 + (mi + 1) * 128],
                                    rhs=xb3[k][:, t * T:(t + 1) * T],
                                    start=(k == 0), stop=(k == DC - 1)),
                                    reads=(wfc_tags[u], f"xb3{k}"), writes=(f"pFC{q + t}",))
                        for t in range(NT_B):
                            fw.op(sc, lambda e, gu=gu, m=m, t=t, q=q: e.activation(
                                out=hid[m][:, t * T:(t + 1) * T], in_=pFC[q + t][:],
                                func=AF.Gelu_apprx_tanh, bias=bfc[gu][:, m:m + 1]),
                                reads=(f"pFC{q + t}", f"bfc{gu}"), writes=(f"hid{m}",))
                if eidx > 0:
                    for t in range(NT_B):
                        fw.op(te, lambda e, gu=gu, t=t: e.matmul(
                            out=pFC[t][:], lhsT=ones128r_bf[:], rhs=grow[gu][0:1, t * T:(t + 1) * T],
                            start=True, stop=True),
                            reads=(f"grow{gu}", "ones"), writes=(f"pFC{t}",))
                        fw.op(v, lambda e, t=t: e.tensor_copy(out=gbc[:, t * T:(t + 1) * T],
                                                              in_=pFC[t][:]),
                              reads=(f"pFC{t}",), writes=("gbc",))
                for mg in range(4):
                    u = pj_ctr[0] % 2
                    pj_ctr[0] += 1
                    fw.dma(gp, lambda e, eidx=eidx, mg=mg, u=u: e.dma_start(
                        out=wpjT[u][:].rearrange("p (k m) -> p k m", k=HC),
                        in_=DR["p3_wpj"][eidx, :, :, mg * 256:(mg + 1) * 256].rearrange(
                            "k p m -> p k m")),
                        writes=(wpj_tags[u],))
                    for mi in range(2):
                        m2 = mg * 2 + mi
                        q = (m2 % 2) * 2
                        for k in range(HC):
                            for t in range(NT_B):
                                fw.op(te, lambda e, u=u, k=k, mi=mi, t=t, q=q: e.matmul(
                                    out=pPJ[q + t][:],
                                    lhsT=wpjT[u][:, k * 256 + mi * 128:k * 256 + (mi + 1) * 128],
                                    rhs=hid[k][:, t * T:(t + 1) * T],
                                    start=(k == 0), stop=(k == HC - 1)),
                                    reads=(wpj_tags[u], f"hid{k}"), writes=(f"pPJ{q + t}",))
                        for t in range(NT_B):
                            if eidx == 0:
                                fw.op(sc, lambda e, gu=gu, m2=m2, t=t, q=q: e.activation(
                                    out=acc[m2][:, t * T:(t + 1) * T], in_=pPJ[q + t][:],
                                    func=AF.Identity, bias=bpj[gu][:, m2:m2 + 1]),
                                    reads=(f"pPJ{q + t}", f"bpj{gu}"), writes=(f"acc{m2}",))
                            else:
                                fw.op(sc, lambda e, gu=gu, m2=m2, t=t, q=q: e.activation(
                                    out=tmp[:], in_=pPJ[q + t][:], func=AF.Identity,
                                    bias=bpj[gu][:, m2:m2 + 1]),
                                    reads=(f"pPJ{q + t}", f"bpj{gu}"), writes=("tmp",))
                                fw.op(v, lambda e, t=t: e.tensor_tensor(
                                    out=tmp[:], in0=tmp[:], in1=gbc[:, t * T:(t + 1) * T],
                                    op=ALU.mult),
                                    reads=("tmp", "gbc"), writes=("tmp",))
                                fw.op(v, lambda e, m2=m2, t=t: e.tensor_tensor(
                                    out=acc[m2][:, t * T:(t + 1) * T],
                                    in0=acc[m2][:, t * T:(t + 1) * T], in1=tmp[:], op=ALU.add),
                                    reads=("tmp", f"acc{m2}"), writes=(f"acc{m2}",))
            for t in range(NT_B):
                t0 = tb0 + t * T
                yv = [acc[c][:, t * T:(t + 1) * T] for c in range(DC)]
                xfv = [xf[c][:, 0:T] for c in range(DC)]
                ybfv = [ybf[c][:, 0:T] for c in range(DC)]
                sqbv = [sqb[0][:, 0:T], sqb[1][:, 0:T]]
                for c in range(DC):
                    fw.dma(sy, lambda e, b=b, c=c, t0=t0, xfv=xfv: e.dma_start(
                        out=xfv[c], in_=DR["x3_s"][b, c, :, t0:t0 + T]),
                        writes=(f"hid{c}",))
                    fw.op(v, lambda e, c=c, yv=yv, xfv=xfv: e.tensor_tensor(
                        out=yv[c], in0=yv[c], in1=xfv[c], op=ALU.add),
                        reads=(f"acc{c}", f"hid{c}"), writes=(f"acc{c}",))
                emit_postln(fw, yv, ytags,
                            ybfv, ybftags, sqbv, sqbtags, rows, rowtags,
                            bankA, "pPJ2", pMB, pRB, ones_bf, ones128r,
                            psMBtag="pPJ0", psRBtag="pPJ1")
                for c in range(DC):
                    fw.dma(sy, lambda e, b=b, c=c, t0=t0, yv=yv: e.dma_start(
                        out=DR["out"][b, c, :, t0:t0 + T], in_=yv[c]),
                        reads=(f"acc{c}",))
        fw.flush(block)



def _ap(x):
    try:
        return x[:]
    except Exception:
        return x


def emit_postln(fw, y, ytags, ybf, ybftags, sqb, sqbtags, rows, rowtags,
                psA, psAtag, psMB, psRB, ones_bf, ones128r,
                psMBtag="psMB", psRBtag="psRB"):
    v, sc, te = "vector", "scalar", "tensor"
    ya = [_ap(t) for t in y]
    ybfa = [_ap(t) for t in ybf]
    sqba = [_ap(t) for t in sqb]
    rowsa = [_ap(t) for t in rows]
    mrow, vrow, srow, rstd = rowsa
    tm, tv, ts_, tr = rowtags
    for c in range(DC):
        fw.op(v, lambda e, c=c: e.tensor_copy(out=ybfa[c], in_=ya[c]),
              reads=(ytags[c],), writes=(ybftags[c],))
        fw.op(sc, lambda e, c=c: e.activation(out=sqba[c % 2], in_=ya[c], func=AF.Square),
              reads=(ytags[c],), writes=(sqbtags[c % 2],))
        fw.op(te, lambda e, c=c: e.matmul(out=psA[32:33, :], lhsT=ones_bf[:], rhs=sqba[c % 2],
                                          start=(c == 0), stop=(c == DC - 1)),
              reads=(sqbtags[c % 2], "ones"), writes=(psAtag,))
    for c in range(DC):
        fw.op(te, lambda e, c=c: e.matmul(out=psA[0:1, :], lhsT=ones_bf[:], rhs=ybfa[c],
                                          start=(c == 0), stop=(c == DC - 1)),
              reads=(ybftags[c], "ones"), writes=(psAtag,))
    fw.op(v, lambda e: e.tensor_scalar(out=mrow, in0=psA[0:1, :], scalar1=1.0 / D,
                                       scalar2=None, op0=ALU.mult),
          reads=(psAtag,), writes=(tm,))
    fw.op(v, lambda e: e.tensor_scalar(out=vrow, in0=psA[32:33, :], scalar1=1.0 / D,
                                       scalar2=EPS, op0=ALU.mult, op1=ALU.add),
          reads=(psAtag,), writes=(tv,))
    fw.op(v, lambda e: e.tensor_tensor(out=srow, in0=mrow, in1=mrow, op=ALU.mult),
          reads=(tm,), writes=(ts_,))
    fw.op(v, lambda e: e.tensor_tensor(out=vrow, in0=vrow, in1=srow, op=ALU.subtract),
          reads=(tv, ts_), writes=(tv,))
    fw.op(sc, lambda e: e.activation(out=srow, in_=vrow, func=AF.Sqrt),
          reads=(tv,), writes=(ts_,))
    fw.op(v, lambda e: e.reciprocal(out=rstd, in_=srow),
          reads=(ts_,), writes=(tr,))
    fw.op(te, lambda e: e.matmul(out=psMB[:], lhsT=ones128r[:], rhs=mrow, start=True, stop=True),
          reads=(tm, "ones"), writes=(psMBtag,))
    for c in range(DC):
        fw.op(v, lambda e, c=c: e.tensor_tensor(out=ya[c], in0=ya[c], in1=psMB[:],
                                                op=ALU.subtract),
              reads=(ytags[c], psMBtag), writes=(ytags[c],))
    fw.op(te, lambda e: e.matmul(out=psRB[:], lhsT=ones128r[:], rhs=rstd, start=True, stop=True),
          reads=(tr, "ones"), writes=(psRBtag,))
    for c in range(DC):
        fw.op(v, lambda e, c=c: e.tensor_tensor(out=ya[c], in0=ya[c], in1=psRB[:],
                                                op=ALU.mult),
              reads=(ytags[c], psRBtag), writes=(ytags[c],))



WEIGHT_SPECS = {
    "p0_wd": (DC, 128, D), "p0_bd": (128, DC), "p0_wb": (DC, 128, 16),
    "p0_bB": (16, 1), "p0_alog": (16, 1), "p0_wo": (16, D), "p0_bo": (128, DC),
    "p0_dp": (128, DC), "ident": (128, 128),
    "p1_w": (DC, 128, D), "p1_b": (128, DC),
    "p2_w1": (DC, 128, D), "p2_b1": (128, DC), "p2_w2": (DC, 128, D),
    "p2_b2": (128, DC), "p2_mw": (DC, 128, MEM + 1),
    "p3_wr": (DC, 128, NEXP), "p3_wfc": (NEXP + 1, DC, 128, HID),
    "p3_wpj": (NEXP + 1, HC, 128, D), "p3_bfc": (NEXP + 1, 128, HC),
    "p3_bpj": (NEXP + 1, 128, DC),
}

SCRATCH_SPECS = {
    "x1_s": ((NSEQ, DC, 128, S), F32),
    "x2_s": ((NSEQ, DC, 128, S), F32),
    "p_s": ((NSEQ, DC, 128, S), BF16),
    "tot_s": ((128, 2 * DC), F32),
    "band_s": ((NSEQ, DC, 128, S), BF16),
    "x3_s": ((NSEQ, DC, 128, S), F32),
    "gates_s": ((NEXP, NSEQ * S), BF16),
    "lmax_out": ((1, 1), F32),
}

PHASE_DEFS = {
    "p0": dict(weights=["p0_wd", "p0_bd", "p0_wb", "p0_bB", "p0_alog", "p0_wo",
                        "p0_bo", "p0_dp", "ident"],
               sin=["x_in"], sout=["x1_s", "lmax_out"]),
    "p1": dict(weights=["p1_w", "p1_b", "ident"],
               sin=["x1_s", "maxes"], sout=["x2_s"]),
    "p2a": dict(weights=["p2_w1", "p2_b1"], sin=["x2_s"], sout=["p_s", "tot_s"]),
    "p2b": dict(weights=["p2_w2", "p2_b2", "p2_mw", "ident"],
                sin=["p_s", "tot_s", "x2_s"], sout=["x3_s"], scratch=["band_s"]),
    "p3": dict(weights=["p3_wr", "p3_wfc", "p3_wpj", "p3_bfc", "p3_bpj", "ident"],
               sin=["x3_s"], sout=["out"], scratch=["gates_s"]),
}


def build_unfused(phase):
    nc = bass.Bass("TRN2", target_bir_lowering=False, debug=False)
    pd = PHASE_DEFS[phase]
    DR = {}
    for w in pd["weights"]:
        DR[w] = nc.dram_tensor(w, list(WEIGHT_SPECS[w]), F32, kind="ExternalInput").ap()
    for s in pd["sin"]:
        if s == "x_in":
            DR[s] = nc.dram_tensor("x_in", [NSEQ, DC, 128, S], F32, kind="ExternalInput").ap()
        elif s == "maxes":
            DR[s] = nc.dram_tensor("maxes", [1, NCORES], F32, kind="ExternalInput").ap()
        else:
            shp, dt = SCRATCH_SPECS[s]
            DR[s] = nc.dram_tensor(s, list(shp), dt, kind="ExternalInput").ap()
    for s in pd["sout"]:
        if s == "out":
            DR[s] = nc.dram_tensor("out", [NSEQ, DC, 128, S], F32, kind="ExternalOutput").ap()
        else:
            shp, dt = SCRATCH_SPECS[s]
            DR[s] = nc.dram_tensor(s, list(shp), dt, kind="ExternalOutput").ap()
    for s in pd.get("scratch", []):
        shp, dt = SCRATCH_SPECS[s]
        DR[s] = nc.dram_tensor(s, list(shp), dt).ap()

    with ExitStack() as es:
        fw = FW(nc, es)
        with nc.Block() as block, ExitStack() as pes:
            if phase == "p0":
                emit_p0(fw, block, DR, pes)
            elif phase == "p1":
                emit_p1(fw, block, DR, pes, n_max=NCORES)
            elif phase == "p2a":
                emit_p2a(fw, block, DR, pes)
            elif phase == "p2b":
                emit_p2b(fw, block, DR, pes)
            elif phase == "p3":
                emit_p3(fw, block, DR, pes)
    return nc


def build_fused():
    from concourse.replica_groups import maybe_share_collective_output_space
    nc = bass.Bass("TRN2", target_bir_lowering=False, debug=False)
    DR = {}
    for w, shp in WEIGHT_SPECS.items():
        DR[w] = nc.dram_tensor(w, list(shp), F32, kind="ExternalInput").ap()
    DR["x_in"] = nc.dram_tensor("x_in", [NSEQ, DC, 128, S], F32, kind="ExternalInput").ap()
    DR["out"] = nc.dram_tensor("out", [NSEQ, DC, 128, S], F32, kind="ExternalOutput").ap()
    for s, (shp, dt) in SCRATCH_SPECS.items():
        DR[s] = nc.dram_tensor(s, list(shp), dt).ap()
    aspace = maybe_share_collective_output_space("AllReduce", [list(range(NCORES))], prefer_local=True)
    DR["maxes"] = nc.dram_tensor("maxes", [1, 1], F32, addr_space=aspace).ap()

    with ExitStack() as es:
        fw = FW(nc, es)
        with nc.Block() as block:
            with ExitStack() as pes:
                emit_p0(fw, block, DR, pes)
            emit_cc(fw, block, DR)
            with ExitStack() as pes:
                emit_p1(fw, block, DR, pes, n_max=1)
            with ExitStack() as pes:
                emit_p2a(fw, block, DR, pes, flush=False)
                emit_p2b(fw, block, DR, pes)
            with ExitStack() as pes:
                emit_p3(fw, block, DR, pes)
    return nc



def kernel(**inputs):
    W = prep_weights(inputs)
    xs = shard_x(inputs["L0_attn_x"])

    if FUSED:
        nc = build_fused()
        in_maps = [dict(W, x_in=xs[c]) for c in range(NCORES)]
        res = run_bass_kernel_spmd(nc, in_maps, core_ids=list(range(NCORES)))
        return unshard_out([r["out"] for r in res.results])

    state = [dict() for _ in range(NCORES)]

    def run(phase, feed_keys, extra=None):
        nc = build_unfused(phase)
        pd = PHASE_DEFS[phase]
        in_maps = []
        for c in range(NCORES):
            m = {w: W[w] for w in pd["weights"]}
            for s in pd["sin"]:
                if s == "x_in":
                    m["x_in"] = xs[c]
                elif s == "maxes":
                    m["maxes"] = extra[c]
                else:
                    m[s] = state[c][s]
            in_maps.append(m)
        res = run_bass_kernel_spmd(nc, in_maps, core_ids=list(range(NCORES)))
        for c in range(NCORES):
            for s in pd["sout"]:
                state[c][s] = res.results[c][s]
        return res

    run("p0", None)
    maxes = np.concatenate([state[c]["lmax_out"].reshape(1, 1) for c in range(NCORES)],
                           axis=1).astype(np.float32)
    run("p1", None, extra=[maxes for _ in range(NCORES)])
    run("p2a", None)
    run("p2b", None)
    run("p3", None)
    return unshard_out([state[c]["out"] for c in range(NCORES)])
